# version 10
# baseline (speedup 1.0000x reference)
import numpy as np
import ml_dtypes
from contextlib import ExitStack
import concourse.bass as bass
import concourse.mybir as mybir
from concourse.bass_utils import run_bass_kernel_spmd

F32 = mybir.dt.float32
BF16 = mybir.dt.bfloat16
U8 = mybir.dt.uint8
AF = mybir.ActivationFunctionType
ALU = mybir.AluOpType
AX = mybir.AxisListType

D = 1024
NH = 16
FF = 4096
NE = 8
EPS = 1e-5
SCALE = 0.125
EMIN = float(np.exp(np.float32(-10.0)))
EMAX = float(np.exp(np.float32(10.0)))

DEBUG_SCRATCH_OUT = False
DEBUG_TAPS = False


class Tok:
    __slots__ = ("sem", "val")

    def __init__(self, sem, val):
        self.sem = sem
        self.val = val


class Buf:
    __slots__ = ("name", "w", "rds", "mw")

    def __init__(self, name="buf", seed=None):
        self.name = name
        self.w = None
        self.rds = dict(seed) if seed else {}
        self.mw = {}


ENGS = ("pe", "dve", "act", "pool", "sp")


class K:
    def __init__(self, nc, es, n_dma_sems=8):
        self.nc = nc
        self.eng = {"pe": nc.tensor, "dve": nc.vector, "act": nc.scalar,
                    "pool": nc.gpsimd, "sp": nc.sync}
        self.sem = {}
        self.cnt = {}
        for e in ENGS:
            self.sem[e] = es.enter_context(nc.semaphore(f"sem_{e}"))
            self.cnt[e] = 0
        self.dsem = {}
        self.dcnt = {}
        self.didx = {}
        for qe in ("sp", "pool", "act"):
            self.dsem[qe] = [es.enter_context(nc.semaphore(f"dma_{qe}{i}"))
                             for i in range(n_dma_sems)]
            self.dcnt[qe] = [0] * n_dma_sems
            self.didx[qe] = 0
        self.hw = {e: {} for e in ENGS}
        self.pending = {e: ([], []) for e in ENGS}
        self.graveyard = {}

    def _wait(self, e, tok):
        if tok is None:
            return
        key = id(tok.sem)
        if self.hw[e].get(key, -1) >= tok.val:
            return
        self.eng[e].wait_ge(tok.sem, tok.val)
        self.hw[e][key] = tok.val

    def _dep_toks(self, reads, writes):
        toks = []
        for b in reads:
            if b.w is not None:
                toks.append(b.w)
            toks.extend(b.mw.values())
        for b in writes:
            if b.w is not None:
                toks.append(b.w)
            toks.extend(b.mw.values())
            toks.extend(b.rds.values())
        return toks

    def waits_for(self, e, reads=(), writes=()):
        for t in self._dep_toks(reads, writes):
            self._wait(e, t)

    def op(self, e, fn, reads=(), writes=(), inc=True, nowait=False):
        if not nowait:
            for t in self._dep_toks(reads, writes):
                self._wait(e, t)
        instr = fn(self.eng[e])
        pr, pw = self.pending[e]
        if inc:
            self.cnt[e] += 1
            tok = Tok(self.sem[e], self.cnt[e])
            instr.then_inc(self.sem[e], 1)
            for b in list(pr) + list(reads):
                b.rds[e] = tok
            for b in list(pw) + list(writes):
                b.w = tok
                b.rds = {}
                b.mw = {}
            pr.clear()
            pw.clear()
            return tok
        pr.extend(reads)
        pw.extend(writes)
        return None

    def dma(self, qe, out_ap, in_ap, reads=(), writes=(), writes_acc=(), **kw):
        for t in self._dep_toks(reads, writes):
            self._wait(qe, t)
        for b in writes_acc:
            for t in b.rds.values():
                self._wait(qe, t)
            if b.w is not None:
                self._wait(qe, b.w)
        i = self.didx[qe]
        self.didx[qe] = (i + 1) % len(self.dsem[qe])
        sem = self.dsem[qe][i]
        if self.dcnt[qe][i] > 0:
            self._wait(qe, Tok(sem, self.dcnt[qe][i]))
        self.eng[qe].dma_start(out=out_ap, in_=in_ap, **kw).then_inc(sem, 16)
        self.dcnt[qe][i] += 16
        tok = Tok(sem, self.dcnt[qe][i])
        for b in reads:
            b.rds[f"q{qe}{i}"] = tok
        for b in writes:
            b.w = tok
            b.rds = {}
            b.mw = {}
        for b in writes_acc:
            b.mw[id(sem)] = tok
        return tok

    def buf(self, name="buf"):
        return Buf(name, seed=self.graveyard)

    def bury(self, *bufs):
        for b in bufs:
            toks = list(b.rds.values()) + list(b.mw.values())
            if b.w is not None:
                toks.append(b.w)
            for t in toks:
                key = id(t.sem)
                old = self.graveyard.get(key)
                if old is None or old.val < t.val:
                    self.graveyard[key] = t


def host_consts(T):
    pos = np.arange(T - 1, -1, -1, dtype=np.float32)
    invf = (1.0 / 10000.0 ** (np.arange(0, D, 2, dtype=np.float32) / D)).astype(np.float32)
    sinu = pos[:, None] * invf[None, :]
    pe = np.concatenate([np.sin(sinu), np.cos(sinu)], axis=1).astype(np.float32)
    peT = np.ascontiguousarray(pe.T)
    ident = np.eye(128, dtype=ml_dtypes.bfloat16)
    identf = np.eye(128, dtype=np.float32)
    tri = np.tril(np.ones((128, 128), dtype=np.uint8), k=-1)
    iota8 = np.broadcast_to(np.arange(8, dtype=np.float32), (128, 8)).copy()
    return peT, ident, identf, tri, iota8


class Builder:
    def __init__(self, T):
        self.T = T
        self.TT = T // 128
        self.TC = T // 512
        self.NTC = T // 256
        nc = bass.Bass("TRN2", target_bir_lowering=False)
        self.nc = nc
        self.es = ExitStack()
        self.es.__enter__()
        self.k = K(nc, self.es)

        def din(name, shape, dt=F32):
            return nc.dram_tensor(name, shape, dt, kind="ExternalInput").ap()

        sk = "ExternalOutput" if DEBUG_SCRATCH_OUT else "Internal"

        def dscr(name, shape, dt=F32):
            return nc.dram_tensor(name, shape, dt, kind=sk).ap()

        i = self.i = {}
        i["x"] = din("x", [T, D])
        i["aWqkv"] = din("aWqkv", [3 * D, D])
        i["aWout"] = din("aWout", [D, D])
        i["fWqkv"] = din("fWqkv", [D, 3 * D])
        i["fWo"] = din("fWo", [D, D])
        i["fWr"] = din("fWr", [D, D])
        i["fWff1"] = din("fWff1", [D, FF])
        i["fWff2"] = din("fWff2", [FF, D])
        i["fWg"] = din("fWg", [D, NE])
        i["fWe1"] = din("fWe1", [NE, D, FF])
        i["fWe2"] = din("fWe2", [NE, FF, D])
        i["fWout"] = din("fWout", [D, D])
        i["uv"] = din("uv", [128, 16])
        i["be1c"] = din("be1c", [128, NE * 32])
        i["bff1c"] = din("bff1c", [128, 32])
        i["bff2r"] = din("bff2r", [128, D])
        i["be2"] = din("be2", [NE, D])
        for nm in ("g1", "b1", "g2", "b2", "g3", "b3"):
            i[nm] = din(nm, [128, D])
        i["peT"] = din("peT", [D, T])
        i["ident"] = din("ident", [128, 128], BF16)
        i["identf"] = din("identf", [128, 128])
        i["tri"] = din("tri", [128, 128], U8)
        i["iota8"] = din("iota8", [128, 8])

        s = self.s = {}
        self.Bs = {}
        s["bd"] = [dscr(f"bd{h}", [T * T], BF16) for h in range(NH)]
        self.Bs["bd"] = [self.k.buf(f"bd{h}") for h in range(NH)]
        self.Bs["bdg"] = [self.k.buf(f"bdg{h}") for h in range(NH)]
        for nm, shape, dt in [("x2", [T, D], F32), ("lnx2", [T, D], F32),
                              ("h1", [T, D], F32), ("h2", [T, D], F32),
                              ("h1T", [D, T], BF16), ("h2T", [D, T], BF16),
                              ("octx", [T, D], BF16)]:
            s[nm] = dscr(nm, shape, dt)
            self.Bs[nm] = self.k.buf(nm)
        self.out = nc.dram_tensor("out", [T, D], F32, kind="ExternalOutput").ap()
        self._tapP = (nc.dram_tensor("tapP", [T, T], BF16, kind="ExternalOutput").ap()
                      if DEBUG_TAPS else None)

        self.ps = [self.es.enter_context(nc.psum_tensor(f"ps{j}", [128, 512], F32))
                   for j in range(8)]
        self.Bps = [self.k.buf(f"ps{j}") for j in range(8)]

        self.sb_p = {}
        self.Bp = {}
        for nm, shape, dt in [
            ("ident", [128, 128], BF16), ("identf", [128, 128], F32),
            ("tri", [128, 128], U8), ("iota8", [128, 8], F32),
            ("uv", [128, 16], F32), ("wg", [128, NE * 8], BF16),
            ("be1c", [128, NE * 32], F32), ("bff1c", [128, 32], F32),
            ("zero", [128, 128], BF16), ("ones1", [1, 128], BF16),
            ("epsc", [128, 1], F32),
            ("bn12a", [128, 12], F32), ("mva", [128, 2], F32),
            ("sda", [128, 1], F32), ("rstda", [128, 1], F32),
            ("bn12b", [128, 12], F32), ("mvb", [128, 2], F32),
            ("sdb", [128, 1], F32), ("rstdb", [128, 1], F32),
            ("zrec", [128, 1], F32),
            ("maskg", [128, (T // 128) * 8], F32),
            ("maskgT", [NE, T], BF16),
            ("gm", [128, 1], F32), ("gz", [128, 1], F32),
            ("ge", [128, 8], F32), ("gv", [128, 8], F32),
        ]:
            self.sb_p[nm] = self.es.enter_context(nc.sbuf_tensor("c_" + nm, shape, dt))
            self.Bp[nm] = self.k.buf(nm)

    def tap(self, name, src_ap, Bsrc, shape):
        if not DEBUG_TAPS:
            return
        t = self.nc.dram_tensor("tap_" + name, list(shape), src_ap.dtype,
                                kind="ExternalOutput").ap()
        n = int(np.prod(shape))
        self.k.dma("sp", bass.AP(t.tensor, 0, [[1, n]]) if False else t, src_ap,
                   reads=[Bsrc])

    def sb(self, es, name, shape, dt, side="left"):
        t = es.enter_context(self.nc.sbuf_tensor(name, shape, dt, side=side))
        return t, self.k.buf(name)

    def ap3(self, t, off, d1, d2):
        free = int(np.prod(t.shape[1:]))
        return bass.AP(t, off, [[free, t.shape[0]], list(d1), list(d2)])

    def rowload(self, qe, dst_sb, dram_ap, r0, ncols, Bdst, nrows=128, Bsrc=None):
        reads = [Bsrc] if Bsrc is not None else []
        self.k.dma(qe, dst_sb, bass.AP(dram_ap.tensor, r0 * ncols,
                                       [[ncols, nrows], [1, ncols]]),
                   reads=reads, writes=[Bdst])

    def rowstore(self, dst_dram_ap, r0, ncols, src_sb, Bsrc, Bdst, qe="act"):
        self.k.dma(qe, bass.AP(dst_dram_ap.tensor, r0 * ncols,
                               [[ncols, 128], [1, ncols]]),
                   src_sb, reads=[Bsrc], writes_acc=[Bdst])

    def rowstore_bf(self, dst_dram_ap, r0, ncols, src_sb, Bsrc, Bdst):
        self.rowstore(dst_dram_ap, r0, ncols, src_sb, Bsrc, Bdst, qe="pool")

    def ctx_proj_tile(self, tt, W, BW, stgb, Bstgb, stgT, BstgT, psjT, pbase,
                      dst, Bdst, res, Bres):
        k, T = self.k, self.T
        self.rowload("sp", stgb[:, :], self.s["octx"], tt * 128, D, Bstgb,
                     Bsrc=self.Bs["octx"])
        psb = self.ps[psjT].ap().bitcast(BF16)
        for c in range(8):
            k.op("pe", lambda e, c=c: e.matmul(
                psb[:, c * 128:(c + 1) * 128], stgb[:, c * 128:(c + 1) * 128],
                self.sb_p["ident"][:, :], start=True, stop=True,
                is_transpose=True, skip_group_check=True),
                reads=[Bstgb, self.Bp["ident"]], writes=[self.Bps[psjT]],
                inc=(c == 7))
        k.op("dve", lambda e: e.tensor_copy(stgT[:, :], psb[:, 0:1024]),
             reads=[self.Bps[psjT]], writes=[BstgT])
        for half in range(2):
            pj2 = pbase + half
            for dc in range(8):
                k.op("pe", lambda e, dc=dc, half=half, pj2=pj2: e.matmul(
                    self.ps[pj2][:, :], stgT[:, dc * 128:(dc + 1) * 128],
                    W[:, dc * D + half * 512: dc * D + half * 512 + 512],
                    start=(dc == 0), stop=(dc == 7)),
                    reads=[BstgT, BW], writes=[self.Bps[pj2]], inc=(dc == 7))
            k.op("dve", lambda e, half=half, pj2=pj2, dst=dst, res=res: e.tensor_tensor(
                dst[:, half * 512: half * 512 + 512], self.ps[pj2][:, :],
                res[:, half * 512: half * 512 + 512], ALU.add),
                reads=[self.Bps[pj2], Bres], writes=[Bdst])

    def ln_tile(self, src, Bsrc, out_f32, Bout_f32, out_bf=None, Bout_bf=None,
                g=None, Bg=None, b=None, Bb=None, slot=0):
        k, p, B = self.k, self.sb_p, self.Bp
        sx = "a" if slot % 2 == 0 else "b"
        bn12, mv, sd, rstd = (p["bn12" + sx], p["mv" + sx], p["sd" + sx],
                              p["rstd" + sx])
        Bbn, Bmv, Bsd, Brs = (B["bn12" + sx], B["mv" + sx], B["sd" + sx],
                              B["rstd" + sx])
        k.op("dve", lambda e: e.bn_stats(bn12[:, 0:6], src[:, 0:512]),
             reads=[Bsrc], writes=[Bbn], inc=False)
        k.op("dve", lambda e: e.bn_stats(bn12[:, 6:12], src[:, 512:1024]),
             reads=[Bsrc], writes=[Bbn])
        k.op("dve", lambda e: e.bn_aggr(mv[:, :], bn12[:, :]),
             reads=[Bbn], writes=[Bmv])
        k.op("act", lambda e: e.activation(sd[:, :], mv[:, 1:2], AF.Sqrt,
                                           bias=p["epsc"][:, :]),
             reads=[Bmv, B["epsc"]], writes=[Bsd])
        k.op("dve", lambda e: e.reciprocal(rstd[:, :], sd[:, :]),
             reads=[Bsd], writes=[Brs])
        if g is None:
            k.op("dve", lambda e: e.tensor_scalar(
                out_f32, src, mv[:, 0:1], rstd[:, :], ALU.subtract, ALU.mult),
                reads=[Bsrc, Bmv, Brs], writes=[Bout_f32])
        else:
            k.op("dve", lambda e: e.tensor_scalar(
                out_f32, src, mv[:, 0:1], None, ALU.subtract, ALU.bypass),
                reads=[Bsrc, Bmv], writes=[Bout_f32])
            k.op("dve", lambda e: e.scalar_tensor_tensor(
                out_f32, out_f32, rstd[:, :], g, ALU.mult, ALU.mult),
                reads=[Bout_f32, Brs, Bg], writes=[Bout_f32])
            k.op("dve", lambda e: e.tensor_tensor(out_f32, out_f32, b, ALU.add),
                 reads=[Bout_f32, Bb], writes=[Bout_f32])
        if out_bf is not None:
            k.op("dve", lambda e: e.tensor_copy(out_bf, out_f32),
                 reads=[Bout_f32], writes=[Bout_bf])

    def transpose_tile(self, src, Bsrc, dst_t, Bdst, tt, psj, nch=8):
        k, T = self.k, self.T
        psb = self.ps[psj].ap().bitcast(BF16)
        idn = self.sb_p["ident"]
        for c in range(nch):
            k.op("pe", lambda e, c=c: e.matmul(
                psb[:, c * 128:(c + 1) * 128], src[:, c * 128:(c + 1) * 128],
                idn[:, :], start=True, stop=True, is_transpose=True,
                skip_group_check=True),
                reads=[Bsrc, self.Bp["ident"]], writes=[self.Bps[psj]],
                inc=(c == nch - 1))
        k.op("dve", lambda e: e.tensor_copy(
            self.ap3(dst_t, tt * 128, [T, nch], [1, 128]), psb[:, 0:nch * 128]),
            reads=[self.Bps[psj]], writes=[Bdst])

    def load_wT(self, qe, dram, nrow, wst, dstT, BdstT, ncols_t, psj=6, row0=0):
        k = self.k
        for ec in range(nrow):
            w, Bw = wst[ec % 2]
            self.rowload(qe, w[:, :], dram, (row0 + ec) * 128, D, Bw)
            pj = psj if psj != -1 else ec % 2
            psb = self.ps[pj].ap().bitcast(BF16)
            for dc in range(8):
                k.op("pe", lambda e, dc=dc, w=w, psb=psb: e.matmul(
                    psb[:, dc * 128:(dc + 1) * 128], w[:, dc * 128:(dc + 1) * 128],
                    self.sb_p["ident"][:, :], start=True, stop=True,
                    is_transpose=True, skip_group_check=True),
                    reads=[Bw, self.Bp["ident"]], writes=[self.Bps[pj]],
                    inc=(dc == 7))
            k.op("dve", lambda e, ec=ec, psb=psb: e.tensor_copy(
                self.ap3(dstT, ec * 128, [ncols_t, 8], [1, 128]), psb[:, 0:1024]),
                reads=[self.Bps[pj]], writes=[BdstT])

    def load_consts(self):
        k, i, p, B = self.k, self.i, self.sb_p, self.Bp
        T = self.T
        for nm in ("ident", "identf", "tri", "iota8", "uv", "be1c", "bff1c"):
            k.dma("sp", p[nm][:, :], i[nm], writes=[B[nm]])
        k.dma("pool", p["wg"][:, :],
              bass.AP(i["fWg"].tensor, 0, [[8, 128], [128 * 8, 8], [1, 8]]),
              writes=[B["wg"]])
        k.op("dve", lambda e: e.memset(p["zero"][:, :], 0.0), writes=[B["zero"]])
        k.op("dve", lambda e: e.memset(p["epsc"][:, :], EPS), writes=[B["epsc"]])
        k.op("dve", lambda e: e.memset(p["ones1"][:, :], 1.0), writes=[B["ones1"]])
        zg = ExitStack()
        z2, Bz2 = self.sb(zg, "zfill", [128, 1024], BF16)
        k.op("dve", lambda e: e.memset(z2[:, :], 0.0), writes=[Bz2])
        rows_per = 1024
        for h in range(NH):
            for r0 in range(0, T, rows_per):
                nr = min(rows_per, T - r0)
                k.dma("sp",
                      bass.AP(self.s["bd"][h].tensor, r0 * T, [[T, nr], [1, 128]]),
                      bass.AP(z2, 0, [[1024, 128], [1, nr]]),
                      reads=[Bz2], writes_acc=[self.Bs["bdg"][h]])
        self.k.bury(Bz2)
        zg.close()

    def build(self):
        k, T, TT, TC = self.k, self.T, self.TT, self.TC
        self.load_consts()

        g1 = ExitStack()
        lnxT, _ = self.sb(g1, "lnxT", [128, 8 * T], BF16)
        BlnxTl = [self.k.buf(f"lnxT{t}") for t in range(TT)]
        wst = [self.sb(g1, f"wst{j}", [128, D], BF16) for j in range(2)]
        xin = [self.sb(g1, f"xin{j}", [128, D], F32) for j in range(3)]
        lnb = [self.sb(g1, f"lnb{j}", [128, D], BF16) for j in range(3)]
        g2 = ExitStack()
        QT, _ = self.sb(g2, "QT", [128, 8 * T], BF16, side="right")
        KT, _ = self.sb(g2, "KT", [128, 8 * T], BF16, side="right")
        BQTl = [self.k.buf(f"QT{h}") for h in range(8)]
        BKTl = [self.k.buf(f"KT{h}") for h in range(8)]
        g1b = ExitStack()
        Wqk, BWqk = self.sb(g1b, "aWqkT", [128, 8 * 2048], BF16)

        self.load_wT("pool", self.i["aWqkv"], 16, wst, Wqk, BWqk, 2048, psj=-1)
        for tt in range(TT):
            xt, Bxt = xin[tt % 3]
            self.rowload("sp", xt[:, :], self.i["x"], tt * 128, D, Bxt)
            lb, Blb = lnb[tt % 3]
            self.ln_tile(xt[:, :], Bxt, xt[:, :], Bxt, out_bf=lb[:, :], Bout_bf=Blb,
                         slot=tt)
            self.transpose_tile(lb[:, :], Blb, lnxT, BlnxTl[tt], tt, 2 + tt % 2)
        self.qkv(Wqk, BWqk, 2048, lnxT, BlnxTl, QT, BQTl, KT, BKTl)
        k.bury(BWqk)
        g1b.close()
        g2b = ExitStack()
        VA, _ = self.sb(g2b, "VA", [128, TT * 1040], BF16, side="right")
        BVAl = [self.k.buf(f"VA{t}") for t in range(TT)]
        g1c = ExitStack()
        Wv, BWv = self.sb(g1c, "aWvT", [128, 8 * 1024], BF16)
        self.load_wT("pool", self.i["aWqkv"], 8, wst, Wv, BWv, 1024, psj=-1, row0=16)
        self.qkv_v(Wv, BWv, 1024, 0, lnxT, BlnxTl, VA, BVAl)
        k.bury(BWv, *BlnxTl, *[b for _, b in wst], *[b for _, b in xin],
               *[b for _, b in lnb])
        g1c.close()
        g1.close()

        g3 = ExitStack()
        Psets = []
        for si in range(2):
            Pa, _ = self.sb(g3, f"P{si}a", [128, TT * 512], BF16)
            Pb, _ = self.sb(g3, f"P{si}b", [128, TT * 512], BF16)
            Psets.append((Pa, [self.k.buf(f"P{si}a{t}") for t in range(TT)],
                          Pb, [self.k.buf(f"P{si}b{t}") for t in range(TT)]))
        ONc = [self.sb(g3, "ONc0", [128, 4 * 1024], BF16)]
        self.attention(QT, BQTl, KT, BKTl, VA, BVAl, Psets, ONc, bd=False,
                       QuStg=None)
        for (Pa, BPal, Pb, BPbl) in Psets:
            k.bury(*BPal, *BPbl)
        k.bury(*BQTl, *BKTl, *BVAl, *[b for _, b in ONc])
        g3.close()
        g2b.close()
        g2.close()

        g4 = ExitStack()
        lnx2T, _ = self.sb(g4, "lnx2T", [128, 8 * T], BF16, side="right")
        Blnx2Tl = [self.k.buf(f"lnx2T{t}") for t in range(TT)]
        g4b = ExitStack()
        WoT, BWoT = self.sb(g4b, "aWoutT", [128, 8 * D], BF16)
        wst2 = [self.sb(g4b, f"wsg{j}", [128, D], BF16) for j in range(2)]
        self.load_wT("pool", self.i["aWout"], 8, wst2, WoT, BWoT, D, psj=6)
        xin2 = [self.sb(g4b, f"xi2{j}", [128, D], F32) for j in range(3)]
        x2t = [self.sb(g4b, f"x2t{j}", [128, D], F32) for j in range(3)]
        lnf2 = [self.sb(g4b, f"lnf2{j}", [128, D], F32) for j in range(2)]
        lnb2 = [self.sb(g4b, f"lb2{j}", [128, D], BF16) for j in range(2)]
        ocb = [self.sb(g4b, f"ocb{j}", [128, D], BF16) for j in range(3)]
        ocT = [self.sb(g4b, f"ocT{j}", [128, D], BF16) for j in range(3)]

        def a4_s2(tt):
            x2, Bx2 = x2t[tt % 3]
            self.rowstore(self.s["x2"], tt * 128, D, x2[:, :], Bx2, self.Bs["x2"])
            lf, Blf = lnf2[tt % 2]
            self.ln_tile(x2[:, :], Bx2, lf[:, :], Blf, slot=tt)
            self.rowstore(self.s["lnx2"], tt * 128, D, lf[:, :], Blf,
                          self.Bs["lnx2"])
            lb, Blb = lnb2[tt % 2]
            k.op("dve", lambda e, lb=lb, lf=lf: e.tensor_copy(lb[:, :], lf[:, :]),
                 reads=[Blf], writes=[Blb])

        def a4_s3(tt):
            lb, Blb = lnb2[tt % 2]
            self.transpose_tile(lb[:, :], Blb, lnx2T, Blnx2Tl[tt], tt, 4 + tt % 2)

        for it in range(TT + 2):
            if it < TT:
                tt = it
                xt, Bxt = xin2[tt % 3]
                self.rowload("sp", xt[:, :], self.i["x"], tt * 128, D, Bxt)
                x2, Bx2 = x2t[tt % 3]
                sb_, Bsb_ = ocb[tt % 3]
                sT, BsT = ocT[tt % 3]
                self.ctx_proj_tile(tt, WoT, BWoT, sb_, Bsb_, sT, BsT, 6 + tt % 2,
                                   (tt % 2) * 2, x2, Bx2, xt, Bxt)
            if 1 <= it <= TT:
                a4_s2(it - 1)
            if it >= 2:
                a4_s3(it - 2)
        k.bury(BWoT, *[b for _, b in wst2], *[b for _, b in xin2],
               *[b for _, b in x2t], *[b for _, b in lnb2],
               *[b for _, b in lnf2],
               *[b for _, b in ocb], *[b for _, b in ocT])
        g4b.close()

        g6 = ExitStack()
        QT2, _ = self.sb(g6, "QT2", [128, 8 * T], BF16)
        KT2, _ = self.sb(g6, "KT2", [128, 8 * T], BF16)
        BQT2l = [self.k.buf(f"QT2{h}") for h in range(8)]
        BKT2l = [self.k.buf(f"KT2{h}") for h in range(8)]
        g6b = ExitStack()
        VA2, _ = self.sb(g6b, "VA2", [128, TT * 1040], BF16)
        BVA2l = [self.k.buf(f"VA2{t}") for t in range(TT)]
        g5 = ExitStack()
        fq, Bfq = self.sb(g5, "fqkv", [128, 8 * 3 * D], BF16, side="right")
        for dc in range(8):
            self.k.dma("pool", fq[:, dc * 3 * D:(dc + 1) * 3 * D],
                       bass.AP(self.i["fWqkv"].tensor, dc * 128 * 3 * D,
                               [[3 * D, 128], [1, 3 * D]]),
                       writes=[Bfq])
        self.qkv(fq, Bfq, 3 * D, lnx2T, Blnx2Tl, QT2, BQT2l, KT2, BKT2l)
        self.qkv_v(fq, Bfq, 3 * D, 2048, lnx2T, Blnx2Tl, VA2, BVA2l)
        k.bury(Bfq, *Blnx2Tl)
        g5.close()
        g4.close()

        g8 = ExitStack()
        rT, _ = self.sb(g8, "rT", [128, 8 * T], BF16, side="right")
        BrTl = [self.k.buf(f"rT{h}") for h in range(8)]
        g7 = ExitStack()
        peT, BpeT = self.sb(g7, "s_peT", [128, 8 * T], BF16, side="right")
        Wr, BWr = self.sb(g7, "Wr", [128, 8 * D], BF16, side="right")
        for dc in range(8):
            self.k.dma("pool", peT[:, dc * T:(dc + 1) * T],
                       bass.AP(self.i["peT"].tensor, dc * 128 * T, [[T, 128], [1, T]]),
                       writes=[BpeT])
            self.k.dma("pool", Wr[:, dc * D:(dc + 1) * D],
                       bass.AP(self.i["fWr"].tensor, dc * 128 * D, [[D, 128], [1, D]]),
                       writes=[BWr])
        for hp in range(8):
            for tc in range(TC):
                psj = (hp * TC + tc) % 2
                for dc in range(8):
                    k.op("pe", lambda e, dc=dc, hp=hp, tc=tc, psj=psj: e.matmul(
                        self.ps[psj][:, :],
                        Wr[:, dc * D + hp * 128: dc * D + hp * 128 + 128],
                        peT[:, dc * T + tc * 512: dc * T + tc * 512 + 512],
                        start=(dc == 0), stop=(dc == 7)),
                        reads=[BWr, BpeT], writes=[self.Bps[psj]], inc=(dc == 7))
                if (hp * TC + tc) % 2 == 0:
                    k.op("act", lambda e, hp=hp, tc=tc, psj=psj: e.activation(
                        rT[:, hp * T + tc * 512: hp * T + tc * 512 + 512],
                        self.ps[psj][:, :], AF.Copy),
                        reads=[self.Bps[psj]], writes=[BrTl[hp]])
                else:
                    k.op("dve", lambda e, hp=hp, tc=tc, psj=psj: e.tensor_copy(
                        rT[:, hp * T + tc * 512: hp * T + tc * 512 + 512],
                        self.ps[psj][:, :]),
                        reads=[self.Bps[psj]], writes=[BrTl[hp]])
        k.bury(BpeT, BWr)
        g7.close()

        stg8 = ExitStack()
        qv = [self.sb(stg8, f"qv{j}", [128, 128], BF16, side="right") for j in range(2)]
        bst = [self.sb(stg8, f"bst{j}", [128, TC * 512], BF16, side="right")
               for j in range(2)]
        for hp in range(8):
            for tt in range(TT):
                q, Bq = qv[tt % 2]
                k.op("dve", lambda e, hp=hp, tt=tt, q=q: e.tensor_scalar(
                    q[:, :], QT2[:, hp * T + tt * 128: hp * T + tt * 128 + 128],
                    self.sb_p["uv"][:, 8 + hp: 9 + hp], None, ALU.add, ALU.bypass),
                    reads=[BQT2l[hp], self.Bp["uv"]], writes=[Bq])
                jc_min = max(0, (T - 128 * (tt + 1)) // 512)
                njc = TC - jc_min
                for jc in range(jc_min, TC):
                    for h in range(2):
                        psj = h * 2 + (jc % 2)
                        k.waits_for("pe", reads=[Bq, BrTl[hp]],
                                    writes=[self.Bps[psj]])
                    for h in range(2):
                        psj = h * 2 + (jc % 2)
                        r0 = h * 64
                        k.op("pe", lambda e, jc=jc, hp=hp, r0=r0, q=q, psj=psj: e.matmul(
                            self.ps[psj][:, :], q[r0:r0 + 64, :],
                            rT[r0:r0 + 64, hp * T + jc * 512: hp * T + jc * 512 + 512],
                            start=True, stop=True, tile_position=(r0, 0)),
                            reads=[Bq, BrTl[hp]], writes=[self.Bps[psj]],
                            nowait=True)
                    for h in range(2):
                        psj = h * 2 + (jc % 2)
                        b, Bb = bst[h]
                        if (jc + h) % 2 == 0:
                            k.op("act", lambda e, jc=jc, jc_min=jc_min, b=b, psj=psj: e.activation(
                                b[:, (jc - jc_min) * 512: (jc - jc_min) * 512 + 512],
                                self.ps[psj][:, :], AF.Copy),
                                reads=[self.Bps[psj]], writes=[Bb])
                        else:
                            k.op("dve", lambda e, jc=jc, jc_min=jc_min, b=b, psj=psj: e.tensor_copy(
                                b[:, (jc - jc_min) * 512: (jc - jc_min) * 512 + 512],
                                self.ps[psj][:, :]),
                                reads=[self.Bps[psj]], writes=[Bb])
                for h in range(2):
                    head = 2 * hp + h
                    b, Bb = bst[h]
                    njc = TC - jc_min
                    k.dma("pool",
                          bass.AP(self.s["bd"][head].tensor,
                                  tt * 128 * T + jc_min * 512,
                                  [[T, 128], [1, njc * 512]]),
                          b[:, 0: njc * 512],
                          reads=[Bb, self.Bs["bdg"][head]],
                          writes_acc=[self.Bs["bd"][head]])
        k.bury(*[b for _, b in qv], *[b for _, b in bst], *BrTl)
        stg8.close()
        g8.close()

        g9 = ExitStack()
        Psets2 = []
        for si in range(2):
            Pa, _ = self.sb(g9, f"fP{si}a", [128, TT * 512], BF16, side="right")
            Pb, _ = self.sb(g9, f"fP{si}b", [128, TT * 512], BF16, side="right")
            Psets2.append((Pa, [self.k.buf(f"fP{si}a{t}") for t in range(TT)],
                           Pb, [self.k.buf(f"fP{si}b{t}") for t in range(TT)]))
        ONc2 = [self.sb(g9, "fONc0", [128, 4 * 1024], BF16, side="right")]
        stg9 = ExitStack()
        self.bdsh = [[self.sb(stg9, f"bdsh{h}{j}", [128, 512], BF16, side="right")
                      for j in range(4)] for h in range(2)]
        QuStg = [self.sb(stg9, f"qu{j}", [128, 512], BF16, side="right")
                 for j in range(2)]
        self.attention(QT2, BQT2l, KT2, BKT2l, VA2, BVA2l, Psets2, ONc2,
                       bd=True, QuStg=QuStg)
        for (Pa, BPal, Pb, BPbl) in Psets2:
            k.bury(*BPal, *BPbl)
        k.bury(*BQT2l, *BKT2l, *BVA2l, *[b for _, b in ONc2],
               *[b for _, b in QuStg])
        for hh in self.bdsh:
            for _, bb in hh:
                k.bury(bb)
        stg9.close()
        g9.close()
        g6b.close()
        g6.close()

        gW1 = ExitStack()
        W1, BW1g = self.alloc_w1(gW1)
        self.load_w1(W1, BW1g, self.i["fWff1"].tensor, 0)

        g10 = ExitStack()
        Wo, BWo = self.sb(g10, "fWoS", [128, 8 * D], BF16)
        g1s, Bg1s = self.sb(g10, "s_g1", [128, D], F32)
        b1s, Bb1s = self.sb(g10, "s_b1", [128, D], F32)
        for dc in range(8):
            self.k.dma("pool", Wo[:, dc * D:(dc + 1) * D],
                       bass.AP(self.i["fWo"].tensor, dc * 128 * D, [[D, 128], [1, D]]),
                       writes=[BWo])
        k.dma("sp", g1s[:, :], self.i["g1"], writes=[Bg1s])
        k.dma("sp", b1s[:, :], self.i["b1"], writes=[Bb1s])
        lx = [self.sb(g10, f"lx{j}", [128, D], F32) for j in range(3)]
        s1t = [self.sb(g10, f"s1{j}", [128, D], F32) for j in range(3)]
        h1b = [self.sb(g10, f"h1b{j}", [128, D], BF16) for j in range(3)]
        h1ts = [self.sb(g10, f"h1ts{j}", [128, 1024], BF16) for j in range(3)]
        ocb2 = [self.sb(g10, f"oc2{j}", [128, D], BF16) for j in range(3)]
        ocT2 = [self.sb(g10, f"oT2{j}", [128, D], BF16) for j in range(3)]
        h1f = [self.sb(g10, f"h1f{j}", [128, D], F32) for j in range(2)]

        def g10_s2(tt):
            s1, Bs1 = s1t[tt % 3]
            lf, Blf = h1f[tt % 2]
            self.ln_tile(s1[:, :], Bs1, lf[:, :], Blf,
                         g=g1s[:, :], Bg=Bg1s, b=b1s[:, :], Bb=Bb1s, slot=tt)
            self.rowstore(self.s["h1"], tt * 128, D, lf[:, :], Blf, self.Bs["h1"])
            hb, Bhb = h1b[tt % 3]
            k.op("dve", lambda e, hb=hb, lf=lf: e.tensor_copy(hb[:, :], lf[:, :]),
                 reads=[Blf], writes=[Bhb])

        def g10_s3(tt):
            hb, Bhb = h1b[tt % 3]
            ht, Bht = h1ts[tt % 3]
            self.t_tile_to_dram(hb, Bhb, ht, Bht, self.s["h1T"], self.Bs["h1T"],
                                tt, 4 + tt % 2)

        for it in range(TT + 2):
            if it < TT:
                tt = it
                l, Bl = lx[tt % 3]
                self.rowload("sp", l[:, :], self.s["lnx2"], tt * 128, D, Bl,
                             Bsrc=self.Bs["lnx2"])
                s1, Bs1 = s1t[tt % 3]
                sb_, Bsb_ = ocb2[tt % 3]
                sT, BsT = ocT2[tt % 3]
                self.ctx_proj_tile(tt, Wo, BWo, sb_, Bsb_, sT, BsT, 6 + tt % 2,
                                   (tt % 2) * 2, s1, Bs1, l, Bl)
            if 1 <= it <= TT:
                g10_s2(it - 1)
            if it >= 2:
                g10_s3(it - 2)
        k.bury(BWo, Bg1s, Bb1s, *[b for _, b in lx], *[b for _, b in s1t],
               *[b for _, b in h1b], *[b for _, b in h1ts],
               *[b for _, b in h1f],
               *[b for _, b in ocb2], *[b for _, b in ocT2])
        g10.close()

        self.ff_moe_out(gW1, W1, BW1g)

        for qe in ("sp", "pool", "act"):
            for j, c in enumerate(self.k.dcnt[qe]):
                if c > 0:
                    self.k.eng["sp"].wait_ge(self.k.dsem[qe][j], c)
        self.es.__exit__(None, None, None)
        return self.nc

    def qkv(self, Wqk, BWqk, wrow_qk, xT, BxTl, QTo, BQTl, KTo, BKTl):
        k, T, TT, TC = self.k, self.T, self.TT, self.TC
        for ec in range(16):
            dstT, Bdl = (QTo, BQTl) if ec < 8 else (KTo, BKTl)
            hp = ec % 8
            for tc in range(TC):
                psj = (ec * TC + tc) % 2
                xdeps = [BxTl[tc * 4 + i] for i in range(4)]
                for dc in range(8):
                    k.op("pe", lambda e, dc=dc, ec=ec, tc=tc, psj=psj: e.matmul(
                        self.ps[psj][:, :],
                        Wqk[:, dc * wrow_qk + ec * 128: dc * wrow_qk + ec * 128 + 128],
                        xT[:, dc * T + tc * 512: dc * T + tc * 512 + 512],
                        start=(dc == 0), stop=(dc == 7)),
                        reads=[BWqk] + xdeps, writes=[self.Bps[psj]], inc=(dc == 7))
                if (ec * TC + tc) % 2 == 0:
                    k.op("act", lambda e, hp=hp, tc=tc, psj=psj, dstT=dstT: e.activation(
                        dstT[:, hp * T + tc * 512: hp * T + tc * 512 + 512],
                        self.ps[psj][:, :], AF.Copy),
                        reads=[self.Bps[psj]], writes=[Bdl[hp]])
                else:
                    k.op("dve", lambda e, hp=hp, tc=tc, psj=psj, dstT=dstT: e.tensor_copy(
                        dstT[:, hp * T + tc * 512: hp * T + tc * 512 + 512],
                        self.ps[psj][:, :]),
                        reads=[self.Bps[psj]], writes=[Bdl[hp]])

    def qkv_v(self, Wv, BWv, wrow_v, v_off, xT, BxTl, VAo, BVAl):
        k, T, TT = self.k, self.T, self.TT
        for tt in range(TT):
            for half in range(2):
                psj = 2 + (tt * 2 + half) % 2
                for dc in range(8):
                    k.op("pe", lambda e, dc=dc, tt=tt, half=half, psj=psj: e.matmul(
                        self.ps[psj][:, :],
                        xT[:, dc * T + tt * 128: dc * T + tt * 128 + 128],
                        Wv[:, dc * wrow_v + v_off + half * 512: dc * wrow_v + v_off + half * 512 + 512],
                        start=(dc == 0), stop=(dc == 7)),
                        reads=[BxTl[tt], BWv], writes=[self.Bps[psj]], inc=(dc == 7))
                k.op("dve", lambda e, tt=tt, half=half, psj=psj: e.tensor_copy(
                    self.ap3(VAo, tt * 1040 + half * 8 * 65, [65, 8], [1, 64]),
                    self.ps[psj][:, :]),
                    reads=[self.Bps[psj]], writes=[BVAl[tt]])
            k.op("dve", lambda e, tt=tt: e.memset(
                self.ap3(VAo, tt * 1040 + 64, [65, 16], [1, 1]), 1.0),
                writes=[BVAl[tt]])

    def attention(self, QT, BQTl, KT, BKTl, VA, BVAl, Psets, ONc, bd, QuStg):
        k, T, TT, TC = self.k, self.T, self.TT, self.TC
        tri, zero = self.sb_p["tri"], self.sb_p["zero"]

        def emit_pv(tcc, hp, pset, onc, Bonc):
            P0, BP0l, P1, BP1l = pset
            Pt, BPt = (P0, P1), (BP0l, BP1l)
            for j in range(4):
                tt = tcc * 4 + j
                for h in range(2):
                    head = 2 * hp + h
                    psj = 6 + (tt * 2 + h) % 2
                    pb, Bpl = Pt[h], BPt[h]
                    for sc in range(tt + 1):
                        k.op("pe", lambda e, sc=sc, j=j, psj=psj, pb=pb,
                             tt=tt, head=head: e.matmul(
                            self.ps[psj][:, 0:65],
                            pb[:, sc * 512 + j * 128: sc * 512 + j * 128 + 128],
                            VA[:, sc * 1040 + head * 65: sc * 1040 + head * 65 + 65],
                            start=(sc == 0), stop=(sc == tt)),
                            reads=[Bpl[sc], BVAl[sc]], writes=[self.Bps[psj]],
                            inc=(sc == tt))
                    k.op("dve", lambda e, psj=psj: e.reciprocal(
                        self.sb_p["zrec"][:, :], self.ps[psj][:, 64:65]),
                        reads=[self.Bps[psj]], writes=[self.Bp["zrec"]])
                    k.op("dve", lambda e, psj=psj, j=j, head=head,
                         onc=onc: e.tensor_scalar(
                        onc[:, j * 1024 + head * 64: j * 1024 + head * 64 + 64],
                        self.ps[psj][:, 0:64], self.sb_p["zrec"][:, :], None,
                        ALU.mult, ALU.bypass),
                        reads=[self.Bps[psj], self.Bp["zrec"]], writes=[Bonc])

        for tcc in range(TC):
            onc, Bonc = ONc[0]
            prev = None
            for hp in range(8):
                pset = Psets[hp % 2]
                P0, BP0l, P1, BP1l = pset
                Pt, BPt = (P0, P1), (BP0l, BP1l)
                if bd:
                    qu, Bqu = QuStg[hp % 2]
                    k.op("dve", lambda e, hp=hp, tcc=tcc, qu=qu: e.tensor_scalar(
                        qu[:, :], QT[:, hp * T + tcc * 512: hp * T + tcc * 512 + 512],
                        self.sb_p["uv"][:, hp:hp + 1], None, ALU.add, ALU.bypass),
                        reads=[BQTl[hp], self.Bp["uv"]], writes=[Bqu])
                    rhs_q, rq_off, qdep = qu, 0, Bqu
                else:
                    rhs_q, rq_off, qdep = QT, None, None
                sc_max = tcc * 4 + 4
                for g0 in range(0, sc_max, 3):
                    scg = range(g0, min(g0 + 3, sc_max))
                    if bd:
                        for sc in scg:
                            jmin = max(0, sc - tcc * 4)
                            nj = 4 - jmin
                            for h in range(2):
                                head = 2 * hp + h
                                w, Bw = self.bdsh[h][sc % 3]
                                k.dma("sp", w[:, 0:nj * 128],
                                      bass.AP(self.s["bd"][head].tensor,
                                              ((tcc * 4 + jmin) * 128) * (T - 1)
                                              + (T - 1) + sc * 128,
                                              [[T - 1, 128], [128 * (T - 1), nj],
                                               [1, 128]]),
                                      reads=[self.Bs["bd"][head],
                                             self.Bs["bdg"][head]],
                                      writes=[Bw])
                    for sc in scg:
                        c0 = max(0, sc - tcc * 4) * 128
                        for h in range(2):
                            psj = h + (sc % 3) * 2
                            qd = qdep if qdep is not None else BQTl[hp]
                            k.waits_for("pe", reads=[BKTl[hp], qd],
                                        writes=[self.Bps[psj]])
                    for sc in scg:
                        c0 = max(0, sc - tcc * 4) * 128
                        for h in range(2):
                            psj = h + (sc % 3) * 2
                            r0 = h * 64
                            qoff = rq_off if rq_off is not None else hp * T + tcc * 512
                            qd = qdep if qdep is not None else BQTl[hp]
                            k.op("pe", lambda e, hp=hp, sc=sc, h=h, psj=psj, r0=r0,
                                 c0=c0, rhs_q=rhs_q, qoff=qoff: e.matmul(
                                self.ps[psj][:, c0:512],
                                KT[r0:r0 + 64, hp * T + sc * 128: hp * T + sc * 128 + 128],
                                rhs_q[r0:r0 + 64, qoff + c0: qoff + 512],
                                start=True, stop=not bd, tile_position=(r0, 0)),
                                reads=[BKTl[hp], qd], writes=[self.Bps[psj]],
                                inc=not bd, nowait=True)
                    if bd:
                        for sc in scg:
                            jmin = max(0, sc - tcc * 4)
                            for h in range(2):
                                psj = h + (sc % 3) * 2
                                w, Bw = self.bdsh[h][sc % 3]
                                k.waits_for("pe", reads=[Bw, self.Bp["ident"]],
                                            writes=[self.Bps[psj]])
                        for sc in scg:
                            jmin = max(0, sc - tcc * 4)
                            for h in range(2):
                                psj = h + (sc % 3) * 2
                                w, Bw = self.bdsh[h][sc % 3]
                                for j in range(jmin, 4):
                                    k.op("pe", lambda e, j=j, jmin=jmin, psj=psj,
                                         w=w: e.matmul(
                                        self.ps[psj][:, j * 128:(j + 1) * 128],
                                        w[:, (j - jmin) * 128: (j - jmin) * 128 + 128],
                                        self.sb_p["ident"][:, :],
                                        start=False, stop=(j == 3)),
                                        reads=[Bw, self.Bp["ident"]],
                                        writes=[self.Bps[psj]], inc=(j == 3),
                                        nowait=True)
                    for sc in scg:
                        jmin = max(0, sc - tcc * 4)
                        c0 = jmin * 128
                        for h in range(2):
                            psj = h + (sc % 3) * 2
                            pb, Bpl = Pt[h], BPt[h]
                            po = sc * 512 + c0
                            ncol = 512 - c0
                            k.op("act", lambda e, psj=psj, pb=pb, po=po, ncol=ncol,
                                 c0=c0: e.activation(
                                pb[:, po: po + ncol], self.ps[psj][:, c0:512], AF.Exp,
                                scale=SCALE),
                                reads=[self.Bps[psj]], writes=[Bpl[sc]])
                            if not bd:
                                k.op("dve", lambda e, pb=pb, po=po, ncol=ncol: e.tensor_scalar(
                                    pb[:, po: po + ncol], pb[:, po: po + ncol],
                                    EMIN, EMAX, ALU.max, ALU.min),
                                    reads=[Bpl[sc]], writes=[Bpl[sc]])
                            if sc >= tcc * 4:
                                dof = sc * 512 + (sc - tcc * 4) * 128
                                k.op("dve", lambda e, pb=pb, dof=dof: e.copy_predicated(
                                    pb[:, dof: dof + 128], tri[:, :], zero[:, :]),
                                    reads=[self.Bp["tri"], self.Bp["zero"], Bpl[sc]],
                                    writes=[Bpl[sc]])
                            if DEBUG_TAPS and hp == 0 and h == 0 and not bd:
                                tp = self._tapP
                                k.dma("sp", bass.AP(tp.tensor,
                                                    (sc * 128) * T + tcc * 512 + c0,
                                                    [[T, 128], [1, ncol]]),
                                      pb[:, po: po + ncol], reads=[Bpl[sc]])
                if prev is not None:
                    emit_pv(tcc, prev[0], prev[1], onc, Bonc)
                prev = (hp, pset)
            emit_pv(tcc, prev[0], prev[1], onc, Bonc)
            for j in range(4):
                tt = tcc * 4 + j
                self.rowstore_bf(self.s["octx"], tt * 128, D,
                                 onc[:, j * 1024: j * 1024 + 1024], Bonc,
                                 self.Bs["octx"])

    def t_tile_to_dram(self, hb, Bhb, ht, Bht, dram, Bdram, tt, psj):
        k, T = self.k, self.T
        psb = self.ps[psj].ap().bitcast(BF16)
        for c in range(8):
            k.op("pe", lambda e, c=c, hb=hb, psb=psb: e.matmul(
                psb[:, c * 128:(c + 1) * 128], hb[:, c * 128:(c + 1) * 128],
                self.sb_p["ident"][:, :], start=True, stop=True,
                is_transpose=True, skip_group_check=True),
                reads=[Bhb, self.Bp["ident"]], writes=[self.Bps[psj]], inc=(c == 7))
        k.op("dve", lambda e, ht=ht, psb=psb: e.tensor_copy(ht[:, :], psb[:, 0:1024]),
             reads=[self.Bps[psj]], writes=[Bht])
        k.dma("act",
              bass.AP(dram.tensor, tt * 128, [[T, 128], [128 * T, 8], [1, 128]]),
              self.ap3(ht, 0, [128, 8], [1, 128]), reads=[Bht], writes_acc=[Bdram])

    def alloc_w1(self, es):
        W1, _ = self.sb(es, "W1", [128, 8 * FF], BF16, side="right")
        BW1g = [self.k.buf(f"W1g{g}") for g in range(4)]
        return W1, BW1g

    def load_w1(self, W1, BW1g, src_tensor, base_off):
        for fg in range(4):
            for dc in range(8):
                self.k.dma("pool",
                           W1[:, dc * FF + fg * 1024: dc * FF + fg * 1024 + 1024],
                           bass.AP(src_tensor, base_off + dc * 128 * FF + fg * 1024,
                                   [[FF, 128], [1, 1024]]),
                           writes_acc=[BW1g[fg]])

    def load_w2(self, W2, BW2g, src_tensor, base_off):
        for fg in range(4):
            for f8 in range(8):
                fc = fg * 8 + f8
                self.k.dma("pool", W2[:, fc * D:(fc + 1) * D],
                           bass.AP(src_tensor, base_off + fc * 128 * D,
                                   [[D, 128], [1, D]]),
                           writes_acc=[BW2g[fg]])

    def ff_moe_out(self, gW1, W1, BW1g):
        k, T, TT, NTC = self.k, self.T, self.TT, self.NTC
        gW = ExitStack()
        W2, _ = self.sb(gW, "W2", [128, 32 * D], BF16, side="right")
        BW2g = [k.buf(f"W2g{g}") for g in range(4)]
        hid, Bhid = self.sb(gW, "hidT", [128, 32 * 256], BF16, side="right")

        gF = ExitStack()
        g2s, Bg2s = self.sb(gF, "g2s", [128, D], F32)
        b2s, Bb2s = self.sb(gF, "b2s", [128, D], F32)
        bf2, Bbf2 = self.sb(gF, "bff2s", [128, D], F32)
        k.dma("sp", g2s[:, :], self.i["g2"], writes=[Bg2s])
        k.dma("sp", b2s[:, :], self.i["b2"], writes=[Bb2s])
        k.dma("sp", bf2[:, :], self.i["bff2r"], writes=[Bbf2])
        self.load_w2(W2, BW2g, self.i["fWff2"].tensor, 0)
        hw = [self.sb(gF, f"hw{j}", [128, 8 * 256], BF16) for j in range(2)]
        h1r = [self.sb(gF, f"h1r{j}", [128, D], F32) for j in range(2)]
        h2b = [self.sb(gF, f"h2b{j}", [128, D], BF16) for j in range(2)]
        h2ts = [self.sb(gF, f"h2ts{j}", [128, 1024], BF16) for j in range(2)]
        for tc in range(NTC):
            w, Bw = hw[tc % 2]
            k.dma("sp", w[:, :],
                  bass.AP(self.s["h1T"].tensor, tc * 256,
                          [[T, 128], [128 * T, 8], [1, 256]]),
                  reads=[self.Bs["h1T"]], writes=[Bw])
            self.ff_block(W1, BW1g, W2, BW2g, hid, Bhid, w, Bw,
                          (self.sb_p["bff1c"], self.Bp["bff1c"], 0))
            for t2 in range(2):
                ti = tc * 2 + t2
                hr, Bhr = h1r[t2]
                self.rowload("sp", hr[:, :], self.s["h1"], ti * 128, D, Bhr,
                             Bsrc=self.Bs["h1"])
                for dh in range(2):
                    psj = 4 + t2 * 2 + dh
                    k.op("dve", lambda e, psj=psj, dh=dh, hr=hr: e.tensor_tensor(
                        hr[:, dh * 512: dh * 512 + 512], self.ps[psj][:, :],
                        hr[:, dh * 512: dh * 512 + 512], ALU.add),
                        reads=[self.Bps[psj], Bhr], writes=[Bhr])
                    k.op("dve", lambda e, dh=dh, hr=hr: e.tensor_tensor(
                        hr[:, dh * 512: dh * 512 + 512],
                        hr[:, dh * 512: dh * 512 + 512],
                        bf2[:, dh * 512: dh * 512 + 512], ALU.add),
                        reads=[Bhr, Bbf2], writes=[Bhr])
                self.ln_tile(hr[:, :], Bhr, hr[:, :], Bhr,
                             g=g2s[:, :], Bg=Bg2s, b=b2s[:, :], Bb=Bb2s, slot=ti)
                self.rowstore(self.s["h2"], ti * 128, D, hr[:, :], Bhr, self.Bs["h2"])
                hb, Bhb = h2b[t2]
                k.op("dve", lambda e, hb=hb, hr=hr: e.tensor_copy(hb[:, :], hr[:, :]),
                     reads=[Bhr], writes=[Bhb])
                ht, Bht = h2ts[t2]
                self.t_tile_to_dram(hb, Bhb, ht, Bht, self.s["h2T"], self.Bs["h2T"],
                                    ti, 2)
                psj = 3
                for dc in range(8):
                    k.op("pe", lambda e, dc=dc, ht=ht, psj=psj: e.matmul(
                        self.ps[psj][:, 0:8], ht[:, dc * 128:(dc + 1) * 128],
                        self.sb_p["wg"][:, dc * 8: dc * 8 + 8],
                        start=(dc == 0), stop=(dc == 7)),
                        reads=[Bht, self.Bp["wg"]], writes=[self.Bps[psj]],
                        inc=(dc == 7))
                self.gating(psj, ti)
        k.bury(Bg2s, Bb2s, Bbf2, *[b for _, b in hw], *[b for _, b in h1r],
               *[b for _, b in h2b], *[b for _, b in h2ts])
        gF.close()

        gM = ExitStack()
        msel, Bmsel = self.sb(gM, "msel", [128, TT * 1024], BF16)
        k.op("dve", lambda e: e.memset(msel[:, :], 0.0), writes=[Bmsel])
        mhw = [self.sb(gM, f"mhw{j}", [128, 8 * 256], BF16) for j in range(2)]
        for ex in range(NE):
            self.load_w1(W1, BW1g, self.i["fWe1"].tensor, ex * D * FF)
            self.load_w2(W2, BW2g, self.i["fWe2"].tensor, ex * FF * D)
            for tc in range(NTC):
                w, Bw = mhw[tc % 2]
                k.dma("sp", w[:, :],
                      bass.AP(self.s["h2T"].tensor, tc * 256,
                              [[T, 128], [128 * T, 8], [1, 256]]),
                      reads=[self.Bs["h2T"]], writes=[Bw])
                self.ff_block(W1, BW1g, W2, BW2g, hid, Bhid, w, Bw,
                              (self.sb_p["be1c"], self.Bp["be1c"], ex * 32))
                for t2 in range(2):
                    ti = tc * 2 + t2
                    for dh in range(2):
                        psj = 4 + t2 * 2 + dh
                        k.op("dve", lambda e, psj=psj, ti=ti, dh=dh, ex=ex: e.scalar_tensor_tensor(
                            msel[:, ti * 1024 + dh * 512: ti * 1024 + dh * 512 + 512],
                            self.ps[psj][:, :],
                            self.sb_p["maskg"][:, ti * 8 + ex: ti * 8 + ex + 1],
                            msel[:, ti * 1024 + dh * 512: ti * 1024 + dh * 512 + 512],
                            ALU.mult, ALU.add),
                            reads=[self.Bps[psj], self.Bp["maskg"], Bmsel],
                            writes=[Bmsel])
        k.bury(*BW1g, *BW2g, Bhid, *[b for _, b in mhw])
        gW.close()
        gW1.close()

        gO = ExitStack()
        WoS, BWoS = self.sb(gO, "fWoutS", [128, 8 * D], BF16, side="right")
        be2b, Bbe2b = self.sb(gO, "be2b", [NE, D], BF16, side="right")
        g3s, Bg3s = self.sb(gO, "g3s", [128, D], F32, side="right")
        b3s, Bb3s = self.sb(gO, "b3s", [128, D], F32, side="right")
        for dc in range(8):
            k.dma("pool", WoS[:, dc * D:(dc + 1) * D],
                  bass.AP(self.i["fWout"].tensor, dc * 128 * D, [[D, 128], [1, D]]),
                  writes=[BWoS])
        k.dma("pool", be2b[:, :], self.i["be2"], writes=[Bbe2b])
        k.dma("sp", g3s[:, :], self.i["g3"], writes=[Bg3s])
        k.dma("sp", b3s[:, :], self.i["b3"], writes=[Bb3s])
        h2r = [self.sb(gO, f"h2r{j}", [128, D], F32, side="right") for j in range(3)]
        h3b = [self.sb(gO, f"h3b{j}", [128, D], BF16, side="right") for j in range(3)]
        h3ts = [self.sb(gO, f"h3ts{j}", [128, D], BF16, side="right") for j in range(3)]
        x2r = [self.sb(gO, f"x2r{j}", [128, D], F32, side="right") for j in range(3)]
        outt = [self.sb(gO, f"ou{j}", [128, D], F32, side="right") for j in range(3)]
        h3f = [self.sb(gO, f"h3f{j}", [128, D], F32, side="right") for j in range(2)]

        def go_s2(tt):
            hr, Bhr = h2r[tt % 3]
            lf, Blf = h3f[tt % 2]
            self.ln_tile(hr[:, :], Bhr, lf[:, :], Blf,
                         g=g3s[:, :], Bg=Bg3s, b=b3s[:, :], Bb=Bb3s, slot=tt)
            hb, Bhb = h3b[tt % 3]
            k.op("dve", lambda e, hb=hb, lf=lf: e.tensor_copy(hb[:, :], lf[:, :]),
                 reads=[Blf], writes=[Bhb])

        def go_s3(tt):
            hb, Bhb = h3b[tt % 3]
            ht, Bht = h3ts[tt % 3]
            psjT = 2 + tt % 2
            psb = self.ps[psjT].ap().bitcast(BF16)
            for c in range(8):
                k.op("pe", lambda e, c=c, hb=hb, psb=psb: e.matmul(
                    psb[:, c * 128:(c + 1) * 128], hb[:, c * 128:(c + 1) * 128],
                    self.sb_p["ident"][:, :], start=True, stop=True,
                    is_transpose=True, skip_group_check=True),
                    reads=[Bhb, self.Bp["ident"]], writes=[self.Bps[psjT]],
                    inc=(c == 7))
            k.op("dve", lambda e, ht=ht, psb=psb: e.tensor_copy(ht[:, :],
                                                               psb[:, 0:1024]),
                 reads=[self.Bps[psjT]], writes=[Bht])
            xr, Bxr = x2r[tt % 3]
            self.rowload("sp", xr[:, :], self.s["x2"], tt * 128, D, Bxr,
                         Bsrc=self.Bs["x2"])
            ot, Bot = outt[tt % 3]
            for half in range(2):
                psj = 4 + (tt % 2) * 2 + half
                for dc in range(8):
                    k.op("pe", lambda e, dc=dc, tt=tt, half=half, psj=psj, ht=ht: e.matmul(
                        self.ps[psj][:, :],
                        ht[:, dc * 128:(dc + 1) * 128],
                        WoS[:, dc * D + half * 512: dc * D + half * 512 + 512],
                        start=(dc == 0), stop=(dc == 7)),
                        reads=[Bht, BWoS], writes=[self.Bps[psj]], inc=(dc == 7))
                k.op("dve", lambda e, half=half, psj=psj, ot=ot, xr=xr: e.tensor_tensor(
                    ot[:, half * 512: half * 512 + 512], self.ps[psj][:, :],
                    xr[:, half * 512: half * 512 + 512], ALU.add),
                    reads=[self.Bps[psj], Bxr], writes=[Bot])
            k.dma("act", bass.AP(self.out.tensor, tt * 128 * D, [[D, 128], [1, D]]),
                  ot[:, :], reads=[Bot])

        def go_s1(tt):
            hr, Bhr = h2r[tt % 3]
            self.rowload("sp", hr[:, :], self.s["h2"], tt * 128, D, Bhr,
                         Bsrc=self.Bs["h2"])
            for dh in range(2):
                psj = dh
                k.op("pe", lambda e, tt=tt, dh=dh, psj=psj: e.matmul(
                    self.ps[psj][:, :],
                    self.sb_p["maskgT"][0:NE, tt * 128: tt * 128 + 128],
                    be2b[0:NE, dh * 512: dh * 512 + 512],
                    start=True, stop=True),
                    reads=[self.Bp["maskgT"], Bbe2b], writes=[self.Bps[psj]])
                k.op("dve", lambda e, psj=psj, dh=dh, tt=tt, hr=hr: e.tensor_tensor(
                    hr[:, dh * 512: dh * 512 + 512], self.ps[psj][:, :],
                    hr[:, dh * 512: dh * 512 + 512], ALU.add),
                    reads=[self.Bps[psj], Bhr], writes=[Bhr])
                k.op("dve", lambda e, dh=dh, tt=tt, hr=hr: e.tensor_tensor(
                    hr[:, dh * 512: dh * 512 + 512],
                    hr[:, dh * 512: dh * 512 + 512],
                    msel[:, tt * 1024 + dh * 512: tt * 1024 + dh * 512 + 512],
                    ALU.add),
                    reads=[Bhr, Bmsel], writes=[Bhr])

        for it in range(TT + 2):
            if it < TT:
                go_s1(it)
            if 1 <= it <= TT:
                go_s2(it - 1)
            if it >= 2:
                go_s3(it - 2)
        gM.close()
        gO.close()

    def ff_block(self, W1, BW1g, W2, BW2g, hid, Bhid, hT, BhT, bias1):
        k = self.k
        b1sb, Bb1, boff = bias1
        for fc in range(32):
            psj = fc % 2
            for dc in range(8):
                k.op("pe", lambda e, fc=fc, dc=dc, psj=psj: e.matmul(
                    self.ps[psj][:, 0:256],
                    W1[:, dc * FF + fc * 128: dc * FF + fc * 128 + 128],
                    hT[:, dc * 256: dc * 256 + 256],
                    start=(dc == 0), stop=(dc == 7)),
                    reads=[BW1g[fc // 8], BhT], writes=[self.Bps[psj]], inc=(dc == 7))
            k.op("act", lambda e, fc=fc, psj=psj: e.activation(
                hid[:, fc * 256: fc * 256 + 256], self.ps[psj][:, 0:256], AF.Relu,
                bias=b1sb[:, boff + fc: boff + fc + 1]),
                reads=[self.Bps[psj], Bb1], writes=[Bhid])
        for t2 in range(2):
            for dh in range(2):
                psj = 4 + t2 * 2 + dh
                for fc in range(32):
                    k.op("pe", lambda e, fc=fc, t2=t2, dh=dh, psj=psj: e.matmul(
                        self.ps[psj][:, :],
                        hid[:, fc * 256 + t2 * 128: fc * 256 + t2 * 128 + 128],
                        W2[:, fc * D + dh * 512: fc * D + dh * 512 + 512],
                        start=(fc == 0), stop=(fc == 31)),
                        reads=[Bhid, BW2g[fc // 8]], writes=[self.Bps[psj]],
                        inc=(fc == 31))

    def gating(self, psj, ti):
        k, p, B = self.k, self.sb_p, self.Bp
        k.op("dve", lambda e: e.reduce_max(p["gm"][:, :], self.ps[psj][:, 0:8], axis=AX.X),
             reads=[self.Bps[psj]], writes=[B["gm"]])
        k.op("dve", lambda e: e.tensor_scalar(p["ge"][:, :], self.ps[psj][:, 0:8],
                                              p["gm"][:, :], None, ALU.subtract,
                                              ALU.bypass),
             reads=[self.Bps[psj], B["gm"]], writes=[B["ge"]])
        k.op("act", lambda e: e.activation(p["ge"][:, :], p["ge"][:, :], AF.Exp,
                                           accum_out=p["gz"][:, :]),
             reads=[B["ge"]], writes=[B["ge"], B["gz"]])
        k.op("dve", lambda e: e.tensor_scalar(p["gv"][:, :], p["ge"][:, :], 1.0, None,
                                              ALU.is_equal, ALU.bypass),
             reads=[B["ge"]], writes=[B["gv"]])
        k.op("dve", lambda e: e.tensor_scalar(p["ge"][:, :], p["iota8"][:, :], -1.0,
                                              8.0, ALU.mult, ALU.add),
             reads=[B["iota8"]], writes=[B["ge"]])
        k.op("dve", lambda e: e.tensor_tensor(p["ge"][:, :], p["ge"][:, :],
                                              p["gv"][:, :], ALU.mult),
             reads=[B["ge"], B["gv"]], writes=[B["ge"]])
        k.op("dve", lambda e: e.reduce_max(p["gm"][:, :], p["ge"][:, :], axis=AX.X),
             reads=[B["ge"]], writes=[B["gm"]])
        k.op("dve", lambda e: e.tensor_scalar(p["gv"][:, :], p["ge"][:, :],
                                              p["gm"][:, :], None, ALU.is_equal,
                                              ALU.bypass),
             reads=[B["ge"], B["gm"]], writes=[B["gv"]])
        k.op("dve", lambda e: e.reciprocal(p["gz"][:, :], p["gz"][:, :]),
             reads=[B["gz"]], writes=[B["gz"]])
        k.op("dve", lambda e, ti=ti: e.tensor_scalar(
            p["maskg"][:, ti * 8: ti * 8 + 8], p["gv"][:, :], p["gz"][:, :], None,
            ALU.mult, ALU.bypass),
            reads=[B["gv"], B["gz"]], writes=[B["maskg"]])
        psT = 2
        k.op("pe", lambda e, ti=ti: e.matmul(
            self.ps[psT][0:8, 0:128], p["maskg"][:, ti * 8: ti * 8 + 8],
            p["identf"][:, :], start=True, stop=True, is_transpose=True),
            reads=[B["maskg"], B["identf"]], writes=[self.Bps[psT]])
        k.op("dve", lambda e, ti=ti: e.tensor_copy(
            p["maskgT"][0:NE, ti * 128: ti * 128 + 128], self.ps[psT][0:8, 0:128]),
            reads=[self.Bps[psT]], writes=[B["maskgT"]])


_CACHE = {}


def build_program(T):
    if T not in _CACHE:
        _CACHE[T] = Builder(T).build()
    return _CACHE[T]


def make_inputs(full, T):
    peT, ident, identf, tri, iota8 = host_consts(T)
    rep = lambda v: np.ascontiguousarray(np.broadcast_to(v, (128, D)))
    u = full["L0_ffn_u_bias"]
    v = full["L0_ffn_v_bias"]
    uv = np.ascontiguousarray(
        np.concatenate([np.asarray(u).reshape(8, 128).T,
                        np.asarray(v).reshape(8, 128).T], axis=1))
    be1c = np.ascontiguousarray(
        np.asarray(full["L0_ffn_be1"]).reshape(8, 32, 128).transpose(2, 0, 1)
        .reshape(128, 256))
    bff1c = np.ascontiguousarray(np.asarray(full["L0_ffn_bff1"]).reshape(32, 128).T)
    shared = dict(
        aWqkv=full["L0_attn_Wqkv"], aWout=full["L0_attn_Wout"],
        fWqkv=full["L0_ffn_Wqkv"], fWo=full["L0_ffn_Wo"], fWr=full["L0_ffn_Wr"],
        fWff1=full["L0_ffn_Wff1"], fWff2=full["L0_ffn_Wff2"], fWg=full["L0_ffn_Wg"],
        fWe1=full["L0_ffn_We1"], fWe2=full["L0_ffn_We2"], fWout=full["L0_ffn_Wout"],
        uv=uv, be1c=be1c, bff1c=bff1c,
        bff2r=rep(np.asarray(full["L0_ffn_bff2"])), be2=full["L0_ffn_be2"],
        g1=rep(np.asarray(full["L0_ffn_ln1_g"])), b1=rep(np.asarray(full["L0_ffn_ln1_b"])),
        g2=rep(np.asarray(full["L0_ffn_ln2_g"])), b2=rep(np.asarray(full["L0_ffn_ln2_b"])),
        g3=rep(np.asarray(full["L0_ffn_ln3_g"])), b3=rep(np.asarray(full["L0_ffn_ln3_b"])),
        peT=peT, ident=ident, identf=identf, tri=tri, iota8=iota8,
    )
    shared = {k2: np.ascontiguousarray(v2) for k2, v2 in shared.items()}
    x = np.asarray(full["L0_attn_x"])
    in_maps = []
    for c in range(x.shape[0]):
        d = dict(shared)
        d["x"] = np.ascontiguousarray(x[c])
        in_maps.append(d)
    return in_maps


def kernel(**inputs):
    x = inputs["L0_attn_x"]
    B, T, _ = x.shape
    nc = build_program(T)
    in_maps = make_inputs(inputs, T)
    res = run_bass_kernel_spmd(nc, in_maps, core_ids=list(range(B)))
    out = np.stack([np.asarray(r["out"], dtype=np.float32) for r in res.results])
    return out


# version 11
# speedup vs baseline: 1.0025x; 1.0025x over previous
import numpy as np
import ml_dtypes
from contextlib import ExitStack
import concourse.bass as bass
import concourse.mybir as mybir
from concourse.bass_utils import run_bass_kernel_spmd

F32 = mybir.dt.float32
BF16 = mybir.dt.bfloat16
U8 = mybir.dt.uint8
AF = mybir.ActivationFunctionType
ALU = mybir.AluOpType
AX = mybir.AxisListType

D = 1024
NH = 16
FF = 4096
NE = 8
EPS = 1e-5
SCALE = 0.125
EMIN = float(np.exp(np.float32(-10.0)))
EMAX = float(np.exp(np.float32(10.0)))

DEBUG_SCRATCH_OUT = False
DEBUG_TAPS = False


class Tok:
    __slots__ = ("sem", "val")

    def __init__(self, sem, val):
        self.sem = sem
        self.val = val


class Buf:
    __slots__ = ("name", "w", "rds", "mw")

    def __init__(self, name="buf", seed=None):
        self.name = name
        self.w = None
        self.rds = dict(seed) if seed else {}
        self.mw = {}


ENGS = ("pe", "dve", "act", "pool", "sp")


class K:
    def __init__(self, nc, es, n_dma_sems=8):
        self.nc = nc
        self.eng = {"pe": nc.tensor, "dve": nc.vector, "act": nc.scalar,
                    "pool": nc.gpsimd, "sp": nc.sync}
        self.sem = {}
        self.cnt = {}
        for e in ENGS:
            self.sem[e] = es.enter_context(nc.semaphore(f"sem_{e}"))
            self.cnt[e] = 0
        self.dsem = {}
        self.dcnt = {}
        self.didx = {}
        for qe in ("sp", "pool", "act"):
            self.dsem[qe] = [es.enter_context(nc.semaphore(f"dma_{qe}{i}"))
                             for i in range(n_dma_sems)]
            self.dcnt[qe] = [0] * n_dma_sems
            self.didx[qe] = 0
        self.hw = {e: {} for e in ENGS}
        self.pending = {e: ([], []) for e in ENGS}
        self.graveyard = {}

    def _wait(self, e, tok):
        if tok is None:
            return
        key = id(tok.sem)
        if self.hw[e].get(key, -1) >= tok.val:
            return
        self.eng[e].wait_ge(tok.sem, tok.val)
        self.hw[e][key] = tok.val

    def _dep_toks(self, reads, writes):
        toks = []
        for b in reads:
            if b.w is not None:
                toks.append(b.w)
            toks.extend(b.mw.values())
        for b in writes:
            if b.w is not None:
                toks.append(b.w)
            toks.extend(b.mw.values())
            toks.extend(b.rds.values())
        return toks

    def waits_for(self, e, reads=(), writes=()):
        for t in self._dep_toks(reads, writes):
            self._wait(e, t)

    def op(self, e, fn, reads=(), writes=(), inc=True, nowait=False):
        if not nowait:
            for t in self._dep_toks(reads, writes):
                self._wait(e, t)
        instr = fn(self.eng[e])
        pr, pw = self.pending[e]
        if inc:
            self.cnt[e] += 1
            tok = Tok(self.sem[e], self.cnt[e])
            instr.then_inc(self.sem[e], 1)
            for b in list(pr) + list(reads):
                b.rds[e] = tok
            for b in list(pw) + list(writes):
                b.w = tok
                b.rds = {}
                b.mw = {}
            pr.clear()
            pw.clear()
            return tok
        pr.extend(reads)
        pw.extend(writes)
        return None

    def dma(self, qe, out_ap, in_ap, reads=(), writes=(), writes_acc=(), **kw):
        for t in self._dep_toks(reads, writes):
            self._wait(qe, t)
        for b in writes_acc:
            for t in b.rds.values():
                self._wait(qe, t)
            if b.w is not None:
                self._wait(qe, b.w)
        i = self.didx[qe]
        self.didx[qe] = (i + 1) % len(self.dsem[qe])
        sem = self.dsem[qe][i]
        if self.dcnt[qe][i] > 0:
            self._wait(qe, Tok(sem, self.dcnt[qe][i]))
        self.eng[qe].dma_start(out=out_ap, in_=in_ap, **kw).then_inc(sem, 16)
        self.dcnt[qe][i] += 16
        tok = Tok(sem, self.dcnt[qe][i])
        for b in reads:
            b.rds[f"q{qe}{i}"] = tok
        for b in writes:
            b.w = tok
            b.rds = {}
            b.mw = {}
        for b in writes_acc:
            b.mw[id(sem)] = tok
        return tok

    def buf(self, name="buf"):
        return Buf(name, seed=self.graveyard)

    def bury(self, *bufs):
        for b in bufs:
            toks = list(b.rds.values()) + list(b.mw.values())
            if b.w is not None:
                toks.append(b.w)
            for t in toks:
                key = id(t.sem)
                old = self.graveyard.get(key)
                if old is None or old.val < t.val:
                    self.graveyard[key] = t


def host_consts(T):
    pos = np.arange(T - 1, -1, -1, dtype=np.float32)
    invf = (1.0 / 10000.0 ** (np.arange(0, D, 2, dtype=np.float32) / D)).astype(np.float32)
    sinu = pos[:, None] * invf[None, :]
    pe = np.concatenate([np.sin(sinu), np.cos(sinu)], axis=1).astype(np.float32)
    peT = np.ascontiguousarray(pe.T)
    ident = np.eye(128, dtype=ml_dtypes.bfloat16)
    identf = np.eye(128, dtype=np.float32)
    tri = np.tril(np.ones((128, 128), dtype=np.uint8), k=-1)
    iota8 = np.broadcast_to(np.arange(8, dtype=np.float32), (128, 8)).copy()
    return peT, ident, identf, tri, iota8


class Builder:
    def __init__(self, T):
        self.T = T
        self.TT = T // 128
        self.TC = T // 512
        self.NTC = T // 256
        nc = bass.Bass("TRN2", target_bir_lowering=False)
        self.nc = nc
        self.es = ExitStack()
        self.es.__enter__()
        self.k = K(nc, self.es)

        def din(name, shape, dt=F32):
            return nc.dram_tensor(name, shape, dt, kind="ExternalInput").ap()

        sk = "ExternalOutput" if DEBUG_SCRATCH_OUT else "Internal"

        def dscr(name, shape, dt=F32):
            return nc.dram_tensor(name, shape, dt, kind=sk).ap()

        i = self.i = {}
        i["x"] = din("x", [T, D])
        i["aWqkv"] = din("aWqkv", [3 * D, D])
        i["aWout"] = din("aWout", [D, D])
        i["fWqkv"] = din("fWqkv", [D, 3 * D])
        i["fWo"] = din("fWo", [D, D])
        i["fWr"] = din("fWr", [D, D])
        i["fWff1"] = din("fWff1", [D, FF])
        i["fWff2"] = din("fWff2", [FF, D])
        i["fWg"] = din("fWg", [D, NE])
        i["fWe1"] = din("fWe1", [NE, D, FF])
        i["fWe2"] = din("fWe2", [NE, FF, D])
        i["fWout"] = din("fWout", [D, D])
        i["uv"] = din("uv", [128, 16])
        i["be1c"] = din("be1c", [128, NE * 32])
        i["bff1c"] = din("bff1c", [128, 32])
        i["bff2r"] = din("bff2r", [128, D])
        i["be2"] = din("be2", [NE, D])
        for nm in ("g1", "b1", "g2", "b2", "g3", "b3"):
            i[nm] = din(nm, [128, D])
        i["peT"] = din("peT", [D, T])
        i["ident"] = din("ident", [128, 128], BF16)
        i["identf"] = din("identf", [128, 128])
        i["tri"] = din("tri", [128, 128], U8)
        i["iota8"] = din("iota8", [128, 8])

        s = self.s = {}
        self.Bs = {}
        s["bd"] = [dscr(f"bd{h}", [T * T], BF16) for h in range(NH)]
        self.Bs["bd"] = [self.k.buf(f"bd{h}") for h in range(NH)]
        self.Bs["bdg"] = [self.k.buf(f"bdg{h}") for h in range(NH)]
        for nm, shape, dt in [("x2", [T, D], F32), ("lnx2", [T, D], F32),
                              ("h1", [T, D], F32), ("h2", [T, D], F32),
                              ("h1T", [D, T], BF16), ("h2T", [D, T], BF16),
                              ("octx", [T, D], BF16)]:
            s[nm] = dscr(nm, shape, dt)
            self.Bs[nm] = self.k.buf(nm)
        self.out = nc.dram_tensor("out", [T, D], F32, kind="ExternalOutput").ap()
        self._tapP = (nc.dram_tensor("tapP", [T, T], BF16, kind="ExternalOutput").ap()
                      if DEBUG_TAPS else None)

        self.ps = [self.es.enter_context(nc.psum_tensor(f"ps{j}", [128, 512], F32))
                   for j in range(8)]
        self.Bps = [self.k.buf(f"ps{j}") for j in range(8)]

        self.sb_p = {}
        self.Bp = {}
        for nm, shape, dt in [
            ("ident", [128, 128], BF16), ("identf", [128, 128], F32),
            ("tri", [128, 128], U8), ("iota8", [128, 8], F32),
            ("uv", [128, 16], F32), ("wg", [128, NE * 8], BF16),
            ("be1c", [128, NE * 32], F32), ("bff1c", [128, 32], F32),
            ("zero", [128, 128], BF16), ("ones1", [1, 128], BF16),
            ("epsc", [128, 1], F32),
            ("bn12a", [128, 12], F32), ("mva", [128, 2], F32),
            ("sda", [128, 1], F32), ("rstda", [128, 1], F32),
            ("bn12b", [128, 12], F32), ("mvb", [128, 2], F32),
            ("sdb", [128, 1], F32), ("rstdb", [128, 1], F32),
            ("zrec", [128, 1], F32),
            ("maskg", [128, (T // 128) * 8], F32),
            ("maskgT", [NE, T], BF16),
            ("gm", [128, 1], F32), ("gz", [128, 1], F32),
            ("ge", [128, 8], F32), ("gv", [128, 8], F32),
        ]:
            self.sb_p[nm] = self.es.enter_context(nc.sbuf_tensor("c_" + nm, shape, dt))
            self.Bp[nm] = self.k.buf(nm)

    def tap(self, name, src_ap, Bsrc, shape):
        if not DEBUG_TAPS:
            return
        t = self.nc.dram_tensor("tap_" + name, list(shape), src_ap.dtype,
                                kind="ExternalOutput").ap()
        n = int(np.prod(shape))
        self.k.dma("sp", bass.AP(t.tensor, 0, [[1, n]]) if False else t, src_ap,
                   reads=[Bsrc])

    def sb(self, es, name, shape, dt, side="left"):
        t = es.enter_context(self.nc.sbuf_tensor(name, shape, dt, side=side))
        return t, self.k.buf(name)

    def ap3(self, t, off, d1, d2):
        free = int(np.prod(t.shape[1:]))
        return bass.AP(t, off, [[free, t.shape[0]], list(d1), list(d2)])

    def rowload(self, qe, dst_sb, dram_ap, r0, ncols, Bdst, nrows=128, Bsrc=None):
        reads = [Bsrc] if Bsrc is not None else []
        self.k.dma(qe, dst_sb, bass.AP(dram_ap.tensor, r0 * ncols,
                                       [[ncols, nrows], [1, ncols]]),
                   reads=reads, writes=[Bdst])

    def rowstore(self, dst_dram_ap, r0, ncols, src_sb, Bsrc, Bdst, qe="act"):
        self.k.dma(qe, bass.AP(dst_dram_ap.tensor, r0 * ncols,
                               [[ncols, 128], [1, ncols]]),
                   src_sb, reads=[Bsrc], writes_acc=[Bdst])

    def rowstore_bf(self, dst_dram_ap, r0, ncols, src_sb, Bsrc, Bdst):
        self.rowstore(dst_dram_ap, r0, ncols, src_sb, Bsrc, Bdst, qe="pool")

    def ctx_proj_tile(self, tt, W, BW, stgb, Bstgb, stgT, BstgT, psjT, pbase,
                      dst, Bdst, res, Bres):
        k, T = self.k, self.T
        self.rowload("sp", stgb[:, :], self.s["octx"], tt * 128, D, Bstgb,
                     Bsrc=self.Bs["octx"])
        psb = self.ps[psjT].ap().bitcast(BF16)
        for c in range(8):
            k.op("pe", lambda e, c=c: e.matmul(
                psb[:, c * 128:(c + 1) * 128], stgb[:, c * 128:(c + 1) * 128],
                self.sb_p["ident"][:, :], start=True, stop=True,
                is_transpose=True, skip_group_check=True),
                reads=[Bstgb, self.Bp["ident"]], writes=[self.Bps[psjT]],
                inc=(c == 7))
        k.op("dve", lambda e: e.tensor_copy(stgT[:, :], psb[:, 0:1024]),
             reads=[self.Bps[psjT]], writes=[BstgT])
        for half in range(2):
            pj2 = pbase + half
            for dc in range(8):
                k.op("pe", lambda e, dc=dc, half=half, pj2=pj2: e.matmul(
                    self.ps[pj2][:, :], stgT[:, dc * 128:(dc + 1) * 128],
                    W[:, dc * D + half * 512: dc * D + half * 512 + 512],
                    start=(dc == 0), stop=(dc == 7)),
                    reads=[BstgT, BW], writes=[self.Bps[pj2]], inc=(dc == 7))
            k.op("dve", lambda e, half=half, pj2=pj2, dst=dst, res=res: e.tensor_tensor(
                dst[:, half * 512: half * 512 + 512], self.ps[pj2][:, :],
                res[:, half * 512: half * 512 + 512], ALU.add),
                reads=[self.Bps[pj2], Bres], writes=[Bdst])

    def ln_tile(self, src, Bsrc, out_f32, Bout_f32, out_bf=None, Bout_bf=None,
                g=None, Bg=None, b=None, Bb=None, slot=0):
        k, p, B = self.k, self.sb_p, self.Bp
        sx = "a" if slot % 2 == 0 else "b"
        bn12, mv, sd, rstd = (p["bn12" + sx], p["mv" + sx], p["sd" + sx],
                              p["rstd" + sx])
        Bbn, Bmv, Bsd, Brs = (B["bn12" + sx], B["mv" + sx], B["sd" + sx],
                              B["rstd" + sx])
        k.op("dve", lambda e: e.bn_stats(bn12[:, 0:6], src[:, 0:512]),
             reads=[Bsrc], writes=[Bbn], inc=False)
        k.op("dve", lambda e: e.bn_stats(bn12[:, 6:12], src[:, 512:1024]),
             reads=[Bsrc], writes=[Bbn])
        k.op("dve", lambda e: e.bn_aggr(mv[:, :], bn12[:, :]),
             reads=[Bbn], writes=[Bmv])
        k.op("act", lambda e: e.activation(sd[:, :], mv[:, 1:2], AF.Sqrt,
                                           bias=p["epsc"][:, :]),
             reads=[Bmv, B["epsc"]], writes=[Bsd])
        k.op("dve", lambda e: e.reciprocal(rstd[:, :], sd[:, :]),
             reads=[Bsd], writes=[Brs])
        if g is None:
            k.op("dve", lambda e: e.tensor_scalar(
                out_f32, src, mv[:, 0:1], rstd[:, :], ALU.subtract, ALU.mult),
                reads=[Bsrc, Bmv, Brs], writes=[Bout_f32])
        else:
            k.op("dve", lambda e: e.tensor_scalar(
                out_f32, src, mv[:, 0:1], None, ALU.subtract, ALU.bypass),
                reads=[Bsrc, Bmv], writes=[Bout_f32])
            k.op("dve", lambda e: e.scalar_tensor_tensor(
                out_f32, out_f32, rstd[:, :], g, ALU.mult, ALU.mult),
                reads=[Bout_f32, Brs, Bg], writes=[Bout_f32])
            k.op("dve", lambda e: e.tensor_tensor(out_f32, out_f32, b, ALU.add),
                 reads=[Bout_f32, Bb], writes=[Bout_f32])
        if out_bf is not None:
            k.op("dve", lambda e: e.tensor_copy(out_bf, out_f32),
                 reads=[Bout_f32], writes=[Bout_bf])

    def transpose_tile(self, src, Bsrc, dst_t, Bdst, tt, psj, nch=8):
        k, T = self.k, self.T
        psb = self.ps[psj].ap().bitcast(BF16)
        idn = self.sb_p["ident"]
        for c in range(nch):
            k.op("pe", lambda e, c=c: e.matmul(
                psb[:, c * 128:(c + 1) * 128], src[:, c * 128:(c + 1) * 128],
                idn[:, :], start=True, stop=True, is_transpose=True,
                skip_group_check=True),
                reads=[Bsrc, self.Bp["ident"]], writes=[self.Bps[psj]],
                inc=(c == nch - 1))
        k.op("dve", lambda e: e.tensor_copy(
            self.ap3(dst_t, tt * 128, [T, nch], [1, 128]), psb[:, 0:nch * 128]),
            reads=[self.Bps[psj]], writes=[Bdst])

    def load_wT(self, qe, dram, nrow, wst, dstT, BdstT, ncols_t, psj=6, row0=0):
        k = self.k
        for ec in range(nrow):
            w, Bw = wst[ec % 2]
            self.rowload(qe, w[:, :], dram, (row0 + ec) * 128, D, Bw)
            pj = psj if psj != -1 else ec % 2
            psb = self.ps[pj].ap().bitcast(BF16)
            for dc in range(8):
                k.op("pe", lambda e, dc=dc, w=w, psb=psb: e.matmul(
                    psb[:, dc * 128:(dc + 1) * 128], w[:, dc * 128:(dc + 1) * 128],
                    self.sb_p["ident"][:, :], start=True, stop=True,
                    is_transpose=True, skip_group_check=True),
                    reads=[Bw, self.Bp["ident"]], writes=[self.Bps[pj]],
                    inc=(dc == 7))
            k.op("dve", lambda e, ec=ec, psb=psb: e.tensor_copy(
                self.ap3(dstT, ec * 128, [ncols_t, 8], [1, 128]), psb[:, 0:1024]),
                reads=[self.Bps[pj]], writes=[BdstT])

    def load_consts(self):
        k, i, p, B = self.k, self.i, self.sb_p, self.Bp
        T = self.T
        for nm in ("ident", "identf", "tri", "iota8", "uv", "be1c", "bff1c"):
            k.dma("sp", p[nm][:, :], i[nm], writes=[B[nm]])
        k.dma("pool", p["wg"][:, :],
              bass.AP(i["fWg"].tensor, 0, [[8, 128], [128 * 8, 8], [1, 8]]),
              writes=[B["wg"]])
        k.op("dve", lambda e: e.memset(p["zero"][:, :], 0.0), writes=[B["zero"]])
        k.op("dve", lambda e: e.memset(p["epsc"][:, :], EPS), writes=[B["epsc"]])
        k.op("dve", lambda e: e.memset(p["ones1"][:, :], 1.0), writes=[B["ones1"]])
        zg = ExitStack()
        z2, Bz2 = self.sb(zg, "zfill", [128, 1024], BF16)
        k.op("dve", lambda e: e.memset(z2[:, :], 0.0), writes=[Bz2])
        rows_per = 1024
        for h in range(NH):
            for r0 in range(0, T, rows_per):
                nr = min(rows_per, T - r0)
                k.dma("act",
                      bass.AP(self.s["bd"][h].tensor, r0 * T, [[T, nr], [1, 128]]),
                      bass.AP(z2, 0, [[1024, 128], [1, nr]]),
                      reads=[Bz2], writes_acc=[self.Bs["bdg"][h]])
        self.k.bury(Bz2)
        zg.close()

    def build(self):
        k, T, TT, TC = self.k, self.T, self.TT, self.TC
        self.load_consts()

        g1 = ExitStack()
        lnxT, _ = self.sb(g1, "lnxT", [128, 8 * T], BF16)
        BlnxTl = [self.k.buf(f"lnxT{t}") for t in range(TT)]
        wst = [self.sb(g1, f"wst{j}", [128, D], BF16) for j in range(2)]
        xin = [self.sb(g1, f"xin{j}", [128, D], F32) for j in range(3)]
        lnb = [self.sb(g1, f"lnb{j}", [128, D], BF16) for j in range(3)]
        g2 = ExitStack()
        QT, _ = self.sb(g2, "QT", [128, 8 * T], BF16, side="right")
        KT, _ = self.sb(g2, "KT", [128, 8 * T], BF16, side="right")
        BQTl = [self.k.buf(f"QT{h}") for h in range(8)]
        BKTl = [self.k.buf(f"KT{h}") for h in range(8)]
        g1b = ExitStack()
        Wqk, BWqk = self.sb(g1b, "aWqkT", [128, 8 * 2048], BF16)

        self.load_wT("pool", self.i["aWqkv"], 16, wst, Wqk, BWqk, 2048, psj=-1)
        for tt in range(TT):
            xt, Bxt = xin[tt % 3]
            self.rowload("sp", xt[:, :], self.i["x"], tt * 128, D, Bxt)
            lb, Blb = lnb[tt % 3]
            self.ln_tile(xt[:, :], Bxt, xt[:, :], Bxt, out_bf=lb[:, :], Bout_bf=Blb,
                         slot=tt)
            self.transpose_tile(lb[:, :], Blb, lnxT, BlnxTl[tt], tt, 2 + tt % 2)
        self.qkv(Wqk, BWqk, 2048, lnxT, BlnxTl, QT, BQTl, KT, BKTl)
        k.bury(BWqk)
        g1b.close()
        g2b = ExitStack()
        VA, _ = self.sb(g2b, "VA", [128, TT * 1040], BF16, side="right")
        BVAl = [self.k.buf(f"VA{t}") for t in range(TT)]
        g1c = ExitStack()
        Wv, BWv = self.sb(g1c, "aWvT", [128, 8 * 1024], BF16)
        self.load_wT("pool", self.i["aWqkv"], 8, wst, Wv, BWv, 1024, psj=-1, row0=16)
        self.qkv_v(Wv, BWv, 1024, 0, lnxT, BlnxTl, VA, BVAl)
        k.bury(BWv, *BlnxTl, *[b for _, b in wst], *[b for _, b in xin],
               *[b for _, b in lnb])
        g1c.close()
        g1.close()

        g3 = ExitStack()
        Psets = []
        for si in range(2):
            Pa, _ = self.sb(g3, f"P{si}a", [128, TT * 512], BF16)
            Pb, _ = self.sb(g3, f"P{si}b", [128, TT * 512], BF16)
            Psets.append((Pa, [self.k.buf(f"P{si}a{t}") for t in range(TT)],
                          Pb, [self.k.buf(f"P{si}b{t}") for t in range(TT)]))
        ONc = [self.sb(g3, "ONc0", [128, 4 * 1024], BF16)]
        self.attention(QT, BQTl, KT, BKTl, VA, BVAl, Psets, ONc, bd=False,
                       QuStg=None)
        for (Pa, BPal, Pb, BPbl) in Psets:
            k.bury(*BPal, *BPbl)
        k.bury(*BQTl, *BKTl, *BVAl, *[b for _, b in ONc])
        g3.close()
        g2b.close()
        g2.close()

        g4 = ExitStack()
        lnx2T, _ = self.sb(g4, "lnx2T", [128, 8 * T], BF16, side="right")
        Blnx2Tl = [self.k.buf(f"lnx2T{t}") for t in range(TT)]
        g4b = ExitStack()
        WoT, BWoT = self.sb(g4b, "aWoutT", [128, 8 * D], BF16)
        wst2 = [self.sb(g4b, f"wsg{j}", [128, D], BF16) for j in range(2)]
        self.load_wT("pool", self.i["aWout"], 8, wst2, WoT, BWoT, D, psj=6)
        xin2 = [self.sb(g4b, f"xi2{j}", [128, D], F32) for j in range(3)]
        x2t = [self.sb(g4b, f"x2t{j}", [128, D], F32) for j in range(3)]
        lnf2 = [self.sb(g4b, f"lnf2{j}", [128, D], F32) for j in range(2)]
        lnb2 = [self.sb(g4b, f"lb2{j}", [128, D], BF16) for j in range(2)]
        ocb = [self.sb(g4b, f"ocb{j}", [128, D], BF16) for j in range(3)]
        ocT = [self.sb(g4b, f"ocT{j}", [128, D], BF16) for j in range(3)]

        def a4_s2(tt):
            x2, Bx2 = x2t[tt % 3]
            self.rowstore(self.s["x2"], tt * 128, D, x2[:, :], Bx2, self.Bs["x2"])
            lf, Blf = lnf2[tt % 2]
            self.ln_tile(x2[:, :], Bx2, lf[:, :], Blf, slot=tt)
            self.rowstore(self.s["lnx2"], tt * 128, D, lf[:, :], Blf,
                          self.Bs["lnx2"])
            lb, Blb = lnb2[tt % 2]
            k.op("dve", lambda e, lb=lb, lf=lf: e.tensor_copy(lb[:, :], lf[:, :]),
                 reads=[Blf], writes=[Blb])

        def a4_s3(tt):
            lb, Blb = lnb2[tt % 2]
            self.transpose_tile(lb[:, :], Blb, lnx2T, Blnx2Tl[tt], tt, 4 + tt % 2)

        for it in range(TT + 2):
            if it < TT:
                tt = it
                xt, Bxt = xin2[tt % 3]
                self.rowload("sp", xt[:, :], self.i["x"], tt * 128, D, Bxt)
                x2, Bx2 = x2t[tt % 3]
                sb_, Bsb_ = ocb[tt % 3]
                sT, BsT = ocT[tt % 3]
                self.ctx_proj_tile(tt, WoT, BWoT, sb_, Bsb_, sT, BsT, 6 + tt % 2,
                                   (tt % 2) * 2, x2, Bx2, xt, Bxt)
            if 1 <= it <= TT:
                a4_s2(it - 1)
            if it >= 2:
                a4_s3(it - 2)
        k.bury(BWoT, *[b for _, b in wst2], *[b for _, b in xin2],
               *[b for _, b in x2t], *[b for _, b in lnb2],
               *[b for _, b in lnf2],
               *[b for _, b in ocb], *[b for _, b in ocT])
        g4b.close()

        g6 = ExitStack()
        QT2, _ = self.sb(g6, "QT2", [128, 8 * T], BF16)
        KT2, _ = self.sb(g6, "KT2", [128, 8 * T], BF16)
        BQT2l = [self.k.buf(f"QT2{h}") for h in range(8)]
        BKT2l = [self.k.buf(f"KT2{h}") for h in range(8)]
        g6b = ExitStack()
        VA2, _ = self.sb(g6b, "VA2", [128, TT * 1040], BF16)
        BVA2l = [self.k.buf(f"VA2{t}") for t in range(TT)]
        g5 = ExitStack()
        fq, Bfq = self.sb(g5, "fqkv", [128, 8 * 3 * D], BF16, side="right")
        for dc in range(8):
            self.k.dma("pool", fq[:, dc * 3 * D:(dc + 1) * 3 * D],
                       bass.AP(self.i["fWqkv"].tensor, dc * 128 * 3 * D,
                               [[3 * D, 128], [1, 3 * D]]),
                       writes=[Bfq])
        self.qkv(fq, Bfq, 3 * D, lnx2T, Blnx2Tl, QT2, BQT2l, KT2, BKT2l)
        self.qkv_v(fq, Bfq, 3 * D, 2048, lnx2T, Blnx2Tl, VA2, BVA2l)
        k.bury(Bfq, *Blnx2Tl)
        g5.close()
        g4.close()

        g8 = ExitStack()
        rT, _ = self.sb(g8, "rT", [128, 8 * T], BF16, side="right")
        BrTl = [self.k.buf(f"rT{h}") for h in range(8)]
        g7 = ExitStack()
        peT, BpeT = self.sb(g7, "s_peT", [128, 8 * T], BF16, side="right")
        Wr, BWr = self.sb(g7, "Wr", [128, 8 * D], BF16, side="right")
        for dc in range(8):
            self.k.dma("pool", peT[:, dc * T:(dc + 1) * T],
                       bass.AP(self.i["peT"].tensor, dc * 128 * T, [[T, 128], [1, T]]),
                       writes=[BpeT])
            self.k.dma("pool", Wr[:, dc * D:(dc + 1) * D],
                       bass.AP(self.i["fWr"].tensor, dc * 128 * D, [[D, 128], [1, D]]),
                       writes=[BWr])
        for hp in range(8):
            for tc in range(TC):
                psj = (hp * TC + tc) % 2
                for dc in range(8):
                    k.op("pe", lambda e, dc=dc, hp=hp, tc=tc, psj=psj: e.matmul(
                        self.ps[psj][:, :],
                        Wr[:, dc * D + hp * 128: dc * D + hp * 128 + 128],
                        peT[:, dc * T + tc * 512: dc * T + tc * 512 + 512],
                        start=(dc == 0), stop=(dc == 7)),
                        reads=[BWr, BpeT], writes=[self.Bps[psj]], inc=(dc == 7))
                if (hp * TC + tc) % 2 == 0:
                    k.op("act", lambda e, hp=hp, tc=tc, psj=psj: e.activation(
                        rT[:, hp * T + tc * 512: hp * T + tc * 512 + 512],
                        self.ps[psj][:, :], AF.Copy),
                        reads=[self.Bps[psj]], writes=[BrTl[hp]])
                else:
                    k.op("dve", lambda e, hp=hp, tc=tc, psj=psj: e.tensor_copy(
                        rT[:, hp * T + tc * 512: hp * T + tc * 512 + 512],
                        self.ps[psj][:, :]),
                        reads=[self.Bps[psj]], writes=[BrTl[hp]])
        k.bury(BpeT, BWr)
        g7.close()

        stg8 = ExitStack()
        qv = [self.sb(stg8, f"qv{j}", [128, 128], BF16, side="right") for j in range(2)]
        bst = [self.sb(stg8, f"bst{j}", [128, TC * 512], BF16, side="right")
               for j in range(2)]
        for hp in range(8):
            for tt in range(TT):
                q, Bq = qv[tt % 2]
                k.op("dve", lambda e, hp=hp, tt=tt, q=q: e.tensor_scalar(
                    q[:, :], QT2[:, hp * T + tt * 128: hp * T + tt * 128 + 128],
                    self.sb_p["uv"][:, 8 + hp: 9 + hp], None, ALU.add, ALU.bypass),
                    reads=[BQT2l[hp], self.Bp["uv"]], writes=[Bq])
                jc_min = max(0, (T - 128 * (tt + 1)) // 512)
                njc = TC - jc_min
                for jc in range(jc_min, TC):
                    for h in range(2):
                        psj = h * 2 + (jc % 2)
                        k.waits_for("pe", reads=[Bq, BrTl[hp]],
                                    writes=[self.Bps[psj]])
                    for h in range(2):
                        psj = h * 2 + (jc % 2)
                        r0 = h * 64
                        k.op("pe", lambda e, jc=jc, hp=hp, r0=r0, q=q, psj=psj: e.matmul(
                            self.ps[psj][:, :], q[r0:r0 + 64, :],
                            rT[r0:r0 + 64, hp * T + jc * 512: hp * T + jc * 512 + 512],
                            start=True, stop=True, tile_position=(r0, 0)),
                            reads=[Bq, BrTl[hp]], writes=[self.Bps[psj]],
                            nowait=True)
                    for h in range(2):
                        psj = h * 2 + (jc % 2)
                        b, Bb = bst[h]
                        if (jc + h) % 2 == 0:
                            k.op("act", lambda e, jc=jc, jc_min=jc_min, b=b, psj=psj: e.activation(
                                b[:, (jc - jc_min) * 512: (jc - jc_min) * 512 + 512],
                                self.ps[psj][:, :], AF.Copy),
                                reads=[self.Bps[psj]], writes=[Bb])
                        else:
                            k.op("dve", lambda e, jc=jc, jc_min=jc_min, b=b, psj=psj: e.tensor_copy(
                                b[:, (jc - jc_min) * 512: (jc - jc_min) * 512 + 512],
                                self.ps[psj][:, :]),
                                reads=[self.Bps[psj]], writes=[Bb])
                for h in range(2):
                    head = 2 * hp + h
                    b, Bb = bst[h]
                    njc = TC - jc_min
                    k.dma("pool",
                          bass.AP(self.s["bd"][head].tensor,
                                  tt * 128 * T + jc_min * 512,
                                  [[T, 128], [1, njc * 512]]),
                          b[:, 0: njc * 512],
                          reads=[Bb, self.Bs["bdg"][head]],
                          writes_acc=[self.Bs["bd"][head]])
        k.bury(*[b for _, b in qv], *[b for _, b in bst], *BrTl)
        stg8.close()
        g8.close()

        g9 = ExitStack()
        Psets2 = []
        for si in range(2):
            Pa, _ = self.sb(g9, f"fP{si}a", [128, TT * 512], BF16, side="right")
            Pb, _ = self.sb(g9, f"fP{si}b", [128, TT * 512], BF16, side="right")
            Psets2.append((Pa, [self.k.buf(f"fP{si}a{t}") for t in range(TT)],
                           Pb, [self.k.buf(f"fP{si}b{t}") for t in range(TT)]))
        ONc2 = [self.sb(g9, "fONc0", [128, 4 * 1024], BF16, side="right")]
        stg9 = ExitStack()
        self.bdsh = [[[self.sb(stg9, f"bdsh{h}{j}{p}", [128, 512], BF16, side="right")
                       for p in range(2)] for j in range(4)] for h in range(2)]
        QuStg = [self.sb(stg9, f"qu{j}", [128, 512], BF16, side="right")
                 for j in range(2)]
        self.attention(QT2, BQT2l, KT2, BKT2l, VA2, BVA2l, Psets2, ONc2,
                       bd=True, QuStg=QuStg)
        for (Pa, BPal, Pb, BPbl) in Psets2:
            k.bury(*BPal, *BPbl)
        k.bury(*BQT2l, *BKT2l, *BVA2l, *[b for _, b in ONc2],
               *[b for _, b in QuStg])
        for hh in self.bdsh:
            for jj in hh:
                for _, bb in jj:
                    k.bury(bb)
        stg9.close()
        g9.close()
        g6b.close()
        g6.close()

        gW1 = ExitStack()
        W1, BW1g = self.alloc_w1(gW1)
        self.load_w1(W1, BW1g, self.i["fWff1"].tensor, 0)

        g10 = ExitStack()
        Wo, BWo = self.sb(g10, "fWoS", [128, 8 * D], BF16)
        g1s, Bg1s = self.sb(g10, "s_g1", [128, D], F32)
        b1s, Bb1s = self.sb(g10, "s_b1", [128, D], F32)
        for dc in range(8):
            self.k.dma("pool", Wo[:, dc * D:(dc + 1) * D],
                       bass.AP(self.i["fWo"].tensor, dc * 128 * D, [[D, 128], [1, D]]),
                       writes=[BWo])
        k.dma("sp", g1s[:, :], self.i["g1"], writes=[Bg1s])
        k.dma("sp", b1s[:, :], self.i["b1"], writes=[Bb1s])
        lx = [self.sb(g10, f"lx{j}", [128, D], F32) for j in range(3)]
        s1t = [self.sb(g10, f"s1{j}", [128, D], F32) for j in range(3)]
        h1b = [self.sb(g10, f"h1b{j}", [128, D], BF16) for j in range(3)]
        h1ts = [self.sb(g10, f"h1ts{j}", [128, 1024], BF16) for j in range(3)]
        ocb2 = [self.sb(g10, f"oc2{j}", [128, D], BF16) for j in range(3)]
        ocT2 = [self.sb(g10, f"oT2{j}", [128, D], BF16) for j in range(3)]
        h1f = [self.sb(g10, f"h1f{j}", [128, D], F32) for j in range(2)]

        def g10_s2(tt):
            s1, Bs1 = s1t[tt % 3]
            lf, Blf = h1f[tt % 2]
            self.ln_tile(s1[:, :], Bs1, lf[:, :], Blf,
                         g=g1s[:, :], Bg=Bg1s, b=b1s[:, :], Bb=Bb1s, slot=tt)
            self.rowstore(self.s["h1"], tt * 128, D, lf[:, :], Blf, self.Bs["h1"])
            hb, Bhb = h1b[tt % 3]
            k.op("dve", lambda e, hb=hb, lf=lf: e.tensor_copy(hb[:, :], lf[:, :]),
                 reads=[Blf], writes=[Bhb])

        def g10_s3(tt):
            hb, Bhb = h1b[tt % 3]
            ht, Bht = h1ts[tt % 3]
            self.t_tile_to_dram(hb, Bhb, ht, Bht, self.s["h1T"], self.Bs["h1T"],
                                tt, 4 + tt % 2)

        for it in range(TT + 2):
            if it < TT:
                tt = it
                l, Bl = lx[tt % 3]
                self.rowload("sp", l[:, :], self.s["lnx2"], tt * 128, D, Bl,
                             Bsrc=self.Bs["lnx2"])
                s1, Bs1 = s1t[tt % 3]
                sb_, Bsb_ = ocb2[tt % 3]
                sT, BsT = ocT2[tt % 3]
                self.ctx_proj_tile(tt, Wo, BWo, sb_, Bsb_, sT, BsT, 6 + tt % 2,
                                   (tt % 2) * 2, s1, Bs1, l, Bl)
            if 1 <= it <= TT:
                g10_s2(it - 1)
            if it >= 2:
                g10_s3(it - 2)
        k.bury(BWo, Bg1s, Bb1s, *[b for _, b in lx], *[b for _, b in s1t],
               *[b for _, b in h1b], *[b for _, b in h1ts],
               *[b for _, b in h1f],
               *[b for _, b in ocb2], *[b for _, b in ocT2])
        g10.close()

        self.ff_moe_out(gW1, W1, BW1g)

        for qe in ("sp", "pool", "act"):
            for j, c in enumerate(self.k.dcnt[qe]):
                if c > 0:
                    self.k.eng["sp"].wait_ge(self.k.dsem[qe][j], c)
        self.es.__exit__(None, None, None)
        return self.nc

    def qkv(self, Wqk, BWqk, wrow_qk, xT, BxTl, QTo, BQTl, KTo, BKTl):
        k, T, TT, TC = self.k, self.T, self.TT, self.TC
        for ec in range(16):
            dstT, Bdl = (QTo, BQTl) if ec < 8 else (KTo, BKTl)
            hp = ec % 8
            for tc in range(TC):
                psj = (ec * TC + tc) % 2
                xdeps = [BxTl[tc * 4 + i] for i in range(4)]
                for dc in range(8):
                    k.op("pe", lambda e, dc=dc, ec=ec, tc=tc, psj=psj: e.matmul(
                        self.ps[psj][:, :],
                        Wqk[:, dc * wrow_qk + ec * 128: dc * wrow_qk + ec * 128 + 128],
                        xT[:, dc * T + tc * 512: dc * T + tc * 512 + 512],
                        start=(dc == 0), stop=(dc == 7)),
                        reads=[BWqk] + xdeps, writes=[self.Bps[psj]], inc=(dc == 7))
                if (ec * TC + tc) % 2 == 0:
                    k.op("act", lambda e, hp=hp, tc=tc, psj=psj, dstT=dstT: e.activation(
                        dstT[:, hp * T + tc * 512: hp * T + tc * 512 + 512],
                        self.ps[psj][:, :], AF.Copy),
                        reads=[self.Bps[psj]], writes=[Bdl[hp]])
                else:
                    k.op("dve", lambda e, hp=hp, tc=tc, psj=psj, dstT=dstT: e.tensor_copy(
                        dstT[:, hp * T + tc * 512: hp * T + tc * 512 + 512],
                        self.ps[psj][:, :]),
                        reads=[self.Bps[psj]], writes=[Bdl[hp]])

    def qkv_v(self, Wv, BWv, wrow_v, v_off, xT, BxTl, VAo, BVAl):
        k, T, TT = self.k, self.T, self.TT
        for tt in range(TT):
            for half in range(2):
                psj = 2 + (tt * 2 + half) % 2
                for dc in range(8):
                    k.op("pe", lambda e, dc=dc, tt=tt, half=half, psj=psj: e.matmul(
                        self.ps[psj][:, :],
                        xT[:, dc * T + tt * 128: dc * T + tt * 128 + 128],
                        Wv[:, dc * wrow_v + v_off + half * 512: dc * wrow_v + v_off + half * 512 + 512],
                        start=(dc == 0), stop=(dc == 7)),
                        reads=[BxTl[tt], BWv], writes=[self.Bps[psj]], inc=(dc == 7))
                k.op("dve", lambda e, tt=tt, half=half, psj=psj: e.tensor_copy(
                    self.ap3(VAo, tt * 1040 + half * 8 * 65, [65, 8], [1, 64]),
                    self.ps[psj][:, :]),
                    reads=[self.Bps[psj]], writes=[BVAl[tt]])
            k.op("dve", lambda e, tt=tt: e.memset(
                self.ap3(VAo, tt * 1040 + 64, [65, 16], [1, 1]), 1.0),
                writes=[BVAl[tt]])

    def attention(self, QT, BQTl, KT, BKTl, VA, BVAl, Psets, ONc, bd, QuStg):
        k, T, TT, TC = self.k, self.T, self.TT, self.TC
        tri, zero = self.sb_p["tri"], self.sb_p["zero"]

        def emit_pv(tcc, hp, pset, onc, Bonc):
            P0, BP0l, P1, BP1l = pset
            Pt, BPt = (P0, P1), (BP0l, BP1l)
            for j in range(4):
                tt = tcc * 4 + j
                for h in range(2):
                    head = 2 * hp + h
                    psj = 6 + (tt * 2 + h) % 2
                    pb, Bpl = Pt[h], BPt[h]
                    for sc in range(tt + 1):
                        k.op("pe", lambda e, sc=sc, j=j, psj=psj, pb=pb,
                             tt=tt, head=head: e.matmul(
                            self.ps[psj][:, 0:65],
                            pb[:, sc * 512 + j * 128: sc * 512 + j * 128 + 128],
                            VA[:, sc * 1040 + head * 65: sc * 1040 + head * 65 + 65],
                            start=(sc == 0), stop=(sc == tt)),
                            reads=[Bpl[sc], BVAl[sc]], writes=[self.Bps[psj]],
                            inc=(sc == tt))
                    k.op("dve", lambda e, psj=psj: e.reciprocal(
                        self.sb_p["zrec"][:, :], self.ps[psj][:, 64:65]),
                        reads=[self.Bps[psj]], writes=[self.Bp["zrec"]])
                    k.op("dve", lambda e, psj=psj, j=j, head=head,
                         onc=onc: e.tensor_scalar(
                        onc[:, j * 1024 + head * 64: j * 1024 + head * 64 + 64],
                        self.ps[psj][:, 0:64], self.sb_p["zrec"][:, :], None,
                        ALU.mult, ALU.bypass),
                        reads=[self.Bps[psj], self.Bp["zrec"]], writes=[Bonc])

        for tcc in range(TC):
            onc, Bonc = ONc[0]
            prev = None
            for hp in range(8):
                pset = Psets[hp % 2]
                P0, BP0l, P1, BP1l = pset
                Pt, BPt = (P0, P1), (BP0l, BP1l)
                if bd:
                    qu, Bqu = QuStg[hp % 2]
                    k.op("dve", lambda e, hp=hp, tcc=tcc, qu=qu: e.tensor_scalar(
                        qu[:, :], QT[:, hp * T + tcc * 512: hp * T + tcc * 512 + 512],
                        self.sb_p["uv"][:, hp:hp + 1], None, ALU.add, ALU.bypass),
                        reads=[BQTl[hp], self.Bp["uv"]], writes=[Bqu])
                    rhs_q, rq_off, qdep = qu, 0, Bqu
                else:
                    rhs_q, rq_off, qdep = QT, None, None
                sc_max = tcc * 4 + 4
                GW = 3
                for g0 in range(0, sc_max, GW):
                    scg = range(g0, min(g0 + GW, sc_max))
                    gpar = (g0 // GW) % 2
                    scols = (scg[-1] - g0 + 1) * 128
                    if bd:
                        jmin_g = max(0, g0 - tcc * 4)
                        for h in range(2):
                            head = 2 * hp + h
                            for j in range(jmin_g, 4):
                                w, Bw = self.bdsh[h][j][gpar]
                                k.dma("sp", w[:, 0:scols],
                                      bass.AP(self.s["bd"][head].tensor,
                                              ((tcc * 4 + j) * 128) * (T - 1)
                                              + (T - 1) + g0 * 128,
                                              [[T - 1, 128], [1, scols]]),
                                      reads=[self.Bs["bd"][head],
                                             self.Bs["bdg"][head]],
                                      writes=[Bw])
                    for sc in scg:
                        c0 = max(0, sc - tcc * 4) * 128
                        for h in range(2):
                            psj = h + (sc % 3) * 2
                            qd = qdep if qdep is not None else BQTl[hp]
                            k.waits_for("pe", reads=[BKTl[hp], qd],
                                        writes=[self.Bps[psj]])
                    for sc in scg:
                        c0 = max(0, sc - tcc * 4) * 128
                        for h in range(2):
                            psj = h + (sc % 3) * 2
                            r0 = h * 64
                            qoff = rq_off if rq_off is not None else hp * T + tcc * 512
                            qd = qdep if qdep is not None else BQTl[hp]
                            k.op("pe", lambda e, hp=hp, sc=sc, h=h, psj=psj, r0=r0,
                                 c0=c0, rhs_q=rhs_q, qoff=qoff: e.matmul(
                                self.ps[psj][:, c0:512],
                                KT[r0:r0 + 64, hp * T + sc * 128: hp * T + sc * 128 + 128],
                                rhs_q[r0:r0 + 64, qoff + c0: qoff + 512],
                                start=True, stop=not bd, tile_position=(r0, 0)),
                                reads=[BKTl[hp], qd], writes=[self.Bps[psj]],
                                inc=not bd, nowait=True)
                    if bd:
                        for sc in scg:
                            jmin = max(0, sc - tcc * 4)
                            for h in range(2):
                                psj = h + (sc % 3) * 2
                                for j in range(jmin, 4):
                                    w, Bw = self.bdsh[h][j][gpar]
                                    k.waits_for("pe", reads=[Bw, self.Bp["ident"]],
                                                writes=[self.Bps[psj]])
                        for sc in scg:
                            jmin = max(0, sc - tcc * 4)
                            for h in range(2):
                                psj = h + (sc % 3) * 2
                                for j in range(jmin, 4):
                                    w, Bw = self.bdsh[h][j][gpar]
                                    k.op("pe", lambda e, j=j, sc=sc, g0=g0, psj=psj,
                                         w=w: e.matmul(
                                        self.ps[psj][:, j * 128:(j + 1) * 128],
                                        w[:, (sc - g0) * 128: (sc - g0) * 128 + 128],
                                        self.sb_p["ident"][:, :],
                                        start=False, stop=(j == 3)),
                                        reads=[Bw, self.Bp["ident"]],
                                        writes=[self.Bps[psj]], inc=(j == 3),
                                        nowait=True)
                    for sc in scg:
                        jmin = max(0, sc - tcc * 4)
                        c0 = jmin * 128
                        for h in range(2):
                            psj = h + (sc % 3) * 2
                            pb, Bpl = Pt[h], BPt[h]
                            po = sc * 512 + c0
                            ncol = 512 - c0
                            k.op("act", lambda e, psj=psj, pb=pb, po=po, ncol=ncol,
                                 c0=c0: e.activation(
                                pb[:, po: po + ncol], self.ps[psj][:, c0:512], AF.Exp,
                                scale=SCALE),
                                reads=[self.Bps[psj]], writes=[Bpl[sc]])
                            if not bd:
                                k.op("dve", lambda e, pb=pb, po=po, ncol=ncol: e.tensor_scalar(
                                    pb[:, po: po + ncol], pb[:, po: po + ncol],
                                    EMIN, EMAX, ALU.max, ALU.min),
                                    reads=[Bpl[sc]], writes=[Bpl[sc]])
                            if sc >= tcc * 4:
                                dof = sc * 512 + (sc - tcc * 4) * 128
                                k.op("dve", lambda e, pb=pb, dof=dof: e.copy_predicated(
                                    pb[:, dof: dof + 128], tri[:, :], zero[:, :]),
                                    reads=[self.Bp["tri"], self.Bp["zero"], Bpl[sc]],
                                    writes=[Bpl[sc]])
                            if DEBUG_TAPS and hp == 0 and h == 0 and not bd:
                                tp = self._tapP
                                k.dma("sp", bass.AP(tp.tensor,
                                                    (sc * 128) * T + tcc * 512 + c0,
                                                    [[T, 128], [1, ncol]]),
                                      pb[:, po: po + ncol], reads=[Bpl[sc]])
                if prev is not None:
                    emit_pv(tcc, prev[0], prev[1], onc, Bonc)
                prev = (hp, pset)
            emit_pv(tcc, prev[0], prev[1], onc, Bonc)
            for j in range(4):
                tt = tcc * 4 + j
                self.rowstore_bf(self.s["octx"], tt * 128, D,
                                 onc[:, j * 1024: j * 1024 + 1024], Bonc,
                                 self.Bs["octx"])

    def t_tile_to_dram(self, hb, Bhb, ht, Bht, dram, Bdram, tt, psj):
        k, T = self.k, self.T
        psb = self.ps[psj].ap().bitcast(BF16)
        for c in range(8):
            k.op("pe", lambda e, c=c, hb=hb, psb=psb: e.matmul(
                psb[:, c * 128:(c + 1) * 128], hb[:, c * 128:(c + 1) * 128],
                self.sb_p["ident"][:, :], start=True, stop=True,
                is_transpose=True, skip_group_check=True),
                reads=[Bhb, self.Bp["ident"]], writes=[self.Bps[psj]], inc=(c == 7))
        k.op("dve", lambda e, ht=ht, psb=psb: e.tensor_copy(ht[:, :], psb[:, 0:1024]),
             reads=[self.Bps[psj]], writes=[Bht])
        k.dma("act",
              bass.AP(dram.tensor, tt * 128, [[T, 128], [128 * T, 8], [1, 128]]),
              self.ap3(ht, 0, [128, 8], [1, 128]), reads=[Bht], writes_acc=[Bdram])

    def alloc_w1(self, es):
        W1, _ = self.sb(es, "W1", [128, 8 * FF], BF16, side="right")
        BW1g = [self.k.buf(f"W1g{g}") for g in range(4)]
        return W1, BW1g

    def load_w1(self, W1, BW1g, src_tensor, base_off):
        for fg in range(4):
            for dc in range(8):
                self.k.dma("pool",
                           W1[:, dc * FF + fg * 1024: dc * FF + fg * 1024 + 1024],
                           bass.AP(src_tensor, base_off + dc * 128 * FF + fg * 1024,
                                   [[FF, 128], [1, 1024]]),
                           writes_acc=[BW1g[fg]])

    def load_w2(self, W2, BW2g, src_tensor, base_off):
        for fg in range(4):
            for f8 in range(8):
                fc = fg * 8 + f8
                self.k.dma("pool", W2[:, fc * D:(fc + 1) * D],
                           bass.AP(src_tensor, base_off + fc * 128 * D,
                                   [[D, 128], [1, D]]),
                           writes_acc=[BW2g[fg]])

    def ff_moe_out(self, gW1, W1, BW1g):
        k, T, TT, NTC = self.k, self.T, self.TT, self.NTC
        gW = ExitStack()
        W2, _ = self.sb(gW, "W2", [128, 32 * D], BF16, side="right")
        BW2g = [k.buf(f"W2g{g}") for g in range(4)]
        hid, Bhid = self.sb(gW, "hidT", [128, 32 * 256], BF16, side="right")

        gF = ExitStack()
        g2s, Bg2s = self.sb(gF, "g2s", [128, D], F32)
        b2s, Bb2s = self.sb(gF, "b2s", [128, D], F32)
        bf2, Bbf2 = self.sb(gF, "bff2s", [128, D], F32)
        k.dma("sp", g2s[:, :], self.i["g2"], writes=[Bg2s])
        k.dma("sp", b2s[:, :], self.i["b2"], writes=[Bb2s])
        k.dma("sp", bf2[:, :], self.i["bff2r"], writes=[Bbf2])
        self.load_w2(W2, BW2g, self.i["fWff2"].tensor, 0)
        hw = [self.sb(gF, f"hw{j}", [128, 8 * 256], BF16) for j in range(2)]
        h1r = [self.sb(gF, f"h1r{j}", [128, D], F32) for j in range(2)]
        h2b = [self.sb(gF, f"h2b{j}", [128, D], BF16) for j in range(2)]
        h2ts = [self.sb(gF, f"h2ts{j}", [128, 1024], BF16) for j in range(2)]
        for tc in range(NTC):
            w, Bw = hw[tc % 2]
            k.dma("sp", w[:, :],
                  bass.AP(self.s["h1T"].tensor, tc * 256,
                          [[T, 128], [128 * T, 8], [1, 256]]),
                  reads=[self.Bs["h1T"]], writes=[Bw])
            self.ff_block(W1, BW1g, W2, BW2g, hid, Bhid, w, Bw,
                          (self.sb_p["bff1c"], self.Bp["bff1c"], 0))
            for t2 in range(2):
                ti = tc * 2 + t2
                hr, Bhr = h1r[t2]
                self.rowload("sp", hr[:, :], self.s["h1"], ti * 128, D, Bhr,
                             Bsrc=self.Bs["h1"])
                for dh in range(2):
                    psj = 4 + t2 * 2 + dh
                    k.op("dve", lambda e, psj=psj, dh=dh, hr=hr: e.tensor_tensor(
                        hr[:, dh * 512: dh * 512 + 512], self.ps[psj][:, :],
                        hr[:, dh * 512: dh * 512 + 512], ALU.add),
                        reads=[self.Bps[psj], Bhr], writes=[Bhr])
                    k.op("dve", lambda e, dh=dh, hr=hr: e.tensor_tensor(
                        hr[:, dh * 512: dh * 512 + 512],
                        hr[:, dh * 512: dh * 512 + 512],
                        bf2[:, dh * 512: dh * 512 + 512], ALU.add),
                        reads=[Bhr, Bbf2], writes=[Bhr])
                self.ln_tile(hr[:, :], Bhr, hr[:, :], Bhr,
                             g=g2s[:, :], Bg=Bg2s, b=b2s[:, :], Bb=Bb2s, slot=ti)
                self.rowstore(self.s["h2"], ti * 128, D, hr[:, :], Bhr, self.Bs["h2"])
                hb, Bhb = h2b[t2]
                k.op("dve", lambda e, hb=hb, hr=hr: e.tensor_copy(hb[:, :], hr[:, :]),
                     reads=[Bhr], writes=[Bhb])
                ht, Bht = h2ts[t2]
                self.t_tile_to_dram(hb, Bhb, ht, Bht, self.s["h2T"], self.Bs["h2T"],
                                    ti, 2)
                psj = 3
                for dc in range(8):
                    k.op("pe", lambda e, dc=dc, ht=ht, psj=psj: e.matmul(
                        self.ps[psj][:, 0:8], ht[:, dc * 128:(dc + 1) * 128],
                        self.sb_p["wg"][:, dc * 8: dc * 8 + 8],
                        start=(dc == 0), stop=(dc == 7)),
                        reads=[Bht, self.Bp["wg"]], writes=[self.Bps[psj]],
                        inc=(dc == 7))
                self.gating(psj, ti)
        k.bury(Bg2s, Bb2s, Bbf2, *[b for _, b in hw], *[b for _, b in h1r],
               *[b for _, b in h2b], *[b for _, b in h2ts])
        gF.close()

        gM = ExitStack()
        msel, Bmsel = self.sb(gM, "msel", [128, TT * 1024], BF16)
        k.op("dve", lambda e: e.memset(msel[:, :], 0.0), writes=[Bmsel])
        mhw = [self.sb(gM, f"mhw{j}", [128, 8 * 256], BF16) for j in range(2)]
        for ex in range(NE):
            self.load_w1(W1, BW1g, self.i["fWe1"].tensor, ex * D * FF)
            self.load_w2(W2, BW2g, self.i["fWe2"].tensor, ex * FF * D)
            for tc in range(NTC):
                w, Bw = mhw[tc % 2]
                k.dma("sp", w[:, :],
                      bass.AP(self.s["h2T"].tensor, tc * 256,
                              [[T, 128], [128 * T, 8], [1, 256]]),
                      reads=[self.Bs["h2T"]], writes=[Bw])
                self.ff_block(W1, BW1g, W2, BW2g, hid, Bhid, w, Bw,
                              (self.sb_p["be1c"], self.Bp["be1c"], ex * 32))
                for t2 in range(2):
                    ti = tc * 2 + t2
                    for dh in range(2):
                        psj = 4 + t2 * 2 + dh
                        k.op("dve", lambda e, psj=psj, ti=ti, dh=dh, ex=ex: e.scalar_tensor_tensor(
                            msel[:, ti * 1024 + dh * 512: ti * 1024 + dh * 512 + 512],
                            self.ps[psj][:, :],
                            self.sb_p["maskg"][:, ti * 8 + ex: ti * 8 + ex + 1],
                            msel[:, ti * 1024 + dh * 512: ti * 1024 + dh * 512 + 512],
                            ALU.mult, ALU.add),
                            reads=[self.Bps[psj], self.Bp["maskg"], Bmsel],
                            writes=[Bmsel])
        k.bury(*BW1g, *BW2g, Bhid, *[b for _, b in mhw])
        gW.close()
        gW1.close()

        gO = ExitStack()
        WoS, BWoS = self.sb(gO, "fWoutS", [128, 8 * D], BF16, side="right")
        be2b, Bbe2b = self.sb(gO, "be2b", [NE, D], BF16, side="right")
        g3s, Bg3s = self.sb(gO, "g3s", [128, D], F32, side="right")
        b3s, Bb3s = self.sb(gO, "b3s", [128, D], F32, side="right")
        for dc in range(8):
            k.dma("pool", WoS[:, dc * D:(dc + 1) * D],
                  bass.AP(self.i["fWout"].tensor, dc * 128 * D, [[D, 128], [1, D]]),
                  writes=[BWoS])
        k.dma("pool", be2b[:, :], self.i["be2"], writes=[Bbe2b])
        k.dma("sp", g3s[:, :], self.i["g3"], writes=[Bg3s])
        k.dma("sp", b3s[:, :], self.i["b3"], writes=[Bb3s])
        h2r = [self.sb(gO, f"h2r{j}", [128, D], F32, side="right") for j in range(3)]
        h3b = [self.sb(gO, f"h3b{j}", [128, D], BF16, side="right") for j in range(3)]
        h3ts = [self.sb(gO, f"h3ts{j}", [128, D], BF16, side="right") for j in range(3)]
        x2r = [self.sb(gO, f"x2r{j}", [128, D], F32, side="right") for j in range(3)]
        outt = [self.sb(gO, f"ou{j}", [128, D], F32, side="right") for j in range(3)]
        h3f = [self.sb(gO, f"h3f{j}", [128, D], F32, side="right") for j in range(2)]

        def go_s2(tt):
            hr, Bhr = h2r[tt % 3]
            lf, Blf = h3f[tt % 2]
            self.ln_tile(hr[:, :], Bhr, lf[:, :], Blf,
                         g=g3s[:, :], Bg=Bg3s, b=b3s[:, :], Bb=Bb3s, slot=tt)
            hb, Bhb = h3b[tt % 3]
            k.op("dve", lambda e, hb=hb, lf=lf: e.tensor_copy(hb[:, :], lf[:, :]),
                 reads=[Blf], writes=[Bhb])

        def go_s3(tt):
            hb, Bhb = h3b[tt % 3]
            ht, Bht = h3ts[tt % 3]
            psjT = 2 + tt % 2
            psb = self.ps[psjT].ap().bitcast(BF16)
            for c in range(8):
                k.op("pe", lambda e, c=c, hb=hb, psb=psb: e.matmul(
                    psb[:, c * 128:(c + 1) * 128], hb[:, c * 128:(c + 1) * 128],
                    self.sb_p["ident"][:, :], start=True, stop=True,
                    is_transpose=True, skip_group_check=True),
                    reads=[Bhb, self.Bp["ident"]], writes=[self.Bps[psjT]],
                    inc=(c == 7))
            k.op("dve", lambda e, ht=ht, psb=psb: e.tensor_copy(ht[:, :],
                                                               psb[:, 0:1024]),
                 reads=[self.Bps[psjT]], writes=[Bht])
            xr, Bxr = x2r[tt % 3]
            self.rowload("sp", xr[:, :], self.s["x2"], tt * 128, D, Bxr,
                         Bsrc=self.Bs["x2"])
            ot, Bot = outt[tt % 3]
            for half in range(2):
                psj = 4 + (tt % 2) * 2 + half
                for dc in range(8):
                    k.op("pe", lambda e, dc=dc, tt=tt, half=half, psj=psj, ht=ht: e.matmul(
                        self.ps[psj][:, :],
                        ht[:, dc * 128:(dc + 1) * 128],
                        WoS[:, dc * D + half * 512: dc * D + half * 512 + 512],
                        start=(dc == 0), stop=(dc == 7)),
                        reads=[Bht, BWoS], writes=[self.Bps[psj]], inc=(dc == 7))
                k.op("dve", lambda e, half=half, psj=psj, ot=ot, xr=xr: e.tensor_tensor(
                    ot[:, half * 512: half * 512 + 512], self.ps[psj][:, :],
                    xr[:, half * 512: half * 512 + 512], ALU.add),
                    reads=[self.Bps[psj], Bxr], writes=[Bot])
            k.dma("act", bass.AP(self.out.tensor, tt * 128 * D, [[D, 128], [1, D]]),
                  ot[:, :], reads=[Bot])

        def go_s1(tt):
            hr, Bhr = h2r[tt % 3]
            self.rowload("sp", hr[:, :], self.s["h2"], tt * 128, D, Bhr,
                         Bsrc=self.Bs["h2"])
            for dh in range(2):
                psj = dh
                k.op("pe", lambda e, tt=tt, dh=dh, psj=psj: e.matmul(
                    self.ps[psj][:, :],
                    self.sb_p["maskgT"][0:NE, tt * 128: tt * 128 + 128],
                    be2b[0:NE, dh * 512: dh * 512 + 512],
                    start=True, stop=True),
                    reads=[self.Bp["maskgT"], Bbe2b], writes=[self.Bps[psj]])
                k.op("dve", lambda e, psj=psj, dh=dh, tt=tt, hr=hr: e.tensor_tensor(
                    hr[:, dh * 512: dh * 512 + 512], self.ps[psj][:, :],
                    hr[:, dh * 512: dh * 512 + 512], ALU.add),
                    reads=[self.Bps[psj], Bhr], writes=[Bhr])
                k.op("dve", lambda e, dh=dh, tt=tt, hr=hr: e.tensor_tensor(
                    hr[:, dh * 512: dh * 512 + 512],
                    hr[:, dh * 512: dh * 512 + 512],
                    msel[:, tt * 1024 + dh * 512: tt * 1024 + dh * 512 + 512],
                    ALU.add),
                    reads=[Bhr, Bmsel], writes=[Bhr])

        for it in range(TT + 2):
            if it < TT:
                go_s1(it)
            if 1 <= it <= TT:
                go_s2(it - 1)
            if it >= 2:
                go_s3(it - 2)
        gM.close()
        gO.close()

    def ff_block(self, W1, BW1g, W2, BW2g, hid, Bhid, hT, BhT, bias1):
        k = self.k
        b1sb, Bb1, boff = bias1
        for fc in range(32):
            psj = fc % 2
            for dc in range(8):
                k.op("pe", lambda e, fc=fc, dc=dc, psj=psj: e.matmul(
                    self.ps[psj][:, 0:256],
                    W1[:, dc * FF + fc * 128: dc * FF + fc * 128 + 128],
                    hT[:, dc * 256: dc * 256 + 256],
                    start=(dc == 0), stop=(dc == 7)),
                    reads=[BW1g[fc // 8], BhT], writes=[self.Bps[psj]], inc=(dc == 7))
            k.op("act", lambda e, fc=fc, psj=psj: e.activation(
                hid[:, fc * 256: fc * 256 + 256], self.ps[psj][:, 0:256], AF.Relu,
                bias=b1sb[:, boff + fc: boff + fc + 1]),
                reads=[self.Bps[psj], Bb1], writes=[Bhid])
        for t2 in range(2):
            for dh in range(2):
                psj = 4 + t2 * 2 + dh
                for fc in range(32):
                    k.op("pe", lambda e, fc=fc, t2=t2, dh=dh, psj=psj: e.matmul(
                        self.ps[psj][:, :],
                        hid[:, fc * 256 + t2 * 128: fc * 256 + t2 * 128 + 128],
                        W2[:, fc * D + dh * 512: fc * D + dh * 512 + 512],
                        start=(fc == 0), stop=(fc == 31)),
                        reads=[Bhid, BW2g[fc // 8]], writes=[self.Bps[psj]],
                        inc=(fc == 31))

    def gating(self, psj, ti):
        k, p, B = self.k, self.sb_p, self.Bp
        k.op("dve", lambda e: e.reduce_max(p["gm"][:, :], self.ps[psj][:, 0:8], axis=AX.X),
             reads=[self.Bps[psj]], writes=[B["gm"]])
        k.op("dve", lambda e: e.tensor_scalar(p["ge"][:, :], self.ps[psj][:, 0:8],
                                              p["gm"][:, :], None, ALU.subtract,
                                              ALU.bypass),
             reads=[self.Bps[psj], B["gm"]], writes=[B["ge"]])
        k.op("act", lambda e: e.activation(p["ge"][:, :], p["ge"][:, :], AF.Exp,
                                           accum_out=p["gz"][:, :]),
             reads=[B["ge"]], writes=[B["ge"], B["gz"]])
        k.op("dve", lambda e: e.tensor_scalar(p["gv"][:, :], p["ge"][:, :], 1.0, None,
                                              ALU.is_equal, ALU.bypass),
             reads=[B["ge"]], writes=[B["gv"]])
        k.op("dve", lambda e: e.tensor_scalar(p["ge"][:, :], p["iota8"][:, :], -1.0,
                                              8.0, ALU.mult, ALU.add),
             reads=[B["iota8"]], writes=[B["ge"]])
        k.op("dve", lambda e: e.tensor_tensor(p["ge"][:, :], p["ge"][:, :],
                                              p["gv"][:, :], ALU.mult),
             reads=[B["ge"], B["gv"]], writes=[B["ge"]])
        k.op("dve", lambda e: e.reduce_max(p["gm"][:, :], p["ge"][:, :], axis=AX.X),
             reads=[B["ge"]], writes=[B["gm"]])
        k.op("dve", lambda e: e.tensor_scalar(p["gv"][:, :], p["ge"][:, :],
                                              p["gm"][:, :], None, ALU.is_equal,
                                              ALU.bypass),
             reads=[B["ge"], B["gm"]], writes=[B["gv"]])
        k.op("dve", lambda e: e.reciprocal(p["gz"][:, :], p["gz"][:, :]),
             reads=[B["gz"]], writes=[B["gz"]])
        k.op("dve", lambda e, ti=ti: e.tensor_scalar(
            p["maskg"][:, ti * 8: ti * 8 + 8], p["gv"][:, :], p["gz"][:, :], None,
            ALU.mult, ALU.bypass),
            reads=[B["gv"], B["gz"]], writes=[B["maskg"]])
        psT = 2
        k.op("pe", lambda e, ti=ti: e.matmul(
            self.ps[psT][0:8, 0:128], p["maskg"][:, ti * 8: ti * 8 + 8],
            p["identf"][:, :], start=True, stop=True, is_transpose=True),
            reads=[B["maskg"], B["identf"]], writes=[self.Bps[psT]])
        k.op("dve", lambda e, ti=ti: e.tensor_copy(
            p["maskgT"][0:NE, ti * 128: ti * 128 + 128], self.ps[psT][0:8, 0:128]),
            reads=[self.Bps[psT]], writes=[B["maskgT"]])


_CACHE = {}


def build_program(T):
    if T not in _CACHE:
        _CACHE[T] = Builder(T).build()
    return _CACHE[T]


def make_inputs(full, T):
    peT, ident, identf, tri, iota8 = host_consts(T)
    rep = lambda v: np.ascontiguousarray(np.broadcast_to(v, (128, D)))
    u = full["L0_ffn_u_bias"]
    v = full["L0_ffn_v_bias"]
    uv = np.ascontiguousarray(
        np.concatenate([np.asarray(u).reshape(8, 128).T,
                        np.asarray(v).reshape(8, 128).T], axis=1))
    be1c = np.ascontiguousarray(
        np.asarray(full["L0_ffn_be1"]).reshape(8, 32, 128).transpose(2, 0, 1)
        .reshape(128, 256))
    bff1c = np.ascontiguousarray(np.asarray(full["L0_ffn_bff1"]).reshape(32, 128).T)
    shared = dict(
        aWqkv=full["L0_attn_Wqkv"], aWout=full["L0_attn_Wout"],
        fWqkv=full["L0_ffn_Wqkv"], fWo=full["L0_ffn_Wo"], fWr=full["L0_ffn_Wr"],
        fWff1=full["L0_ffn_Wff1"], fWff2=full["L0_ffn_Wff2"], fWg=full["L0_ffn_Wg"],
        fWe1=full["L0_ffn_We1"], fWe2=full["L0_ffn_We2"], fWout=full["L0_ffn_Wout"],
        uv=uv, be1c=be1c, bff1c=bff1c,
        bff2r=rep(np.asarray(full["L0_ffn_bff2"])), be2=full["L0_ffn_be2"],
        g1=rep(np.asarray(full["L0_ffn_ln1_g"])), b1=rep(np.asarray(full["L0_ffn_ln1_b"])),
        g2=rep(np.asarray(full["L0_ffn_ln2_g"])), b2=rep(np.asarray(full["L0_ffn_ln2_b"])),
        g3=rep(np.asarray(full["L0_ffn_ln3_g"])), b3=rep(np.asarray(full["L0_ffn_ln3_b"])),
        peT=peT, ident=ident, identf=identf, tri=tri, iota8=iota8,
    )
    shared = {k2: np.ascontiguousarray(v2) for k2, v2 in shared.items()}
    x = np.asarray(full["L0_attn_x"])
    in_maps = []
    for c in range(x.shape[0]):
        d = dict(shared)
        d["x"] = np.ascontiguousarray(x[c])
        in_maps.append(d)
    return in_maps


def kernel(**inputs):
    x = inputs["L0_attn_x"]
    B, T, _ = x.shape
    nc = build_program(T)
    in_maps = make_inputs(inputs, T)
    res = run_bass_kernel_spmd(nc, in_maps, core_ids=list(range(B)))
    out = np.stack([np.asarray(r["out"], dtype=np.float32) for r in res.results])
    return out


# version 12
# speedup vs baseline: 1.0095x; 1.0069x over previous
import numpy as np
import ml_dtypes
from contextlib import ExitStack
import concourse.bass as bass
import concourse.mybir as mybir
from concourse.bass_utils import run_bass_kernel_spmd

F32 = mybir.dt.float32
BF16 = mybir.dt.bfloat16
U8 = mybir.dt.uint8
AF = mybir.ActivationFunctionType
ALU = mybir.AluOpType
AX = mybir.AxisListType

D = 1024
NH = 16
FF = 4096
NE = 8
EPS = 1e-5
SCALE = 0.125
EMIN = float(np.exp(np.float32(-10.0)))
EMAX = float(np.exp(np.float32(10.0)))

DEBUG_SCRATCH_OUT = False
DEBUG_TAPS = False


class Tok:
    __slots__ = ("sem", "val")

    def __init__(self, sem, val):
        self.sem = sem
        self.val = val


class Buf:
    __slots__ = ("name", "w", "rds", "mw")

    def __init__(self, name="buf", seed=None):
        self.name = name
        self.w = None
        self.rds = dict(seed) if seed else {}
        self.mw = {}


ENGS = ("pe", "dve", "act", "pool", "sp")


class K:
    def __init__(self, nc, es, n_dma_sems=8):
        self.nc = nc
        self.eng = {"pe": nc.tensor, "dve": nc.vector, "act": nc.scalar,
                    "pool": nc.gpsimd, "sp": nc.sync}
        self.sem = {}
        self.cnt = {}
        for e in ENGS:
            self.sem[e] = es.enter_context(nc.semaphore(f"sem_{e}"))
            self.cnt[e] = 0
        self.dsem = {}
        self.dcnt = {}
        self.didx = {}
        for qe in ("sp", "pool", "act"):
            self.dsem[qe] = [es.enter_context(nc.semaphore(f"dma_{qe}{i}"))
                             for i in range(n_dma_sems)]
            self.dcnt[qe] = [0] * n_dma_sems
            self.didx[qe] = 0
        self.hw = {e: {} for e in ENGS}
        self.pending = {e: ([], []) for e in ENGS}
        self.graveyard = {}

    def _wait(self, e, tok):
        if tok is None:
            return
        key = id(tok.sem)
        if self.hw[e].get(key, -1) >= tok.val:
            return
        self.eng[e].wait_ge(tok.sem, tok.val)
        self.hw[e][key] = tok.val

    def _dep_toks(self, reads, writes):
        toks = []
        for b in reads:
            if b.w is not None:
                toks.append(b.w)
            toks.extend(b.mw.values())
        for b in writes:
            if b.w is not None:
                toks.append(b.w)
            toks.extend(b.mw.values())
            toks.extend(b.rds.values())
        return toks

    def waits_for(self, e, reads=(), writes=()):
        for t in self._dep_toks(reads, writes):
            self._wait(e, t)

    def op(self, e, fn, reads=(), writes=(), inc=True, nowait=False):
        if not nowait:
            for t in self._dep_toks(reads, writes):
                self._wait(e, t)
        instr = fn(self.eng[e])
        pr, pw = self.pending[e]
        if inc:
            self.cnt[e] += 1
            tok = Tok(self.sem[e], self.cnt[e])
            instr.then_inc(self.sem[e], 1)
            for b in list(pr) + list(reads):
                b.rds[e] = tok
            for b in list(pw) + list(writes):
                b.w = tok
                b.rds = {}
                b.mw = {}
            pr.clear()
            pw.clear()
            return tok
        pr.extend(reads)
        pw.extend(writes)
        return None

    def dma(self, qe, out_ap, in_ap, reads=(), writes=(), writes_acc=(), **kw):
        for t in self._dep_toks(reads, writes):
            self._wait(qe, t)
        for b in writes_acc:
            for t in b.rds.values():
                self._wait(qe, t)
            if b.w is not None:
                self._wait(qe, b.w)
        i = self.didx[qe]
        self.didx[qe] = (i + 1) % len(self.dsem[qe])
        sem = self.dsem[qe][i]
        if self.dcnt[qe][i] > 0:
            self._wait(qe, Tok(sem, self.dcnt[qe][i]))
        self.eng[qe].dma_start(out=out_ap, in_=in_ap, **kw).then_inc(sem, 16)
        self.dcnt[qe][i] += 16
        tok = Tok(sem, self.dcnt[qe][i])
        for b in reads:
            b.rds[f"q{qe}{i}"] = tok
        for b in writes:
            b.w = tok
            b.rds = {}
            b.mw = {}
        for b in writes_acc:
            b.mw[id(sem)] = tok
        return tok

    def buf(self, name="buf"):
        return Buf(name, seed=self.graveyard)

    def bury(self, *bufs):
        for b in bufs:
            toks = list(b.rds.values()) + list(b.mw.values())
            if b.w is not None:
                toks.append(b.w)
            for t in toks:
                key = id(t.sem)
                old = self.graveyard.get(key)
                if old is None or old.val < t.val:
                    self.graveyard[key] = t


def host_consts(T):
    pos = np.arange(T - 1, -1, -1, dtype=np.float32)
    invf = (1.0 / 10000.0 ** (np.arange(0, D, 2, dtype=np.float32) / D)).astype(np.float32)
    sinu = pos[:, None] * invf[None, :]
    pe = np.concatenate([np.sin(sinu), np.cos(sinu)], axis=1).astype(np.float32)
    peT = np.ascontiguousarray(pe.T)
    ident = np.eye(128, dtype=ml_dtypes.bfloat16)
    identf = np.eye(128, dtype=np.float32)
    tri = np.tril(np.ones((128, 128), dtype=np.uint8), k=-1)
    iota8 = np.broadcast_to(np.arange(8, dtype=np.float32), (128, 8)).copy()
    return peT, ident, identf, tri, iota8


class Builder:
    def __init__(self, T):
        self.T = T
        self.TT = T // 128
        self.TC = T // 512
        self.NTC = T // 256
        nc = bass.Bass("TRN2", target_bir_lowering=False)
        self.nc = nc
        self.es = ExitStack()
        self.es.__enter__()
        self.k = K(nc, self.es)

        def din(name, shape, dt=F32):
            return nc.dram_tensor(name, shape, dt, kind="ExternalInput").ap()

        sk = "ExternalOutput" if DEBUG_SCRATCH_OUT else "Internal"

        def dscr(name, shape, dt=F32):
            return nc.dram_tensor(name, shape, dt, kind=sk).ap()

        i = self.i = {}
        i["x"] = din("x", [T, D])
        i["aWqkv"] = din("aWqkv", [3 * D, D])
        i["aWout"] = din("aWout", [D, D])
        i["fWqkv"] = din("fWqkv", [D, 3 * D])
        i["fWo"] = din("fWo", [D, D])
        i["fWr"] = din("fWr", [D, D])
        i["fWff1"] = din("fWff1", [D, FF])
        i["fWff2"] = din("fWff2", [FF, D])
        i["fWg"] = din("fWg", [D, NE])
        i["fWe1"] = din("fWe1", [NE, D, FF])
        i["fWe2"] = din("fWe2", [NE, FF, D])
        i["fWout"] = din("fWout", [D, D])
        i["uv"] = din("uv", [128, 16])
        i["be1c"] = din("be1c", [128, NE * 32])
        i["bff1c"] = din("bff1c", [128, 32])
        i["bff2r"] = din("bff2r", [128, D])
        i["be2"] = din("be2", [NE, D])
        for nm in ("g1", "b1", "g2", "b2", "g3", "b3"):
            i[nm] = din(nm, [128, D])
        i["peT"] = din("peT", [D, T])
        i["ident"] = din("ident", [128, 128], BF16)
        i["identf"] = din("identf", [128, 128])
        i["tri"] = din("tri", [128, 128], U8)
        i["iota8"] = din("iota8", [128, 8])

        s = self.s = {}
        self.Bs = {}
        s["bd"] = [dscr(f"bd{h}", [T * T], BF16) for h in range(NH)]
        self.Bs["bd"] = [self.k.buf(f"bd{h}") for h in range(NH)]
        self.Bs["bdg"] = [self.k.buf(f"bdg{h}") for h in range(NH)]
        for nm, shape, dt in [("x2", [T, D], F32), ("lnx2", [T, D], F32),
                              ("h1", [T, D], F32), ("h2", [T, D], F32),
                              ("h1T", [D, T], BF16), ("h2T", [D, T], BF16),
                              ("octx", [T, D], BF16)]:
            s[nm] = dscr(nm, shape, dt)
            self.Bs[nm] = self.k.buf(nm)
        self.out = nc.dram_tensor("out", [T, D], F32, kind="ExternalOutput").ap()
        self._tapP = (nc.dram_tensor("tapP", [T, T], BF16, kind="ExternalOutput").ap()
                      if DEBUG_TAPS else None)

        self.ps = [self.es.enter_context(nc.psum_tensor(f"ps{j}", [128, 512], F32))
                   for j in range(8)]
        self.Bps = [self.k.buf(f"ps{j}") for j in range(8)]

        self.sb_p = {}
        self.Bp = {}
        for nm, shape, dt in [
            ("ident", [128, 128], BF16), ("identf", [128, 128], F32),
            ("tri", [128, 128], U8), ("iota8", [128, 8], F32),
            ("uv", [128, 16], F32), ("wg", [128, NE * 8], BF16),
            ("be1c", [128, NE * 32], F32), ("bff1c", [128, 32], F32),
            ("zero", [128, 128], BF16), ("ones1", [1, 128], BF16),
            ("epsc", [128, 1], F32),
            ("bn12a", [128, 12], F32), ("mva", [128, 2], F32),
            ("sda", [128, 1], F32), ("rstda", [128, 1], F32),
            ("bn12b", [128, 12], F32), ("mvb", [128, 2], F32),
            ("sdb", [128, 1], F32), ("rstdb", [128, 1], F32),
            ("zrec", [128, 1], F32),
            ("maskg", [128, (T // 128) * 8], F32),
            ("maskgT", [NE, T], BF16),
            ("gm", [128, 1], F32), ("gz", [128, 1], F32),
            ("ge", [128, 8], F32), ("gv", [128, 8], F32),
        ]:
            self.sb_p[nm] = self.es.enter_context(nc.sbuf_tensor("c_" + nm, shape, dt))
            self.Bp[nm] = self.k.buf(nm)

    def tap(self, name, src_ap, Bsrc, shape):
        if not DEBUG_TAPS:
            return
        t = self.nc.dram_tensor("tap_" + name, list(shape), src_ap.dtype,
                                kind="ExternalOutput").ap()
        n = int(np.prod(shape))
        self.k.dma("sp", bass.AP(t.tensor, 0, [[1, n]]) if False else t, src_ap,
                   reads=[Bsrc])

    def sb(self, es, name, shape, dt, side="left"):
        t = es.enter_context(self.nc.sbuf_tensor(name, shape, dt, side=side))
        return t, self.k.buf(name)

    def ap3(self, t, off, d1, d2):
        free = int(np.prod(t.shape[1:]))
        return bass.AP(t, off, [[free, t.shape[0]], list(d1), list(d2)])

    def rowload(self, qe, dst_sb, dram_ap, r0, ncols, Bdst, nrows=128, Bsrc=None):
        reads = [Bsrc] if Bsrc is not None else []
        self.k.dma(qe, dst_sb, bass.AP(dram_ap.tensor, r0 * ncols,
                                       [[ncols, nrows], [1, ncols]]),
                   reads=reads, writes=[Bdst])

    def rowstore(self, dst_dram_ap, r0, ncols, src_sb, Bsrc, Bdst, qe="act"):
        self.k.dma(qe, bass.AP(dst_dram_ap.tensor, r0 * ncols,
                               [[ncols, 128], [1, ncols]]),
                   src_sb, reads=[Bsrc], writes_acc=[Bdst])

    def rowstore_bf(self, dst_dram_ap, r0, ncols, src_sb, Bsrc, Bdst):
        self.rowstore(dst_dram_ap, r0, ncols, src_sb, Bsrc, Bdst, qe="pool")

    def ctx_proj_tile(self, tt, W, BW, stgb, Bstgb, stgT, BstgT, psjT, pbase,
                      dst, Bdst, res, Bres):
        k, T = self.k, self.T
        self.rowload("sp", stgb[:, :], self.s["octx"], tt * 128, D, Bstgb,
                     Bsrc=self.Bs["octx"])
        psb = self.ps[psjT].ap().bitcast(BF16)
        for c in range(8):
            k.op("pe", lambda e, c=c: e.matmul(
                psb[:, c * 128:(c + 1) * 128], stgb[:, c * 128:(c + 1) * 128],
                self.sb_p["ident"][:, :], start=True, stop=True,
                is_transpose=True, skip_group_check=True),
                reads=[Bstgb, self.Bp["ident"]], writes=[self.Bps[psjT]],
                inc=(c == 7))
        k.op("dve", lambda e: e.tensor_copy(stgT[:, :], psb[:, 0:1024]),
             reads=[self.Bps[psjT]], writes=[BstgT])
        for half in range(2):
            pj2 = pbase + half
            for dc in range(8):
                k.op("pe", lambda e, dc=dc, half=half, pj2=pj2: e.matmul(
                    self.ps[pj2][:, :], stgT[:, dc * 128:(dc + 1) * 128],
                    W[:, dc * D + half * 512: dc * D + half * 512 + 512],
                    start=(dc == 0), stop=(dc == 7)),
                    reads=[BstgT, BW], writes=[self.Bps[pj2]], inc=(dc == 7))
            k.op("dve", lambda e, half=half, pj2=pj2, dst=dst, res=res: e.tensor_tensor(
                dst[:, half * 512: half * 512 + 512], self.ps[pj2][:, :],
                res[:, half * 512: half * 512 + 512], ALU.add),
                reads=[self.Bps[pj2], Bres], writes=[Bdst])

    def ln_tile(self, src, Bsrc, out_f32, Bout_f32, out_bf=None, Bout_bf=None,
                g=None, Bg=None, b=None, Bb=None, slot=0):
        k, p, B = self.k, self.sb_p, self.Bp
        sx = "a" if slot % 2 == 0 else "b"
        bn12, mv, sd, rstd = (p["bn12" + sx], p["mv" + sx], p["sd" + sx],
                              p["rstd" + sx])
        Bbn, Bmv, Bsd, Brs = (B["bn12" + sx], B["mv" + sx], B["sd" + sx],
                              B["rstd" + sx])
        k.op("dve", lambda e: e.bn_stats(bn12[:, 0:6], src[:, 0:512]),
             reads=[Bsrc], writes=[Bbn], inc=False)
        k.op("dve", lambda e: e.bn_stats(bn12[:, 6:12], src[:, 512:1024]),
             reads=[Bsrc], writes=[Bbn])
        k.op("dve", lambda e: e.bn_aggr(mv[:, :], bn12[:, :]),
             reads=[Bbn], writes=[Bmv])
        k.op("act", lambda e: e.activation(sd[:, :], mv[:, 1:2], AF.Sqrt,
                                           bias=p["epsc"][:, :]),
             reads=[Bmv, B["epsc"]], writes=[Bsd])
        k.op("dve", lambda e: e.reciprocal(rstd[:, :], sd[:, :]),
             reads=[Bsd], writes=[Brs])
        if g is None:
            k.op("dve", lambda e: e.tensor_scalar(
                out_f32, src, mv[:, 0:1], rstd[:, :], ALU.subtract, ALU.mult),
                reads=[Bsrc, Bmv, Brs], writes=[Bout_f32])
        else:
            k.op("dve", lambda e: e.tensor_scalar(
                out_f32, src, mv[:, 0:1], None, ALU.subtract, ALU.bypass),
                reads=[Bsrc, Bmv], writes=[Bout_f32])
            k.op("dve", lambda e: e.scalar_tensor_tensor(
                out_f32, out_f32, rstd[:, :], g, ALU.mult, ALU.mult),
                reads=[Bout_f32, Brs, Bg], writes=[Bout_f32])
            k.op("dve", lambda e: e.tensor_tensor(out_f32, out_f32, b, ALU.add),
                 reads=[Bout_f32, Bb], writes=[Bout_f32])
        if out_bf is not None:
            k.op("dve", lambda e: e.tensor_copy(out_bf, out_f32),
                 reads=[Bout_f32], writes=[Bout_bf])

    def transpose_tile(self, src, Bsrc, dst_t, Bdst, tt, psj, nch=8):
        k, T = self.k, self.T
        psb = self.ps[psj].ap().bitcast(BF16)
        idn = self.sb_p["ident"]
        for c in range(nch):
            k.op("pe", lambda e, c=c: e.matmul(
                psb[:, c * 128:(c + 1) * 128], src[:, c * 128:(c + 1) * 128],
                idn[:, :], start=True, stop=True, is_transpose=True,
                skip_group_check=True),
                reads=[Bsrc, self.Bp["ident"]], writes=[self.Bps[psj]],
                inc=(c == nch - 1))
        k.op("dve", lambda e: e.tensor_copy(
            self.ap3(dst_t, tt * 128, [T, nch], [1, 128]), psb[:, 0:nch * 128]),
            reads=[self.Bps[psj]], writes=[Bdst])

    def load_wT(self, qe, dram, nrow, wst, dstT, BdstT, ncols_t, psj=6, row0=0):
        k = self.k
        for ec in range(nrow):
            w, Bw = wst[ec % 2]
            self.rowload(qe, w[:, :], dram, (row0 + ec) * 128, D, Bw)
            pj = psj if psj != -1 else ec % 2
            psb = self.ps[pj].ap().bitcast(BF16)
            for dc in range(8):
                k.op("pe", lambda e, dc=dc, w=w, psb=psb: e.matmul(
                    psb[:, dc * 128:(dc + 1) * 128], w[:, dc * 128:(dc + 1) * 128],
                    self.sb_p["ident"][:, :], start=True, stop=True,
                    is_transpose=True, skip_group_check=True),
                    reads=[Bw, self.Bp["ident"]], writes=[self.Bps[pj]],
                    inc=(dc == 7))
            k.op("dve", lambda e, ec=ec, psb=psb: e.tensor_copy(
                self.ap3(dstT, ec * 128, [ncols_t, 8], [1, 128]), psb[:, 0:1024]),
                reads=[self.Bps[pj]], writes=[BdstT])

    def load_consts(self):
        k, i, p, B = self.k, self.i, self.sb_p, self.Bp
        T = self.T
        for nm in ("ident", "identf", "tri", "iota8", "uv", "be1c", "bff1c"):
            k.dma("sp", p[nm][:, :], i[nm], writes=[B[nm]])
        k.dma("pool", p["wg"][:, :],
              bass.AP(i["fWg"].tensor, 0, [[8, 128], [128 * 8, 8], [1, 8]]),
              writes=[B["wg"]])
        k.op("dve", lambda e: e.memset(p["zero"][:, :], 0.0), writes=[B["zero"]])
        k.op("dve", lambda e: e.memset(p["epsc"][:, :], EPS), writes=[B["epsc"]])
        k.op("dve", lambda e: e.memset(p["ones1"][:, :], 1.0), writes=[B["ones1"]])
        zg = ExitStack()
        z2, Bz2 = self.sb(zg, "zfill", [128, 1024], BF16)
        k.op("dve", lambda e: e.memset(z2[:, :], 0.0), writes=[Bz2])
        rows_per = 1024
        for h in range(NH):
            for r0 in range(0, T, rows_per):
                nr = min(rows_per, T - r0)
                k.dma("act",
                      bass.AP(self.s["bd"][h].tensor, r0 * T, [[T, nr], [1, 128]]),
                      bass.AP(z2, 0, [[1024, 128], [1, nr]]),
                      reads=[Bz2], writes_acc=[self.Bs["bdg"][h]])
        self.k.bury(Bz2)
        zg.close()

    def build(self):
        k, T, TT, TC = self.k, self.T, self.TT, self.TC
        self.load_consts()

        g1 = ExitStack()
        lnxT, _ = self.sb(g1, "lnxT", [128, 8 * T], BF16)
        BlnxTl = [self.k.buf(f"lnxT{t}") for t in range(TT)]
        wst = [self.sb(g1, f"wst{j}", [128, D], BF16) for j in range(2)]
        xin = [self.sb(g1, f"xin{j}", [128, D], F32) for j in range(3)]
        lnb = [self.sb(g1, f"lnb{j}", [128, D], BF16) for j in range(3)]
        g2 = ExitStack()
        QT, _ = self.sb(g2, "QT", [128, 8 * T], BF16, side="right")
        KT, _ = self.sb(g2, "KT", [128, 8 * T], BF16, side="right")
        BQTl = [self.k.buf(f"QT{h}") for h in range(8)]
        BKTl = [self.k.buf(f"KT{h}") for h in range(8)]
        g1b = ExitStack()
        Wqk, BWqk = self.sb(g1b, "aWqkT", [128, 8 * 2048], BF16)

        self.load_wT("pool", self.i["aWqkv"], 16, wst, Wqk, BWqk, 2048, psj=-1)
        for tt in range(TT):
            xt, Bxt = xin[tt % 3]
            self.rowload("sp", xt[:, :], self.i["x"], tt * 128, D, Bxt)
            lb, Blb = lnb[tt % 3]
            self.ln_tile(xt[:, :], Bxt, xt[:, :], Bxt, out_bf=lb[:, :], Bout_bf=Blb,
                         slot=tt)
            self.transpose_tile(lb[:, :], Blb, lnxT, BlnxTl[tt], tt, 2 + tt % 2)
        self.qkv(Wqk, BWqk, 2048, lnxT, BlnxTl, QT, BQTl, KT, BKTl)
        k.bury(BWqk)
        g1b.close()
        g2b = ExitStack()
        VA, _ = self.sb(g2b, "VA", [128, TT * 1040], BF16, side="right")
        BVAl = [self.k.buf(f"VA{t}") for t in range(TT)]
        g1c = ExitStack()
        Wv, BWv = self.sb(g1c, "aWvT", [128, 8 * 1024], BF16)
        self.load_wT("pool", self.i["aWqkv"], 8, wst, Wv, BWv, 1024, psj=-1, row0=16)
        self.qkv_v(Wv, BWv, 1024, 0, lnxT, BlnxTl, VA, BVAl)
        k.bury(BWv, *BlnxTl, *[b for _, b in wst], *[b for _, b in xin],
               *[b for _, b in lnb])
        g1c.close()
        g1.close()

        g3 = ExitStack()
        Psets = []
        for si in range(2):
            Pa, _ = self.sb(g3, f"P{si}a", [128, TT * 512], BF16)
            Pb, _ = self.sb(g3, f"P{si}b", [128, TT * 512], BF16)
            Psets.append((Pa, [self.k.buf(f"P{si}a{t}") for t in range(TT)],
                          Pb, [self.k.buf(f"P{si}b{t}") for t in range(TT)]))
        ONc = [self.sb(g3, "ONc0", [128, 4 * 1024], BF16)]
        self.attention(QT, BQTl, KT, BKTl, VA, BVAl, Psets, ONc, bd=False,
                       QuStg=None)
        for (Pa, BPal, Pb, BPbl) in Psets:
            k.bury(*BPal, *BPbl)
        k.bury(*BQTl, *BKTl, *BVAl, *[b for _, b in ONc])
        g3.close()
        g2b.close()
        g2.close()

        g4 = ExitStack()
        lnx2T, _ = self.sb(g4, "lnx2T", [128, 8 * T], BF16, side="right")
        Blnx2Tl = [self.k.buf(f"lnx2T{t}") for t in range(TT)]
        g4b = ExitStack()
        WoT, BWoT = self.sb(g4b, "aWoutT", [128, 8 * D], BF16)
        wst2 = [self.sb(g4b, f"wsg{j}", [128, D], BF16) for j in range(2)]
        self.load_wT("pool", self.i["aWout"], 8, wst2, WoT, BWoT, D, psj=6)
        xin2 = [self.sb(g4b, f"xi2{j}", [128, D], F32) for j in range(3)]
        x2t = [self.sb(g4b, f"x2t{j}", [128, D], F32) for j in range(3)]
        lnf2 = [self.sb(g4b, f"lnf2{j}", [128, D], F32) for j in range(2)]
        lnb2 = [self.sb(g4b, f"lb2{j}", [128, D], BF16) for j in range(2)]
        ocb = [self.sb(g4b, f"ocb{j}", [128, D], BF16) for j in range(3)]
        ocT = [self.sb(g4b, f"ocT{j}", [128, D], BF16) for j in range(3)]

        def a4_s2(tt):
            x2, Bx2 = x2t[tt % 3]
            self.rowstore(self.s["x2"], tt * 128, D, x2[:, :], Bx2, self.Bs["x2"])
            lf, Blf = lnf2[tt % 2]
            self.ln_tile(x2[:, :], Bx2, lf[:, :], Blf, slot=tt)
            self.rowstore(self.s["lnx2"], tt * 128, D, lf[:, :], Blf,
                          self.Bs["lnx2"])
            lb, Blb = lnb2[tt % 2]
            k.op("dve", lambda e, lb=lb, lf=lf: e.tensor_copy(lb[:, :], lf[:, :]),
                 reads=[Blf], writes=[Blb])

        def a4_s3(tt):
            lb, Blb = lnb2[tt % 2]
            self.transpose_tile(lb[:, :], Blb, lnx2T, Blnx2Tl[tt], tt, 4 + tt % 2)

        for it in range(TT + 2):
            if it < TT:
                tt = it
                xt, Bxt = xin2[tt % 3]
                self.rowload("sp", xt[:, :], self.i["x"], tt * 128, D, Bxt)
                x2, Bx2 = x2t[tt % 3]
                sb_, Bsb_ = ocb[tt % 3]
                sT, BsT = ocT[tt % 3]
                self.ctx_proj_tile(tt, WoT, BWoT, sb_, Bsb_, sT, BsT, 6 + tt % 2,
                                   (tt % 2) * 2, x2, Bx2, xt, Bxt)
            if 1 <= it <= TT:
                a4_s2(it - 1)
            if it >= 2:
                a4_s3(it - 2)
        k.bury(BWoT, *[b for _, b in wst2], *[b for _, b in xin2],
               *[b for _, b in x2t], *[b for _, b in lnb2],
               *[b for _, b in lnf2],
               *[b for _, b in ocb], *[b for _, b in ocT])
        g4b.close()

        g6 = ExitStack()
        QT2, _ = self.sb(g6, "QT2", [128, 8 * T], BF16)
        KT2, _ = self.sb(g6, "KT2", [128, 8 * T], BF16)
        BQT2l = [self.k.buf(f"QT2{h}") for h in range(8)]
        BKT2l = [self.k.buf(f"KT2{h}") for h in range(8)]
        g6b = ExitStack()
        VA2, _ = self.sb(g6b, "VA2", [128, TT * 1040], BF16)
        BVA2l = [self.k.buf(f"VA2{t}") for t in range(TT)]
        g5 = ExitStack()
        fq, Bfq = self.sb(g5, "fqkv", [128, 8 * 3 * D], BF16, side="right")
        for dc in range(8):
            self.k.dma("pool", fq[:, dc * 3 * D:(dc + 1) * 3 * D],
                       bass.AP(self.i["fWqkv"].tensor, dc * 128 * 3 * D,
                               [[3 * D, 128], [1, 3 * D]]),
                       writes=[Bfq])
        self.qkv(fq, Bfq, 3 * D, lnx2T, Blnx2Tl, QT2, BQT2l, KT2, BKT2l)
        self.qkv_v(fq, Bfq, 3 * D, 2048, lnx2T, Blnx2Tl, VA2, BVA2l)
        k.bury(Bfq, *Blnx2Tl)
        g5.close()
        g4.close()

        g8 = ExitStack()
        rT, _ = self.sb(g8, "rT", [128, 8 * T], BF16, side="right")
        BrTl = [self.k.buf(f"rT{h}") for h in range(8)]
        g7 = ExitStack()
        peT, BpeT = self.sb(g7, "s_peT", [128, 8 * T], BF16, side="right")
        Wr, BWr = self.sb(g7, "Wr", [128, 8 * D], BF16, side="right")
        for dc in range(8):
            self.k.dma("pool", peT[:, dc * T:(dc + 1) * T],
                       bass.AP(self.i["peT"].tensor, dc * 128 * T, [[T, 128], [1, T]]),
                       writes=[BpeT])
            self.k.dma("pool", Wr[:, dc * D:(dc + 1) * D],
                       bass.AP(self.i["fWr"].tensor, dc * 128 * D, [[D, 128], [1, D]]),
                       writes=[BWr])
        for hp in range(8):
            for tc in range(TC):
                psj = (hp * TC + tc) % 2
                for dc in range(8):
                    k.op("pe", lambda e, dc=dc, hp=hp, tc=tc, psj=psj: e.matmul(
                        self.ps[psj][:, :],
                        Wr[:, dc * D + hp * 128: dc * D + hp * 128 + 128],
                        peT[:, dc * T + tc * 512: dc * T + tc * 512 + 512],
                        start=(dc == 0), stop=(dc == 7)),
                        reads=[BWr, BpeT], writes=[self.Bps[psj]], inc=(dc == 7))
                if (hp * TC + tc) % 2 == 0:
                    k.op("act", lambda e, hp=hp, tc=tc, psj=psj: e.activation(
                        rT[:, hp * T + tc * 512: hp * T + tc * 512 + 512],
                        self.ps[psj][:, :], AF.Copy),
                        reads=[self.Bps[psj]], writes=[BrTl[hp]])
                else:
                    k.op("dve", lambda e, hp=hp, tc=tc, psj=psj: e.tensor_copy(
                        rT[:, hp * T + tc * 512: hp * T + tc * 512 + 512],
                        self.ps[psj][:, :]),
                        reads=[self.Bps[psj]], writes=[BrTl[hp]])
        k.bury(BpeT, BWr)
        g7.close()

        stg8 = ExitStack()
        qv = [self.sb(stg8, f"qv{j}", [128, 128], BF16, side="right") for j in range(2)]
        bst = [self.sb(stg8, f"bst{j}", [128, TC * 512], BF16, side="right")
               for j in range(2)]
        for hp in range(8):
            for tt in range(TT):
                q, Bq = qv[tt % 2]
                k.op("dve", lambda e, hp=hp, tt=tt, q=q: e.tensor_scalar(
                    q[:, :], QT2[:, hp * T + tt * 128: hp * T + tt * 128 + 128],
                    self.sb_p["uv"][:, 8 + hp: 9 + hp], None, ALU.add, ALU.bypass),
                    reads=[BQT2l[hp], self.Bp["uv"]], writes=[Bq])
                jc_min = max(0, (T - 128 * (tt + 1)) // 512)
                njc = TC - jc_min
                for jc in range(jc_min, TC):
                    for h in range(2):
                        psj = h * 2 + (jc % 2)
                        k.waits_for("pe", reads=[Bq, BrTl[hp]],
                                    writes=[self.Bps[psj]])
                    for h in range(2):
                        psj = h * 2 + (jc % 2)
                        r0 = h * 64
                        k.op("pe", lambda e, jc=jc, hp=hp, r0=r0, q=q, psj=psj: e.matmul(
                            self.ps[psj][:, :], q[r0:r0 + 64, :],
                            rT[r0:r0 + 64, hp * T + jc * 512: hp * T + jc * 512 + 512],
                            start=True, stop=True, tile_position=(r0, 0)),
                            reads=[Bq, BrTl[hp]], writes=[self.Bps[psj]],
                            nowait=True)
                    for h in range(2):
                        psj = h * 2 + (jc % 2)
                        b, Bb = bst[h]
                        if (jc + h) % 2 == 0:
                            k.op("act", lambda e, jc=jc, jc_min=jc_min, b=b, psj=psj: e.activation(
                                b[:, (jc - jc_min) * 512: (jc - jc_min) * 512 + 512],
                                self.ps[psj][:, :], AF.Copy),
                                reads=[self.Bps[psj]], writes=[Bb])
                        else:
                            k.op("dve", lambda e, jc=jc, jc_min=jc_min, b=b, psj=psj: e.tensor_copy(
                                b[:, (jc - jc_min) * 512: (jc - jc_min) * 512 + 512],
                                self.ps[psj][:, :]),
                                reads=[self.Bps[psj]], writes=[Bb])
                for h in range(2):
                    head = 2 * hp + h
                    b, Bb = bst[h]
                    njc = TC - jc_min
                    k.dma("pool",
                          bass.AP(self.s["bd"][head].tensor,
                                  tt * 128 * T + jc_min * 512,
                                  [[T, 128], [1, njc * 512]]),
                          b[:, 0: njc * 512],
                          reads=[Bb, self.Bs["bdg"][head]],
                          writes_acc=[self.Bs["bd"][head]])
        k.bury(*[b for _, b in qv], *[b for _, b in bst], *BrTl)
        stg8.close()
        g8.close()

        g9 = ExitStack()
        Psets2 = []
        for si in range(2):
            Pa, _ = self.sb(g9, f"fP{si}a", [128, TT * 512], BF16, side="right")
            Pb, _ = self.sb(g9, f"fP{si}b", [128, TT * 512], BF16, side="right")
            Psets2.append((Pa, [self.k.buf(f"fP{si}a{t}") for t in range(TT)],
                           Pb, [self.k.buf(f"fP{si}b{t}") for t in range(TT)]))
        ONc2 = [self.sb(g9, "fONc0", [128, 4 * 1024], BF16, side="right")]
        stg9 = ExitStack()
        self.bdsh = [[[self.sb(stg9, f"bdsh{h}{j}{p}", [128, 512], BF16, side="right")
                       for p in range(2)] for j in range(4)] for h in range(2)]
        QuStg = [self.sb(stg9, f"qu{j}", [128, 512], BF16, side="right")
                 for j in range(2)]
        self.attention(QT2, BQT2l, KT2, BKT2l, VA2, BVA2l, Psets2, ONc2,
                       bd=True, QuStg=QuStg)
        for (Pa, BPal, Pb, BPbl) in Psets2:
            k.bury(*BPal, *BPbl)
        k.bury(*BQT2l, *BKT2l, *BVA2l, *[b for _, b in ONc2],
               *[b for _, b in QuStg])
        for hh in self.bdsh:
            for jj in hh:
                for _, bb in jj:
                    k.bury(bb)
        stg9.close()
        g9.close()
        g6b.close()
        g6.close()

        gW1 = ExitStack()
        W1, BW1g = self.alloc_w1(gW1)
        self.load_w1(W1, BW1g, self.i["fWff1"].tensor, 0)

        g10 = ExitStack()
        Wo, BWo = self.sb(g10, "fWoS", [128, 8 * D], BF16)
        g1s, Bg1s = self.sb(g10, "s_g1", [128, D], F32)
        b1s, Bb1s = self.sb(g10, "s_b1", [128, D], F32)
        for dc in range(8):
            self.k.dma("pool", Wo[:, dc * D:(dc + 1) * D],
                       bass.AP(self.i["fWo"].tensor, dc * 128 * D, [[D, 128], [1, D]]),
                       writes=[BWo])
        k.dma("sp", g1s[:, :], self.i["g1"], writes=[Bg1s])
        k.dma("sp", b1s[:, :], self.i["b1"], writes=[Bb1s])
        lx = [self.sb(g10, f"lx{j}", [128, D], F32) for j in range(3)]
        s1t = [self.sb(g10, f"s1{j}", [128, D], F32) for j in range(3)]
        h1b = [self.sb(g10, f"h1b{j}", [128, D], BF16) for j in range(3)]
        h1ts = [self.sb(g10, f"h1ts{j}", [128, 1024], BF16) for j in range(3)]
        ocb2 = [self.sb(g10, f"oc2{j}", [128, D], BF16) for j in range(3)]
        ocT2 = [self.sb(g10, f"oT2{j}", [128, D], BF16) for j in range(3)]
        h1f = [self.sb(g10, f"h1f{j}", [128, D], F32) for j in range(2)]

        def g10_s2(tt):
            s1, Bs1 = s1t[tt % 3]
            lf, Blf = h1f[tt % 2]
            self.ln_tile(s1[:, :], Bs1, lf[:, :], Blf,
                         g=g1s[:, :], Bg=Bg1s, b=b1s[:, :], Bb=Bb1s, slot=tt)
            self.rowstore(self.s["h1"], tt * 128, D, lf[:, :], Blf, self.Bs["h1"])
            hb, Bhb = h1b[tt % 3]
            k.op("dve", lambda e, hb=hb, lf=lf: e.tensor_copy(hb[:, :], lf[:, :]),
                 reads=[Blf], writes=[Bhb])

        def g10_s3(tt):
            hb, Bhb = h1b[tt % 3]
            ht, Bht = h1ts[tt % 3]
            self.t_tile_to_dram(hb, Bhb, ht, Bht, self.s["h1T"], self.Bs["h1T"],
                                tt, 4 + tt % 2)

        for it in range(TT + 2):
            if it < TT:
                tt = it
                l, Bl = lx[tt % 3]
                self.rowload("sp", l[:, :], self.s["lnx2"], tt * 128, D, Bl,
                             Bsrc=self.Bs["lnx2"])
                s1, Bs1 = s1t[tt % 3]
                sb_, Bsb_ = ocb2[tt % 3]
                sT, BsT = ocT2[tt % 3]
                self.ctx_proj_tile(tt, Wo, BWo, sb_, Bsb_, sT, BsT, 6 + tt % 2,
                                   (tt % 2) * 2, s1, Bs1, l, Bl)
            if 1 <= it <= TT:
                g10_s2(it - 1)
            if it >= 2:
                g10_s3(it - 2)
        k.bury(BWo, Bg1s, Bb1s, *[b for _, b in lx], *[b for _, b in s1t],
               *[b for _, b in h1b], *[b for _, b in h1ts],
               *[b for _, b in h1f],
               *[b for _, b in ocb2], *[b for _, b in ocT2])
        g10.close()

        self.ff_moe_out(gW1, W1, BW1g)

        for qe in ("sp", "pool", "act"):
            for j, c in enumerate(self.k.dcnt[qe]):
                if c > 0:
                    self.k.eng["sp"].wait_ge(self.k.dsem[qe][j], c)
        self.es.__exit__(None, None, None)
        return self.nc

    def qkv(self, Wqk, BWqk, wrow_qk, xT, BxTl, QTo, BQTl, KTo, BKTl):
        k, T, TT, TC = self.k, self.T, self.TT, self.TC
        for ec in range(16):
            dstT, Bdl = (QTo, BQTl) if ec < 8 else (KTo, BKTl)
            hp = ec % 8
            for tc in range(TC):
                psj = (ec * TC + tc) % 2
                xdeps = [BxTl[tc * 4 + i] for i in range(4)]
                for dc in range(8):
                    k.op("pe", lambda e, dc=dc, ec=ec, tc=tc, psj=psj: e.matmul(
                        self.ps[psj][:, :],
                        Wqk[:, dc * wrow_qk + ec * 128: dc * wrow_qk + ec * 128 + 128],
                        xT[:, dc * T + tc * 512: dc * T + tc * 512 + 512],
                        start=(dc == 0), stop=(dc == 7)),
                        reads=[BWqk] + xdeps, writes=[self.Bps[psj]], inc=(dc == 7))
                if (ec * TC + tc) % 2 == 0:
                    k.op("act", lambda e, hp=hp, tc=tc, psj=psj, dstT=dstT: e.activation(
                        dstT[:, hp * T + tc * 512: hp * T + tc * 512 + 512],
                        self.ps[psj][:, :], AF.Copy),
                        reads=[self.Bps[psj]], writes=[Bdl[hp]])
                else:
                    k.op("dve", lambda e, hp=hp, tc=tc, psj=psj, dstT=dstT: e.tensor_copy(
                        dstT[:, hp * T + tc * 512: hp * T + tc * 512 + 512],
                        self.ps[psj][:, :]),
                        reads=[self.Bps[psj]], writes=[Bdl[hp]])

    def qkv_v(self, Wv, BWv, wrow_v, v_off, xT, BxTl, VAo, BVAl):
        k, T, TT = self.k, self.T, self.TT
        for tt in range(TT):
            for half in range(2):
                psj = 2 + (tt * 2 + half) % 2
                for dc in range(8):
                    k.op("pe", lambda e, dc=dc, tt=tt, half=half, psj=psj: e.matmul(
                        self.ps[psj][:, :],
                        xT[:, dc * T + tt * 128: dc * T + tt * 128 + 128],
                        Wv[:, dc * wrow_v + v_off + half * 512: dc * wrow_v + v_off + half * 512 + 512],
                        start=(dc == 0), stop=(dc == 7)),
                        reads=[BxTl[tt], BWv], writes=[self.Bps[psj]], inc=(dc == 7))
                k.op("dve", lambda e, tt=tt, half=half, psj=psj: e.tensor_copy(
                    self.ap3(VAo, tt * 1040 + half * 8 * 65, [65, 8], [1, 64]),
                    self.ps[psj][:, :]),
                    reads=[self.Bps[psj]], writes=[BVAl[tt]])
            k.op("dve", lambda e, tt=tt: e.memset(
                self.ap3(VAo, tt * 1040 + 64, [65, 16], [1, 1]), 1.0),
                writes=[BVAl[tt]])

    def attention(self, QT, BQTl, KT, BKTl, VA, BVAl, Psets, ONc, bd, QuStg):
        k, T, TT, TC = self.k, self.T, self.TT, self.TC
        tri, zero = self.sb_p["tri"], self.sb_p["zero"]

        def emit_pv(tcc, hp, pset, onc, Bonc):
            P0, BP0l, P1, BP1l = pset
            Pt, BPt = (P0, P1), (BP0l, BP1l)
            for j in range(4):
                tt = tcc * 4 + j
                for h in range(2):
                    head = 2 * hp + h
                    psj = 6 + (tt * 2 + h) % 2
                    pb, Bpl = Pt[h], BPt[h]
                    for sc in range(tt + 1):
                        k.op("pe", lambda e, sc=sc, j=j, psj=psj, pb=pb,
                             tt=tt, head=head: e.matmul(
                            self.ps[psj][:, 0:65],
                            pb[:, sc * 512 + j * 128: sc * 512 + j * 128 + 128],
                            VA[:, sc * 1040 + head * 65: sc * 1040 + head * 65 + 65],
                            start=(sc == 0), stop=(sc == tt)),
                            reads=[Bpl[sc], BVAl[sc]], writes=[self.Bps[psj]],
                            inc=(sc == tt))
                    k.op("dve", lambda e, psj=psj: e.reciprocal(
                        self.sb_p["zrec"][:, :], self.ps[psj][:, 64:65]),
                        reads=[self.Bps[psj]], writes=[self.Bp["zrec"]])
                    k.op("dve", lambda e, psj=psj, j=j, head=head,
                         onc=onc: e.tensor_scalar(
                        onc[:, j * 1024 + head * 64: j * 1024 + head * 64 + 64],
                        self.ps[psj][:, 0:64], self.sb_p["zrec"][:, :], None,
                        ALU.mult, ALU.bypass),
                        reads=[self.Bps[psj], self.Bp["zrec"]], writes=[Bonc])

        for tcc in range(TC):
            onc, Bonc = ONc[0]
            prev = None
            for hp in range(8):
                pset = Psets[hp % 2]
                P0, BP0l, P1, BP1l = pset
                Pt, BPt = (P0, P1), (BP0l, BP1l)
                if bd:
                    qu, Bqu = QuStg[hp % 2]
                    k.op("dve", lambda e, hp=hp, tcc=tcc, qu=qu: e.tensor_scalar(
                        qu[:, :], QT[:, hp * T + tcc * 512: hp * T + tcc * 512 + 512],
                        self.sb_p["uv"][:, hp:hp + 1], None, ALU.add, ALU.bypass),
                        reads=[BQTl[hp], self.Bp["uv"]], writes=[Bqu])
                    rhs_q, rq_off, qdep = qu, 0, Bqu
                else:
                    rhs_q, rq_off, qdep = QT, None, None
                sc_max = tcc * 4 + 4
                GW = 3
                for g0 in range(0, sc_max, GW):
                    scg = range(g0, min(g0 + GW, sc_max))
                    gpar = (g0 // GW) % 2
                    scols = (scg[-1] - g0 + 1) * 128
                    if bd:
                        jmin_g = max(0, g0 - tcc * 4)
                        for h in range(2):
                            head = 2 * hp + h
                            for j in range(jmin_g, 4):
                                scols_j = (min(scg[-1], tcc * 4 + j) - g0 + 1) * 128
                                if scols_j <= 0:
                                    continue
                                w, Bw = self.bdsh[h][j][gpar]
                                k.dma("sp", w[:, 0:scols_j],
                                      bass.AP(self.s["bd"][head].tensor,
                                              ((tcc * 4 + j) * 128) * (T - 1)
                                              + (T - 1) + g0 * 128,
                                              [[T - 1, 128], [1, scols_j]]),
                                      reads=[self.Bs["bd"][head],
                                             self.Bs["bdg"][head]],
                                      writes=[Bw])
                    for sc in scg:
                        c0 = max(0, sc - tcc * 4) * 128
                        for h in range(2):
                            psj = h + (sc % 3) * 2
                            qd = qdep if qdep is not None else BQTl[hp]
                            k.waits_for("pe", reads=[BKTl[hp], qd],
                                        writes=[self.Bps[psj]])
                    for sc in scg:
                        c0 = max(0, sc - tcc * 4) * 128
                        for h in range(2):
                            psj = h + (sc % 3) * 2
                            r0 = h * 64
                            qoff = rq_off if rq_off is not None else hp * T + tcc * 512
                            qd = qdep if qdep is not None else BQTl[hp]
                            k.op("pe", lambda e, hp=hp, sc=sc, h=h, psj=psj, r0=r0,
                                 c0=c0, rhs_q=rhs_q, qoff=qoff: e.matmul(
                                self.ps[psj][:, c0:512],
                                KT[r0:r0 + 64, hp * T + sc * 128: hp * T + sc * 128 + 128],
                                rhs_q[r0:r0 + 64, qoff + c0: qoff + 512],
                                start=True, stop=not bd, tile_position=(r0, 0)),
                                reads=[BKTl[hp], qd], writes=[self.Bps[psj]],
                                inc=not bd, nowait=True)
                    if bd:
                        for sc in scg:
                            jmin = max(0, sc - tcc * 4)
                            for h in range(2):
                                psj = h + (sc % 3) * 2
                                for j in range(jmin, 4):
                                    w, Bw = self.bdsh[h][j][gpar]
                                    k.waits_for("pe", reads=[Bw, self.Bp["ident"]],
                                                writes=[self.Bps[psj]])
                        for sc in scg:
                            jmin = max(0, sc - tcc * 4)
                            for h in range(2):
                                psj = h + (sc % 3) * 2
                                for j in range(jmin, 4):
                                    w, Bw = self.bdsh[h][j][gpar]
                                    k.op("pe", lambda e, j=j, sc=sc, g0=g0, psj=psj,
                                         w=w: e.matmul(
                                        self.ps[psj][:, j * 128:(j + 1) * 128],
                                        w[:, (sc - g0) * 128: (sc - g0) * 128 + 128],
                                        self.sb_p["ident"][:, :],
                                        start=False, stop=(j == 3)),
                                        reads=[Bw, self.Bp["ident"]],
                                        writes=[self.Bps[psj]], inc=(j == 3),
                                        nowait=True)
                    for sc in scg:
                        jmin = max(0, sc - tcc * 4)
                        c0 = jmin * 128
                        for h in range(2):
                            psj = h + (sc % 3) * 2
                            pb, Bpl = Pt[h], BPt[h]
                            po = sc * 512 + c0
                            ncol = 512 - c0
                            k.op("act", lambda e, psj=psj, pb=pb, po=po, ncol=ncol,
                                 c0=c0: e.activation(
                                pb[:, po: po + ncol], self.ps[psj][:, c0:512], AF.Exp,
                                scale=SCALE),
                                reads=[self.Bps[psj]], writes=[Bpl[sc]])
                            if not bd:
                                k.op("dve", lambda e, pb=pb, po=po, ncol=ncol: e.tensor_scalar(
                                    pb[:, po: po + ncol], pb[:, po: po + ncol],
                                    EMIN, EMAX, ALU.max, ALU.min),
                                    reads=[Bpl[sc]], writes=[Bpl[sc]])
                            if sc >= tcc * 4:
                                dof = sc * 512 + (sc - tcc * 4) * 128
                                k.op("dve", lambda e, pb=pb, dof=dof: e.copy_predicated(
                                    pb[:, dof: dof + 128], tri[:, :], zero[:, :]),
                                    reads=[self.Bp["tri"], self.Bp["zero"], Bpl[sc]],
                                    writes=[Bpl[sc]])
                            if DEBUG_TAPS and hp == 0 and h == 0 and not bd:
                                tp = self._tapP
                                k.dma("sp", bass.AP(tp.tensor,
                                                    (sc * 128) * T + tcc * 512 + c0,
                                                    [[T, 128], [1, ncol]]),
                                      pb[:, po: po + ncol], reads=[Bpl[sc]])
                if prev is not None:
                    emit_pv(tcc, prev[0], prev[1], onc, Bonc)
                prev = (hp, pset)
            emit_pv(tcc, prev[0], prev[1], onc, Bonc)
            for j in range(4):
                tt = tcc * 4 + j
                self.rowstore_bf(self.s["octx"], tt * 128, D,
                                 onc[:, j * 1024: j * 1024 + 1024], Bonc,
                                 self.Bs["octx"])

    def t_tile_to_dram(self, hb, Bhb, ht, Bht, dram, Bdram, tt, psj):
        k, T = self.k, self.T
        psb = self.ps[psj].ap().bitcast(BF16)
        for c in range(8):
            k.op("pe", lambda e, c=c, hb=hb, psb=psb: e.matmul(
                psb[:, c * 128:(c + 1) * 128], hb[:, c * 128:(c + 1) * 128],
                self.sb_p["ident"][:, :], start=True, stop=True,
                is_transpose=True, skip_group_check=True),
                reads=[Bhb, self.Bp["ident"]], writes=[self.Bps[psj]], inc=(c == 7))
        k.op("dve", lambda e, ht=ht, psb=psb: e.tensor_copy(ht[:, :], psb[:, 0:1024]),
             reads=[self.Bps[psj]], writes=[Bht])
        k.dma("act",
              bass.AP(dram.tensor, tt * 128, [[T, 128], [128 * T, 8], [1, 128]]),
              self.ap3(ht, 0, [128, 8], [1, 128]), reads=[Bht], writes_acc=[Bdram])

    def alloc_w1(self, es):
        W1, _ = self.sb(es, "W1", [128, 8 * FF], BF16, side="right")
        BW1g = [self.k.buf(f"W1g{g}") for g in range(4)]
        return W1, BW1g

    def load_w1(self, W1, BW1g, src_tensor, base_off):
        for fg in range(4):
            for dc in range(8):
                self.k.dma("pool",
                           W1[:, dc * FF + fg * 1024: dc * FF + fg * 1024 + 1024],
                           bass.AP(src_tensor, base_off + dc * 128 * FF + fg * 1024,
                                   [[FF, 128], [1, 1024]]),
                           writes_acc=[BW1g[fg]])

    def load_w2(self, W2, BW2g, src_tensor, base_off):
        for fg in range(4):
            for f8 in range(8):
                fc = fg * 8 + f8
                self.k.dma("pool", W2[:, fc * D:(fc + 1) * D],
                           bass.AP(src_tensor, base_off + fc * 128 * D,
                                   [[D, 128], [1, D]]),
                           writes_acc=[BW2g[fg]])

    def ff_moe_out(self, gW1, W1, BW1g):
        k, T, TT, NTC = self.k, self.T, self.TT, self.NTC
        gW = ExitStack()
        W2, _ = self.sb(gW, "W2", [128, 32 * D], BF16, side="right")
        BW2g = [k.buf(f"W2g{g}") for g in range(4)]
        hid, Bhid = self.sb(gW, "hidT", [128, 32 * 256], BF16, side="right")

        gF = ExitStack()
        g2s, Bg2s = self.sb(gF, "g2s", [128, D], F32)
        b2s, Bb2s = self.sb(gF, "b2s", [128, D], F32)
        bf2, Bbf2 = self.sb(gF, "bff2s", [128, D], F32)
        k.dma("sp", g2s[:, :], self.i["g2"], writes=[Bg2s])
        k.dma("sp", b2s[:, :], self.i["b2"], writes=[Bb2s])
        k.dma("sp", bf2[:, :], self.i["bff2r"], writes=[Bbf2])
        self.load_w2(W2, BW2g, self.i["fWff2"].tensor, 0)
        hw = [self.sb(gF, f"hw{j}", [128, 8 * 256], BF16) for j in range(2)]
        h1r = [self.sb(gF, f"h1r{j}", [128, D], F32) for j in range(2)]
        h2b = [self.sb(gF, f"h2b{j}", [128, D], BF16) for j in range(2)]
        h2ts = [self.sb(gF, f"h2ts{j}", [128, 1024], BF16) for j in range(2)]
        for tc in range(NTC):
            w, Bw = hw[tc % 2]
            k.dma("sp", w[:, :],
                  bass.AP(self.s["h1T"].tensor, tc * 256,
                          [[T, 128], [128 * T, 8], [1, 256]]),
                  reads=[self.Bs["h1T"]], writes=[Bw])
            self.ff_block(W1, BW1g, W2, BW2g, hid, Bhid, w, Bw,
                          (self.sb_p["bff1c"], self.Bp["bff1c"], 0))
            for t2 in range(2):
                ti = tc * 2 + t2
                hr, Bhr = h1r[t2]
                self.rowload("sp", hr[:, :], self.s["h1"], ti * 128, D, Bhr,
                             Bsrc=self.Bs["h1"])
                for dh in range(2):
                    psj = 4 + t2 * 2 + dh
                    k.op("dve", lambda e, psj=psj, dh=dh, hr=hr: e.tensor_tensor(
                        hr[:, dh * 512: dh * 512 + 512], self.ps[psj][:, :],
                        hr[:, dh * 512: dh * 512 + 512], ALU.add),
                        reads=[self.Bps[psj], Bhr], writes=[Bhr])
                    k.op("dve", lambda e, dh=dh, hr=hr: e.tensor_tensor(
                        hr[:, dh * 512: dh * 512 + 512],
                        hr[:, dh * 512: dh * 512 + 512],
                        bf2[:, dh * 512: dh * 512 + 512], ALU.add),
                        reads=[Bhr, Bbf2], writes=[Bhr])
                self.ln_tile(hr[:, :], Bhr, hr[:, :], Bhr,
                             g=g2s[:, :], Bg=Bg2s, b=b2s[:, :], Bb=Bb2s, slot=ti)
                self.rowstore(self.s["h2"], ti * 128, D, hr[:, :], Bhr, self.Bs["h2"])
                hb, Bhb = h2b[t2]
                k.op("dve", lambda e, hb=hb, hr=hr: e.tensor_copy(hb[:, :], hr[:, :]),
                     reads=[Bhr], writes=[Bhb])
                ht, Bht = h2ts[t2]
                self.t_tile_to_dram(hb, Bhb, ht, Bht, self.s["h2T"], self.Bs["h2T"],
                                    ti, 2)
                psj = 3
                for dc in range(8):
                    k.op("pe", lambda e, dc=dc, ht=ht, psj=psj: e.matmul(
                        self.ps[psj][:, 0:8], ht[:, dc * 128:(dc + 1) * 128],
                        self.sb_p["wg"][:, dc * 8: dc * 8 + 8],
                        start=(dc == 0), stop=(dc == 7)),
                        reads=[Bht, self.Bp["wg"]], writes=[self.Bps[psj]],
                        inc=(dc == 7))
                self.gating(psj, ti)
        k.bury(Bg2s, Bb2s, Bbf2, *[b for _, b in hw], *[b for _, b in h1r],
               *[b for _, b in h2b], *[b for _, b in h2ts])
        gF.close()

        gM = ExitStack()
        msel, Bmsel = self.sb(gM, "msel", [128, TT * 1024], BF16)
        k.op("dve", lambda e: e.memset(msel[:, :], 0.0), writes=[Bmsel])
        mhw = [self.sb(gM, f"mhw{j}", [128, 8 * 256], BF16) for j in range(2)]
        for ex in range(NE):
            self.load_w1(W1, BW1g, self.i["fWe1"].tensor, ex * D * FF)
            self.load_w2(W2, BW2g, self.i["fWe2"].tensor, ex * FF * D)
            for tc in range(NTC):
                w, Bw = mhw[tc % 2]
                k.dma("sp", w[:, :],
                      bass.AP(self.s["h2T"].tensor, tc * 256,
                              [[T, 128], [128 * T, 8], [1, 256]]),
                      reads=[self.Bs["h2T"]], writes=[Bw])
                self.ff_block(W1, BW1g, W2, BW2g, hid, Bhid, w, Bw,
                              (self.sb_p["be1c"], self.Bp["be1c"], ex * 32))
                for t2 in range(2):
                    ti = tc * 2 + t2
                    for dh in range(2):
                        psj = 4 + t2 * 2 + dh
                        k.op("dve", lambda e, psj=psj, ti=ti, dh=dh, ex=ex: e.scalar_tensor_tensor(
                            msel[:, ti * 1024 + dh * 512: ti * 1024 + dh * 512 + 512],
                            self.ps[psj][:, :],
                            self.sb_p["maskg"][:, ti * 8 + ex: ti * 8 + ex + 1],
                            msel[:, ti * 1024 + dh * 512: ti * 1024 + dh * 512 + 512],
                            ALU.mult, ALU.add),
                            reads=[self.Bps[psj], self.Bp["maskg"], Bmsel],
                            writes=[Bmsel])
        k.bury(*BW1g, *BW2g, Bhid, *[b for _, b in mhw])
        gW.close()
        gW1.close()

        gO = ExitStack()
        WoS, BWoS = self.sb(gO, "fWoutS", [128, 8 * D], BF16, side="right")
        be2b, Bbe2b = self.sb(gO, "be2b", [NE, D], BF16, side="right")
        g3s, Bg3s = self.sb(gO, "g3s", [128, D], F32, side="right")
        b3s, Bb3s = self.sb(gO, "b3s", [128, D], F32, side="right")
        for dc in range(8):
            k.dma("pool", WoS[:, dc * D:(dc + 1) * D],
                  bass.AP(self.i["fWout"].tensor, dc * 128 * D, [[D, 128], [1, D]]),
                  writes=[BWoS])
        k.dma("pool", be2b[:, :], self.i["be2"], writes=[Bbe2b])
        k.dma("sp", g3s[:, :], self.i["g3"], writes=[Bg3s])
        k.dma("sp", b3s[:, :], self.i["b3"], writes=[Bb3s])
        h2r = [self.sb(gO, f"h2r{j}", [128, D], F32, side="right") for j in range(3)]
        h3b = [self.sb(gO, f"h3b{j}", [128, D], BF16, side="right") for j in range(3)]
        h3ts = [self.sb(gO, f"h3ts{j}", [128, D], BF16, side="right") for j in range(3)]
        x2r = [self.sb(gO, f"x2r{j}", [128, D], F32, side="right") for j in range(3)]
        outt = [self.sb(gO, f"ou{j}", [128, D], F32, side="right") for j in range(3)]
        h3f = [self.sb(gO, f"h3f{j}", [128, D], F32, side="right") for j in range(2)]

        def go_s2(tt):
            hr, Bhr = h2r[tt % 3]
            lf, Blf = h3f[tt % 2]
            self.ln_tile(hr[:, :], Bhr, lf[:, :], Blf,
                         g=g3s[:, :], Bg=Bg3s, b=b3s[:, :], Bb=Bb3s, slot=tt)
            hb, Bhb = h3b[tt % 3]
            k.op("dve", lambda e, hb=hb, lf=lf: e.tensor_copy(hb[:, :], lf[:, :]),
                 reads=[Blf], writes=[Bhb])

        def go_s3(tt):
            hb, Bhb = h3b[tt % 3]
            ht, Bht = h3ts[tt % 3]
            psjT = 2 + tt % 2
            psb = self.ps[psjT].ap().bitcast(BF16)
            for c in range(8):
                k.op("pe", lambda e, c=c, hb=hb, psb=psb: e.matmul(
                    psb[:, c * 128:(c + 1) * 128], hb[:, c * 128:(c + 1) * 128],
                    self.sb_p["ident"][:, :], start=True, stop=True,
                    is_transpose=True, skip_group_check=True),
                    reads=[Bhb, self.Bp["ident"]], writes=[self.Bps[psjT]],
                    inc=(c == 7))
            k.op("dve", lambda e, ht=ht, psb=psb: e.tensor_copy(ht[:, :],
                                                               psb[:, 0:1024]),
                 reads=[self.Bps[psjT]], writes=[Bht])
            xr, Bxr = x2r[tt % 3]
            self.rowload("sp", xr[:, :], self.s["x2"], tt * 128, D, Bxr,
                         Bsrc=self.Bs["x2"])
            ot, Bot = outt[tt % 3]
            for half in range(2):
                psj = 4 + (tt % 2) * 2 + half
                for dc in range(8):
                    k.op("pe", lambda e, dc=dc, tt=tt, half=half, psj=psj, ht=ht: e.matmul(
                        self.ps[psj][:, :],
                        ht[:, dc * 128:(dc + 1) * 128],
                        WoS[:, dc * D + half * 512: dc * D + half * 512 + 512],
                        start=(dc == 0), stop=(dc == 7)),
                        reads=[Bht, BWoS], writes=[self.Bps[psj]], inc=(dc == 7))
                k.op("dve", lambda e, half=half, psj=psj, ot=ot, xr=xr: e.tensor_tensor(
                    ot[:, half * 512: half * 512 + 512], self.ps[psj][:, :],
                    xr[:, half * 512: half * 512 + 512], ALU.add),
                    reads=[self.Bps[psj], Bxr], writes=[Bot])
            k.dma("act", bass.AP(self.out.tensor, tt * 128 * D, [[D, 128], [1, D]]),
                  ot[:, :], reads=[Bot])

        def go_s1(tt):
            hr, Bhr = h2r[tt % 3]
            self.rowload("sp", hr[:, :], self.s["h2"], tt * 128, D, Bhr,
                         Bsrc=self.Bs["h2"])
            for dh in range(2):
                psj = dh
                k.op("pe", lambda e, tt=tt, dh=dh, psj=psj: e.matmul(
                    self.ps[psj][:, :],
                    self.sb_p["maskgT"][0:NE, tt * 128: tt * 128 + 128],
                    be2b[0:NE, dh * 512: dh * 512 + 512],
                    start=True, stop=True),
                    reads=[self.Bp["maskgT"], Bbe2b], writes=[self.Bps[psj]])
                k.op("dve", lambda e, psj=psj, dh=dh, tt=tt, hr=hr: e.tensor_tensor(
                    hr[:, dh * 512: dh * 512 + 512], self.ps[psj][:, :],
                    hr[:, dh * 512: dh * 512 + 512], ALU.add),
                    reads=[self.Bps[psj], Bhr], writes=[Bhr])
                k.op("dve", lambda e, dh=dh, tt=tt, hr=hr: e.tensor_tensor(
                    hr[:, dh * 512: dh * 512 + 512],
                    hr[:, dh * 512: dh * 512 + 512],
                    msel[:, tt * 1024 + dh * 512: tt * 1024 + dh * 512 + 512],
                    ALU.add),
                    reads=[Bhr, Bmsel], writes=[Bhr])

        for it in range(TT + 2):
            if it < TT:
                go_s1(it)
            if 1 <= it <= TT:
                go_s2(it - 1)
            if it >= 2:
                go_s3(it - 2)
        gM.close()
        gO.close()

    def ff_block(self, W1, BW1g, W2, BW2g, hid, Bhid, hT, BhT, bias1):
        k = self.k
        b1sb, Bb1, boff = bias1
        for fc in range(32):
            psj = fc % 2
            for dc in range(8):
                k.op("pe", lambda e, fc=fc, dc=dc, psj=psj: e.matmul(
                    self.ps[psj][:, 0:256],
                    W1[:, dc * FF + fc * 128: dc * FF + fc * 128 + 128],
                    hT[:, dc * 256: dc * 256 + 256],
                    start=(dc == 0), stop=(dc == 7)),
                    reads=[BW1g[fc // 8], BhT], writes=[self.Bps[psj]], inc=(dc == 7))
            k.op("act", lambda e, fc=fc, psj=psj: e.activation(
                hid[:, fc * 256: fc * 256 + 256], self.ps[psj][:, 0:256], AF.Relu,
                bias=b1sb[:, boff + fc: boff + fc + 1]),
                reads=[self.Bps[psj], Bb1], writes=[Bhid])
        for t2 in range(2):
            for dh in range(2):
                psj = 4 + t2 * 2 + dh
                for fc in range(32):
                    k.op("pe", lambda e, fc=fc, t2=t2, dh=dh, psj=psj: e.matmul(
                        self.ps[psj][:, :],
                        hid[:, fc * 256 + t2 * 128: fc * 256 + t2 * 128 + 128],
                        W2[:, fc * D + dh * 512: fc * D + dh * 512 + 512],
                        start=(fc == 0), stop=(fc == 31)),
                        reads=[Bhid, BW2g[fc // 8]], writes=[self.Bps[psj]],
                        inc=(fc == 31))

    def gating(self, psj, ti):
        k, p, B = self.k, self.sb_p, self.Bp
        k.op("dve", lambda e: e.reduce_max(p["gm"][:, :], self.ps[psj][:, 0:8], axis=AX.X),
             reads=[self.Bps[psj]], writes=[B["gm"]])
        k.op("dve", lambda e: e.tensor_scalar(p["ge"][:, :], self.ps[psj][:, 0:8],
                                              p["gm"][:, :], None, ALU.subtract,
                                              ALU.bypass),
             reads=[self.Bps[psj], B["gm"]], writes=[B["ge"]])
        k.op("act", lambda e: e.activation(p["ge"][:, :], p["ge"][:, :], AF.Exp,
                                           accum_out=p["gz"][:, :]),
             reads=[B["ge"]], writes=[B["ge"], B["gz"]])
        k.op("dve", lambda e: e.tensor_scalar(p["gv"][:, :], p["ge"][:, :], 1.0, None,
                                              ALU.is_equal, ALU.bypass),
             reads=[B["ge"]], writes=[B["gv"]])
        k.op("dve", lambda e: e.tensor_scalar(p["ge"][:, :], p["iota8"][:, :], -1.0,
                                              8.0, ALU.mult, ALU.add),
             reads=[B["iota8"]], writes=[B["ge"]])
        k.op("dve", lambda e: e.tensor_tensor(p["ge"][:, :], p["ge"][:, :],
                                              p["gv"][:, :], ALU.mult),
             reads=[B["ge"], B["gv"]], writes=[B["ge"]])
        k.op("dve", lambda e: e.reduce_max(p["gm"][:, :], p["ge"][:, :], axis=AX.X),
             reads=[B["ge"]], writes=[B["gm"]])
        k.op("dve", lambda e: e.tensor_scalar(p["gv"][:, :], p["ge"][:, :],
                                              p["gm"][:, :], None, ALU.is_equal,
                                              ALU.bypass),
             reads=[B["ge"], B["gm"]], writes=[B["gv"]])
        k.op("dve", lambda e: e.reciprocal(p["gz"][:, :], p["gz"][:, :]),
             reads=[B["gz"]], writes=[B["gz"]])
        k.op("dve", lambda e, ti=ti: e.tensor_scalar(
            p["maskg"][:, ti * 8: ti * 8 + 8], p["gv"][:, :], p["gz"][:, :], None,
            ALU.mult, ALU.bypass),
            reads=[B["gv"], B["gz"]], writes=[B["maskg"]])
        psT = 2
        k.op("pe", lambda e, ti=ti: e.matmul(
            self.ps[psT][0:8, 0:128], p["maskg"][:, ti * 8: ti * 8 + 8],
            p["identf"][:, :], start=True, stop=True, is_transpose=True),
            reads=[B["maskg"], B["identf"]], writes=[self.Bps[psT]])
        k.op("dve", lambda e, ti=ti: e.tensor_copy(
            p["maskgT"][0:NE, ti * 128: ti * 128 + 128], self.ps[psT][0:8, 0:128]),
            reads=[self.Bps[psT]], writes=[B["maskgT"]])


_CACHE = {}


def build_program(T):
    if T not in _CACHE:
        _CACHE[T] = Builder(T).build()
    return _CACHE[T]


def make_inputs(full, T):
    peT, ident, identf, tri, iota8 = host_consts(T)
    rep = lambda v: np.ascontiguousarray(np.broadcast_to(v, (128, D)))
    u = full["L0_ffn_u_bias"]
    v = full["L0_ffn_v_bias"]
    uv = np.ascontiguousarray(
        np.concatenate([np.asarray(u).reshape(8, 128).T,
                        np.asarray(v).reshape(8, 128).T], axis=1))
    be1c = np.ascontiguousarray(
        np.asarray(full["L0_ffn_be1"]).reshape(8, 32, 128).transpose(2, 0, 1)
        .reshape(128, 256))
    bff1c = np.ascontiguousarray(np.asarray(full["L0_ffn_bff1"]).reshape(32, 128).T)
    shared = dict(
        aWqkv=full["L0_attn_Wqkv"], aWout=full["L0_attn_Wout"],
        fWqkv=full["L0_ffn_Wqkv"], fWo=full["L0_ffn_Wo"], fWr=full["L0_ffn_Wr"],
        fWff1=full["L0_ffn_Wff1"], fWff2=full["L0_ffn_Wff2"], fWg=full["L0_ffn_Wg"],
        fWe1=full["L0_ffn_We1"], fWe2=full["L0_ffn_We2"], fWout=full["L0_ffn_Wout"],
        uv=uv, be1c=be1c, bff1c=bff1c,
        bff2r=rep(np.asarray(full["L0_ffn_bff2"])), be2=full["L0_ffn_be2"],
        g1=rep(np.asarray(full["L0_ffn_ln1_g"])), b1=rep(np.asarray(full["L0_ffn_ln1_b"])),
        g2=rep(np.asarray(full["L0_ffn_ln2_g"])), b2=rep(np.asarray(full["L0_ffn_ln2_b"])),
        g3=rep(np.asarray(full["L0_ffn_ln3_g"])), b3=rep(np.asarray(full["L0_ffn_ln3_b"])),
        peT=peT, ident=ident, identf=identf, tri=tri, iota8=iota8,
    )
    shared = {k2: np.ascontiguousarray(v2) for k2, v2 in shared.items()}
    x = np.asarray(full["L0_attn_x"])
    in_maps = []
    for c in range(x.shape[0]):
        d = dict(shared)
        d["x"] = np.ascontiguousarray(x[c])
        in_maps.append(d)
    return in_maps


def kernel(**inputs):
    x = inputs["L0_attn_x"]
    B, T, _ = x.shape
    nc = build_program(T)
    in_maps = make_inputs(inputs, T)
    res = run_bass_kernel_spmd(nc, in_maps, core_ids=list(range(B)))
    out = np.stack([np.asarray(r["out"], dtype=np.float32) for r in res.results])
    return out


# version 13 (final)
# speedup vs baseline: 1.0829x; 1.0728x over previous
import numpy as np
import ml_dtypes
from contextlib import ExitStack
import concourse.bass as bass
import concourse.mybir as mybir
from concourse.bass_utils import run_bass_kernel_spmd

F32 = mybir.dt.float32
BF16 = mybir.dt.bfloat16
U8 = mybir.dt.uint8
AF = mybir.ActivationFunctionType
ALU = mybir.AluOpType
AX = mybir.AxisListType

D = 1024
NH = 16
FF = 4096
NE = 8
EPS = 1e-5
SCALE = 0.125
EMIN = float(np.exp(np.float32(-10.0)))
EMAX = float(np.exp(np.float32(10.0)))

DEBUG_SCRATCH_OUT = False
DEBUG_TAPS = False


class Tok:
    __slots__ = ("sem", "val")

    def __init__(self, sem, val):
        self.sem = sem
        self.val = val


class Buf:
    __slots__ = ("name", "w", "rds", "mw")

    def __init__(self, name="buf", seed=None):
        self.name = name
        self.w = None
        self.rds = dict(seed) if seed else {}
        self.mw = {}


ENGS = ("pe", "dve", "act", "pool", "sp")


class K:
    def __init__(self, nc, es, n_dma_sems=8):
        self.nc = nc
        self.eng = {"pe": nc.tensor, "dve": nc.vector, "act": nc.scalar,
                    "pool": nc.gpsimd, "sp": nc.sync}
        self.sem = {}
        self.cnt = {}
        for e in ENGS:
            self.sem[e] = es.enter_context(nc.semaphore(f"sem_{e}"))
            self.cnt[e] = 0
        self.dsem = {}
        self.dcnt = {}
        self.didx = {}
        for qe in ("sp", "pool", "act"):
            self.dsem[qe] = [es.enter_context(nc.semaphore(f"dma_{qe}{i}"))
                             for i in range(n_dma_sems)]
            self.dcnt[qe] = [0] * n_dma_sems
            self.didx[qe] = 0
        self.hw = {e: {} for e in ENGS}
        self.pending = {e: ([], []) for e in ENGS}
        self.graveyard = {}

    def _wait(self, e, tok):
        if tok is None:
            return
        key = id(tok.sem)
        if self.hw[e].get(key, -1) >= tok.val:
            return
        self.eng[e].wait_ge(tok.sem, tok.val)
        self.hw[e][key] = tok.val

    def _dep_toks(self, reads, writes):
        toks = []
        for b in reads:
            if b.w is not None:
                toks.append(b.w)
            toks.extend(b.mw.values())
        for b in writes:
            if b.w is not None:
                toks.append(b.w)
            toks.extend(b.mw.values())
            toks.extend(b.rds.values())
        return toks

    def waits_for(self, e, reads=(), writes=()):
        for t in self._dep_toks(reads, writes):
            self._wait(e, t)

    def op(self, e, fn, reads=(), writes=(), inc=True, nowait=False):
        if not nowait:
            for t in self._dep_toks(reads, writes):
                self._wait(e, t)
        instr = fn(self.eng[e])
        pr, pw = self.pending[e]
        if inc:
            self.cnt[e] += 1
            tok = Tok(self.sem[e], self.cnt[e])
            instr.then_inc(self.sem[e], 1)
            for b in list(pr) + list(reads):
                b.rds[e] = tok
            for b in list(pw) + list(writes):
                b.w = tok
                b.rds = {}
                b.mw = {}
            pr.clear()
            pw.clear()
            return tok
        pr.extend(reads)
        pw.extend(writes)
        return None

    def dma(self, qe, out_ap, in_ap, reads=(), writes=(), writes_acc=(), **kw):
        for t in self._dep_toks(reads, writes):
            self._wait(qe, t)
        for b in writes_acc:
            for t in b.rds.values():
                self._wait(qe, t)
            if b.w is not None:
                self._wait(qe, b.w)
        i = self.didx[qe]
        self.didx[qe] = (i + 1) % len(self.dsem[qe])
        sem = self.dsem[qe][i]
        if self.dcnt[qe][i] > 0:
            self._wait(qe, Tok(sem, self.dcnt[qe][i]))
        self.eng[qe].dma_start(out=out_ap, in_=in_ap, **kw).then_inc(sem, 16)
        self.dcnt[qe][i] += 16
        tok = Tok(sem, self.dcnt[qe][i])
        for b in reads:
            b.rds[f"q{qe}{i}"] = tok
        for b in writes:
            b.w = tok
            b.rds = {}
            b.mw = {}
        for b in writes_acc:
            b.mw[id(sem)] = tok
        return tok

    def buf(self, name="buf"):
        return Buf(name, seed=self.graveyard)

    def bury(self, *bufs):
        for b in bufs:
            toks = list(b.rds.values()) + list(b.mw.values())
            if b.w is not None:
                toks.append(b.w)
            for t in toks:
                key = id(t.sem)
                old = self.graveyard.get(key)
                if old is None or old.val < t.val:
                    self.graveyard[key] = t


def host_consts(T):
    pos = np.arange(T - 1, -1, -1, dtype=np.float32)
    invf = (1.0 / 10000.0 ** (np.arange(0, D, 2, dtype=np.float32) / D)).astype(np.float32)
    sinu = pos[:, None] * invf[None, :]
    pe = np.concatenate([np.sin(sinu), np.cos(sinu)], axis=1).astype(np.float32)
    peT = np.ascontiguousarray(pe.T)
    ident = np.eye(128, dtype=ml_dtypes.bfloat16)
    identf = np.eye(128, dtype=np.float32)
    tri = np.tril(np.ones((128, 128), dtype=np.uint8), k=-1)
    iota8 = np.broadcast_to(np.arange(8, dtype=np.float32), (128, 8)).copy()
    return peT, ident, identf, tri, iota8


class Builder:
    def __init__(self, T):
        self.T = T
        self.TT = T // 128
        self.TC = T // 512
        self.NTC = T // 256
        nc = bass.Bass("TRN2", target_bir_lowering=False)
        self.nc = nc
        self.es = ExitStack()
        self.es.__enter__()
        self.k = K(nc, self.es)

        def din(name, shape, dt=F32):
            return nc.dram_tensor(name, shape, dt, kind="ExternalInput").ap()

        sk = "ExternalOutput" if DEBUG_SCRATCH_OUT else "Internal"

        def dscr(name, shape, dt=F32):
            return nc.dram_tensor(name, shape, dt, kind=sk).ap()

        i = self.i = {}
        i["x"] = din("x", [T, D])
        i["aWqkv"] = din("aWqkv", [3 * D, D])
        i["aWout"] = din("aWout", [D, D])
        i["fWqkv"] = din("fWqkv", [D, 3 * D])
        i["fWo"] = din("fWo", [D, D])
        i["fWr"] = din("fWr", [D, D])
        i["fWff1"] = din("fWff1", [D, FF])
        i["fWff2"] = din("fWff2", [FF, D])
        i["fWg"] = din("fWg", [D, NE])
        i["fWe1"] = din("fWe1", [NE, D, FF])
        i["fWe2"] = din("fWe2", [NE, FF, D])
        i["fWout"] = din("fWout", [D, D])
        i["uv"] = din("uv", [128, 16])
        i["be1c"] = din("be1c", [128, NE * 32])
        i["bff1c"] = din("bff1c", [128, 32])
        i["bff2r"] = din("bff2r", [128, D])
        i["be2"] = din("be2", [NE, D])
        for nm in ("g1", "b1", "g2", "b2", "g3", "b3"):
            i[nm] = din(nm, [128, D])
        i["peT"] = din("peT", [D, T])
        i["ident"] = din("ident", [128, 128], BF16)
        i["identf"] = din("identf", [128, 128])
        i["tri"] = din("tri", [128, 128], U8)
        i["iota8"] = din("iota8", [128, 8])

        s = self.s = {}
        self.Bs = {}
        s["bd"] = [dscr(f"bd{h}", [T * T], BF16) for h in range(NH)]
        self.Bs["bd"] = [self.k.buf(f"bd{h}") for h in range(NH)]
        self.Bs["bdg"] = [self.k.buf(f"bdg{h}") for h in range(NH)]
        for nm, shape, dt in [("x2", [T, D], F32), ("lnx2", [T, D], F32),
                              ("h1", [T, D], F32), ("h2", [T, D], F32),
                              ("h1T", [D, T], BF16), ("h2T", [D, T], BF16),
                              ("octx", [T, D], BF16)]:
            s[nm] = dscr(nm, shape, dt)
            self.Bs[nm] = self.k.buf(nm)
        self.out = nc.dram_tensor("out", [T, D], F32, kind="ExternalOutput").ap()
        self._tapP = (nc.dram_tensor("tapP", [T, T], BF16, kind="ExternalOutput").ap()
                      if DEBUG_TAPS else None)

        self.ps = [self.es.enter_context(nc.psum_tensor(f"ps{j}", [128, 512], F32))
                   for j in range(8)]
        self.Bps = [self.k.buf(f"ps{j}") for j in range(8)]

        self.sb_p = {}
        self.Bp = {}
        for nm, shape, dt in [
            ("ident", [128, 128], BF16), ("identf", [128, 128], F32),
            ("tri", [128, 128], U8), ("iota8", [128, 8], F32),
            ("uv", [128, 16], F32), ("wg", [128, NE * 8], BF16),
            ("be1c", [128, NE * 32], F32), ("bff1c", [128, 32], F32),
            ("zero", [128, 128], BF16), ("ones1", [1, 128], BF16),
            ("epsc", [128, 1], F32),
            ("bn12a", [128, 12], F32), ("mva", [128, 2], F32),
            ("sda", [128, 1], F32), ("rstda", [128, 1], F32),
            ("bn12b", [128, 12], F32), ("mvb", [128, 2], F32),
            ("sdb", [128, 1], F32), ("rstdb", [128, 1], F32),
            ("zrec", [128, 1], F32),
            ("maskg", [128, (T // 128) * 8], F32),
            ("maskgT", [NE, T], BF16),
            ("gm", [128, 1], F32), ("gz", [128, 1], F32),
            ("ge", [128, 8], F32), ("gv", [128, 8], F32),
        ]:
            self.sb_p[nm] = self.es.enter_context(nc.sbuf_tensor("c_" + nm, shape, dt))
            self.Bp[nm] = self.k.buf(nm)

    def tap(self, name, src_ap, Bsrc, shape):
        if not DEBUG_TAPS:
            return
        t = self.nc.dram_tensor("tap_" + name, list(shape), src_ap.dtype,
                                kind="ExternalOutput").ap()
        n = int(np.prod(shape))
        self.k.dma("sp", bass.AP(t.tensor, 0, [[1, n]]) if False else t, src_ap,
                   reads=[Bsrc])

    def sb(self, es, name, shape, dt, side="left"):
        t = es.enter_context(self.nc.sbuf_tensor(name, shape, dt, side=side))
        return t, self.k.buf(name)

    def ap3(self, t, off, d1, d2):
        free = int(np.prod(t.shape[1:]))
        return bass.AP(t, off, [[free, t.shape[0]], list(d1), list(d2)])

    def rowload(self, qe, dst_sb, dram_ap, r0, ncols, Bdst, nrows=128, Bsrc=None):
        reads = [Bsrc] if Bsrc is not None else []
        self.k.dma(qe, dst_sb, bass.AP(dram_ap.tensor, r0 * ncols,
                                       [[ncols, nrows], [1, ncols]]),
                   reads=reads, writes=[Bdst])

    def rowstore(self, dst_dram_ap, r0, ncols, src_sb, Bsrc, Bdst, qe="act"):
        self.k.dma(qe, bass.AP(dst_dram_ap.tensor, r0 * ncols,
                               [[ncols, 128], [1, ncols]]),
                   src_sb, reads=[Bsrc], writes_acc=[Bdst])

    def rowstore_bf(self, dst_dram_ap, r0, ncols, src_sb, Bsrc, Bdst):
        self.rowstore(dst_dram_ap, r0, ncols, src_sb, Bsrc, Bdst, qe="pool")

    def ctx_proj_tile(self, tt, W, BW, stgb, Bstgb, stgT, BstgT, psjT, pbase,
                      dst, Bdst, res, Bres):
        k, T = self.k, self.T
        self.rowload("sp", stgb[:, :], self.s["octx"], tt * 128, D, Bstgb,
                     Bsrc=self.Bs["octx"])
        psb = self.ps[psjT].ap().bitcast(BF16)
        for c in range(8):
            k.op("pe", lambda e, c=c: e.matmul(
                psb[:, c * 128:(c + 1) * 128], stgb[:, c * 128:(c + 1) * 128],
                self.sb_p["ident"][:, :], start=True, stop=True,
                is_transpose=True, skip_group_check=True),
                reads=[Bstgb, self.Bp["ident"]], writes=[self.Bps[psjT]],
                inc=(c == 7))
        k.op("dve", lambda e: e.tensor_copy(stgT[:, :], psb[:, 0:1024]),
             reads=[self.Bps[psjT]], writes=[BstgT])
        for half in range(2):
            pj2 = pbase + half
            for dc in range(8):
                k.op("pe", lambda e, dc=dc, half=half, pj2=pj2: e.matmul(
                    self.ps[pj2][:, :], stgT[:, dc * 128:(dc + 1) * 128],
                    W[:, dc * D + half * 512: dc * D + half * 512 + 512],
                    start=(dc == 0), stop=(dc == 7)),
                    reads=[BstgT, BW], writes=[self.Bps[pj2]], inc=(dc == 7))
            k.op("dve", lambda e, half=half, pj2=pj2, dst=dst, res=res: e.tensor_tensor(
                dst[:, half * 512: half * 512 + 512], self.ps[pj2][:, :],
                res[:, half * 512: half * 512 + 512], ALU.add),
                reads=[self.Bps[pj2], Bres], writes=[Bdst])

    def ln_tile(self, src, Bsrc, out_f32, Bout_f32, out_bf=None, Bout_bf=None,
                g=None, Bg=None, b=None, Bb=None, slot=0):
        k, p, B = self.k, self.sb_p, self.Bp
        sx = "a" if slot % 2 == 0 else "b"
        bn12, mv, sd, rstd = (p["bn12" + sx], p["mv" + sx], p["sd" + sx],
                              p["rstd" + sx])
        Bbn, Bmv, Bsd, Brs = (B["bn12" + sx], B["mv" + sx], B["sd" + sx],
                              B["rstd" + sx])
        k.op("dve", lambda e: e.bn_stats(bn12[:, 0:6], src[:, 0:512]),
             reads=[Bsrc], writes=[Bbn], inc=False)
        k.op("dve", lambda e: e.bn_stats(bn12[:, 6:12], src[:, 512:1024]),
             reads=[Bsrc], writes=[Bbn])
        k.op("dve", lambda e: e.bn_aggr(mv[:, :], bn12[:, :]),
             reads=[Bbn], writes=[Bmv])
        k.op("act", lambda e: e.activation(sd[:, :], mv[:, 1:2], AF.Sqrt,
                                           bias=p["epsc"][:, :]),
             reads=[Bmv, B["epsc"]], writes=[Bsd])
        k.op("dve", lambda e: e.reciprocal(rstd[:, :], sd[:, :]),
             reads=[Bsd], writes=[Brs])
        if g is None:
            k.op("dve", lambda e: e.tensor_scalar(
                out_f32, src, mv[:, 0:1], rstd[:, :], ALU.subtract, ALU.mult),
                reads=[Bsrc, Bmv, Brs], writes=[Bout_f32])
        else:
            k.op("dve", lambda e: e.tensor_scalar(
                out_f32, src, mv[:, 0:1], None, ALU.subtract, ALU.bypass),
                reads=[Bsrc, Bmv], writes=[Bout_f32])
            k.op("dve", lambda e: e.scalar_tensor_tensor(
                out_f32, out_f32, rstd[:, :], g, ALU.mult, ALU.mult),
                reads=[Bout_f32, Brs, Bg], writes=[Bout_f32])
            k.op("dve", lambda e: e.tensor_tensor(out_f32, out_f32, b, ALU.add),
                 reads=[Bout_f32, Bb], writes=[Bout_f32])
        if out_bf is not None:
            k.op("dve", lambda e: e.tensor_copy(out_bf, out_f32),
                 reads=[Bout_f32], writes=[Bout_bf])

    def transpose_tile(self, src, Bsrc, dst_t, Bdst, tt, psj, nch=8):
        k, T = self.k, self.T
        psb = self.ps[psj].ap().bitcast(BF16)
        idn = self.sb_p["ident"]
        for c in range(nch):
            k.op("pe", lambda e, c=c: e.matmul(
                psb[:, c * 128:(c + 1) * 128], src[:, c * 128:(c + 1) * 128],
                idn[:, :], start=True, stop=True, is_transpose=True,
                skip_group_check=True),
                reads=[Bsrc, self.Bp["ident"]], writes=[self.Bps[psj]],
                inc=(c == nch - 1))
        k.op("dve", lambda e: e.tensor_copy(
            self.ap3(dst_t, tt * 128, [T, nch], [1, 128]), psb[:, 0:nch * 128]),
            reads=[self.Bps[psj]], writes=[Bdst])

    def load_wT(self, qe, dram, nrow, wst, dstT, BdstT, ncols_t, psj=6, row0=0):
        k = self.k
        for ec in range(nrow):
            w, Bw = wst[ec % 2]
            self.rowload(qe, w[:, :], dram, (row0 + ec) * 128, D, Bw)
            pj = psj if psj != -1 else ec % 2
            psb = self.ps[pj].ap().bitcast(BF16)
            for dc in range(8):
                k.op("pe", lambda e, dc=dc, w=w, psb=psb: e.matmul(
                    psb[:, dc * 128:(dc + 1) * 128], w[:, dc * 128:(dc + 1) * 128],
                    self.sb_p["ident"][:, :], start=True, stop=True,
                    is_transpose=True, skip_group_check=True),
                    reads=[Bw, self.Bp["ident"]], writes=[self.Bps[pj]],
                    inc=(dc == 7))
            k.op("dve", lambda e, ec=ec, psb=psb: e.tensor_copy(
                self.ap3(dstT, ec * 128, [ncols_t, 8], [1, 128]), psb[:, 0:1024]),
                reads=[self.Bps[pj]], writes=[BdstT])

    def load_consts(self):
        k, i, p, B = self.k, self.i, self.sb_p, self.Bp
        T = self.T
        for nm in ("ident", "identf", "tri", "iota8", "uv", "be1c", "bff1c"):
            k.dma("sp", p[nm][:, :], i[nm], writes=[B[nm]])
        k.dma("pool", p["wg"][:, :],
              bass.AP(i["fWg"].tensor, 0, [[8, 128], [128 * 8, 8], [1, 8]]),
              writes=[B["wg"]])
        k.op("dve", lambda e: e.memset(p["zero"][:, :], 0.0), writes=[B["zero"]])
        k.op("dve", lambda e: e.memset(p["epsc"][:, :], EPS), writes=[B["epsc"]])
        k.op("dve", lambda e: e.memset(p["ones1"][:, :], 1.0), writes=[B["ones1"]])
        zg = ExitStack()
        z2, Bz2 = self.sb(zg, "zfill", [128, 1024], BF16)
        k.op("dve", lambda e: e.memset(z2[:, :], 0.0), writes=[Bz2])
        rows_per = 1024
        for h in range(NH):
            for r0 in range(0, T, rows_per):
                nr = min(rows_per, T - r0)
                k.dma("act",
                      bass.AP(self.s["bd"][h].tensor, r0 * T, [[T, nr], [1, 128]]),
                      bass.AP(z2, 0, [[1024, 128], [1, nr]]),
                      reads=[Bz2], writes_acc=[self.Bs["bdg"][h]])
        self.k.bury(Bz2)
        zg.close()

    def build(self):
        k, T, TT, TC = self.k, self.T, self.TT, self.TC
        self.load_consts()

        g1 = ExitStack()
        lnxT, _ = self.sb(g1, "lnxT", [128, 8 * T], BF16)
        BlnxTl = [self.k.buf(f"lnxT{t}") for t in range(TT)]
        wst = [self.sb(g1, f"wst{j}", [128, D], BF16) for j in range(2)]
        xin = [self.sb(g1, f"xin{j}", [128, D], F32) for j in range(3)]
        lnb = [self.sb(g1, f"lnb{j}", [128, D], BF16) for j in range(3)]
        g2 = ExitStack()
        QT, _ = self.sb(g2, "QT", [128, 8 * T], BF16, side="right")
        KT, _ = self.sb(g2, "KT", [128, 8 * T], BF16, side="right")
        BQTl = [self.k.buf(f"QT{h}") for h in range(8)]
        BKTl = [self.k.buf(f"KT{h}") for h in range(8)]
        g1b = ExitStack()
        Wqk, BWqk = self.sb(g1b, "aWqkT", [128, 8 * 2048], BF16)

        self.load_wT("pool", self.i["aWqkv"], 16, wst, Wqk, BWqk, 2048, psj=-1)
        for tt in range(TT):
            xt, Bxt = xin[tt % 3]
            self.rowload("sp", xt[:, :], self.i["x"], tt * 128, D, Bxt)
            lb, Blb = lnb[tt % 3]
            self.ln_tile(xt[:, :], Bxt, xt[:, :], Bxt, out_bf=lb[:, :], Bout_bf=Blb,
                         slot=tt)
            self.transpose_tile(lb[:, :], Blb, lnxT, BlnxTl[tt], tt, 2 + tt % 2)
        self.qkv(Wqk, BWqk, 2048, lnxT, BlnxTl, QT, BQTl, KT, BKTl)
        k.bury(BWqk)
        g1b.close()
        g2b = ExitStack()
        VA, _ = self.sb(g2b, "VA", [128, TT * 1040], BF16, side="right")
        BVAl = [self.k.buf(f"VA{t}") for t in range(TT)]
        g1c = ExitStack()
        Wv, BWv = self.sb(g1c, "aWvT", [128, 8 * 1024], BF16)
        self.load_wT("pool", self.i["aWqkv"], 8, wst, Wv, BWv, 1024, psj=-1, row0=16)
        self.qkv_v(Wv, BWv, 1024, 0, lnxT, BlnxTl, VA, BVAl)
        k.bury(BWv, *BlnxTl, *[b for _, b in wst], *[b for _, b in xin],
               *[b for _, b in lnb])
        g1c.close()
        g1.close()

        g3 = ExitStack()
        Psets = []
        for si in range(2):
            Pa, _ = self.sb(g3, f"P{si}a", [128, TT * 512], BF16)
            Pb, _ = self.sb(g3, f"P{si}b", [128, TT * 512], BF16)
            Psets.append((Pa, [self.k.buf(f"P{si}a{t}") for t in range(TT)],
                          Pb, [self.k.buf(f"P{si}b{t}") for t in range(TT)]))
        ONc = [self.sb(g3, "ONc0", [128, 4 * 1024], BF16)]
        self.attention(QT, BQTl, KT, BKTl, VA, BVAl, Psets, ONc, bd=False,
                       QuStg=None)
        for (Pa, BPal, Pb, BPbl) in Psets:
            k.bury(*BPal, *BPbl)
        k.bury(*BQTl, *BKTl, *BVAl, *[b for _, b in ONc])
        g3.close()
        g2b.close()
        g2.close()

        g4 = ExitStack()
        lnx2T, _ = self.sb(g4, "lnx2T", [128, 8 * T], BF16, side="right")
        Blnx2Tl = [self.k.buf(f"lnx2T{t}") for t in range(TT)]
        g4b = ExitStack()
        WoT, BWoT = self.sb(g4b, "aWoutT", [128, 8 * D], BF16)
        wst2 = [self.sb(g4b, f"wsg{j}", [128, D], BF16) for j in range(2)]
        self.load_wT("pool", self.i["aWout"], 8, wst2, WoT, BWoT, D, psj=6)
        xin2 = [self.sb(g4b, f"xi2{j}", [128, D], F32) for j in range(3)]
        x2t = [self.sb(g4b, f"x2t{j}", [128, D], F32) for j in range(3)]
        lnf2 = [self.sb(g4b, f"lnf2{j}", [128, D], F32) for j in range(2)]
        lnb2 = [self.sb(g4b, f"lb2{j}", [128, D], BF16) for j in range(2)]
        ocb = [self.sb(g4b, f"ocb{j}", [128, D], BF16) for j in range(3)]
        ocT = [self.sb(g4b, f"ocT{j}", [128, D], BF16) for j in range(3)]

        def a4_s2(tt):
            x2, Bx2 = x2t[tt % 3]
            self.rowstore(self.s["x2"], tt * 128, D, x2[:, :], Bx2, self.Bs["x2"])
            lf, Blf = lnf2[tt % 2]
            self.ln_tile(x2[:, :], Bx2, lf[:, :], Blf, slot=tt)
            self.rowstore(self.s["lnx2"], tt * 128, D, lf[:, :], Blf,
                          self.Bs["lnx2"])
            lb, Blb = lnb2[tt % 2]
            k.op("dve", lambda e, lb=lb, lf=lf: e.tensor_copy(lb[:, :], lf[:, :]),
                 reads=[Blf], writes=[Blb])

        def a4_s3(tt):
            lb, Blb = lnb2[tt % 2]
            self.transpose_tile(lb[:, :], Blb, lnx2T, Blnx2Tl[tt], tt, 4 + tt % 2)

        for it in range(TT + 2):
            if it < TT:
                tt = it
                xt, Bxt = xin2[tt % 3]
                self.rowload("sp", xt[:, :], self.i["x"], tt * 128, D, Bxt)
                x2, Bx2 = x2t[tt % 3]
                sb_, Bsb_ = ocb[tt % 3]
                sT, BsT = ocT[tt % 3]
                self.ctx_proj_tile(tt, WoT, BWoT, sb_, Bsb_, sT, BsT, 6 + tt % 2,
                                   (tt % 2) * 2, x2, Bx2, xt, Bxt)
            if 1 <= it <= TT:
                a4_s2(it - 1)
            if it >= 2:
                a4_s3(it - 2)
        k.bury(BWoT, *[b for _, b in wst2], *[b for _, b in xin2],
               *[b for _, b in x2t], *[b for _, b in lnb2],
               *[b for _, b in lnf2],
               *[b for _, b in ocb], *[b for _, b in ocT])
        g4b.close()

        g6 = ExitStack()
        QT2, _ = self.sb(g6, "QT2", [128, 8 * T], BF16)
        KT2, _ = self.sb(g6, "KT2", [128, 8 * T], BF16)
        BQT2l = [self.k.buf(f"QT2{h}") for h in range(8)]
        BKT2l = [self.k.buf(f"KT2{h}") for h in range(8)]
        g6b = ExitStack()
        VA2, _ = self.sb(g6b, "VA2", [128, TT * 1040], BF16)
        BVA2l = [self.k.buf(f"VA2{t}") for t in range(TT)]
        g5 = ExitStack()
        fq, Bfq = self.sb(g5, "fqkv", [128, 8 * 3 * D], BF16, side="right")
        for dc in range(8):
            self.k.dma("pool", fq[:, dc * 3 * D:(dc + 1) * 3 * D],
                       bass.AP(self.i["fWqkv"].tensor, dc * 128 * 3 * D,
                               [[3 * D, 128], [1, 3 * D]]),
                       writes=[Bfq])
        self.qkv(fq, Bfq, 3 * D, lnx2T, Blnx2Tl, QT2, BQT2l, KT2, BKT2l)
        self.qkv_v(fq, Bfq, 3 * D, 2048, lnx2T, Blnx2Tl, VA2, BVA2l)
        k.bury(Bfq, *Blnx2Tl)
        g5.close()
        g4.close()

        g8 = ExitStack()
        rT, _ = self.sb(g8, "rT", [128, 8 * T], BF16, side="right")
        BrTl = [self.k.buf(f"rT{h}") for h in range(8)]
        g7 = ExitStack()
        peT, BpeT = self.sb(g7, "s_peT", [128, 8 * T], BF16, side="right")
        Wr, BWr = self.sb(g7, "Wr", [128, 8 * D], BF16, side="right")
        for dc in range(8):
            self.k.dma("pool", peT[:, dc * T:(dc + 1) * T],
                       bass.AP(self.i["peT"].tensor, dc * 128 * T, [[T, 128], [1, T]]),
                       writes=[BpeT])
            self.k.dma("pool", Wr[:, dc * D:(dc + 1) * D],
                       bass.AP(self.i["fWr"].tensor, dc * 128 * D, [[D, 128], [1, D]]),
                       writes=[BWr])
        for hp in range(8):
            for tc in range(TC):
                psj = (hp * TC + tc) % 2
                for dc in range(8):
                    k.op("pe", lambda e, dc=dc, hp=hp, tc=tc, psj=psj: e.matmul(
                        self.ps[psj][:, :],
                        Wr[:, dc * D + hp * 128: dc * D + hp * 128 + 128],
                        peT[:, dc * T + tc * 512: dc * T + tc * 512 + 512],
                        start=(dc == 0), stop=(dc == 7)),
                        reads=[BWr, BpeT], writes=[self.Bps[psj]], inc=(dc == 7))
                if (hp * TC + tc) % 2 == 0:
                    k.op("act", lambda e, hp=hp, tc=tc, psj=psj: e.activation(
                        rT[:, hp * T + tc * 512: hp * T + tc * 512 + 512],
                        self.ps[psj][:, :], AF.Copy),
                        reads=[self.Bps[psj]], writes=[BrTl[hp]])
                else:
                    k.op("dve", lambda e, hp=hp, tc=tc, psj=psj: e.tensor_copy(
                        rT[:, hp * T + tc * 512: hp * T + tc * 512 + 512],
                        self.ps[psj][:, :]),
                        reads=[self.Bps[psj]], writes=[BrTl[hp]])
        k.bury(BpeT, BWr)
        g7.close()

        stg8 = ExitStack()
        qv = [self.sb(stg8, f"qv{j}", [128, 128], BF16, side="right") for j in range(2)]
        bst = [[self.sb(stg8, f"bst{j}{p}", [128, TC * 512], BF16, side="right")
                for p in range(2)] for j in range(2)]
        for hp in range(8):
            for tt in range(TT):
                q, Bq = qv[tt % 2]
                k.op("dve", lambda e, hp=hp, tt=tt, q=q: e.tensor_scalar(
                    q[:, :], QT2[:, hp * T + tt * 128: hp * T + tt * 128 + 128],
                    self.sb_p["uv"][:, 8 + hp: 9 + hp], None, ALU.add, ALU.bypass),
                    reads=[BQT2l[hp], self.Bp["uv"]], writes=[Bq])
                jc_min = max(0, (T - 128 * (tt + 1)) // 512)
                njc = TC - jc_min
                for jc in range(jc_min, TC):
                    for h in range(2):
                        psj = h * 2 + (jc % 2)
                        k.waits_for("pe", reads=[Bq, BrTl[hp]],
                                    writes=[self.Bps[psj]])
                    for h in range(2):
                        psj = h * 2 + (jc % 2)
                        r0 = h * 64
                        k.op("pe", lambda e, jc=jc, hp=hp, r0=r0, q=q, psj=psj: e.matmul(
                            self.ps[psj][:, :], q[r0:r0 + 64, :],
                            rT[r0:r0 + 64, hp * T + jc * 512: hp * T + jc * 512 + 512],
                            start=True, stop=True, tile_position=(r0, 0)),
                            reads=[Bq, BrTl[hp]], writes=[self.Bps[psj]],
                            nowait=True)
                    for h in range(2):
                        psj = h * 2 + (jc % 2)
                        b, Bb = bst[h][tt % 2]
                        if (jc + h) % 2 == 0:
                            k.op("act", lambda e, jc=jc, jc_min=jc_min, b=b, psj=psj: e.activation(
                                b[:, (jc - jc_min) * 512: (jc - jc_min) * 512 + 512],
                                self.ps[psj][:, :], AF.Copy),
                                reads=[self.Bps[psj]], writes=[Bb])
                        else:
                            k.op("dve", lambda e, jc=jc, jc_min=jc_min, b=b, psj=psj: e.tensor_copy(
                                b[:, (jc - jc_min) * 512: (jc - jc_min) * 512 + 512],
                                self.ps[psj][:, :]),
                                reads=[self.Bps[psj]], writes=[Bb])
                for h in range(2):
                    head = 2 * hp + h
                    b, Bb = bst[h][tt % 2]
                    njc = TC - jc_min
                    k.dma("pool",
                          bass.AP(self.s["bd"][head].tensor,
                                  tt * 128 * T + jc_min * 512,
                                  [[T, 128], [1, njc * 512]]),
                          b[:, 0: njc * 512],
                          reads=[Bb, self.Bs["bdg"][head]],
                          writes_acc=[self.Bs["bd"][head]])
        k.bury(*[b for _, b in qv], *[b for bb in bst for _, b in bb], *BrTl)
        stg8.close()
        g8.close()

        g9 = ExitStack()
        Psets2 = []
        for si in range(2):
            Pa, _ = self.sb(g9, f"fP{si}a", [128, TT * 512], BF16, side="right")
            Pb, _ = self.sb(g9, f"fP{si}b", [128, TT * 512], BF16, side="right")
            Psets2.append((Pa, [self.k.buf(f"fP{si}a{t}") for t in range(TT)],
                           Pb, [self.k.buf(f"fP{si}b{t}") for t in range(TT)]))
        ONc2 = [self.sb(g9, "fONc0", [128, 4 * 1024], BF16, side="right")]
        stg9 = ExitStack()
        self.bdsh = [[[self.sb(stg9, f"bdsh{h}{j}{p}", [128, 512], BF16, side="right")
                       for p in range(2)] for j in range(4)] for h in range(2)]
        QuStg = [self.sb(stg9, f"qu{j}", [128, 512], BF16, side="right")
                 for j in range(2)]
        self.attention(QT2, BQT2l, KT2, BKT2l, VA2, BVA2l, Psets2, ONc2,
                       bd=True, QuStg=QuStg)
        for (Pa, BPal, Pb, BPbl) in Psets2:
            k.bury(*BPal, *BPbl)
        k.bury(*BQT2l, *BKT2l, *BVA2l, *[b for _, b in ONc2],
               *[b for _, b in QuStg])
        for hh in self.bdsh:
            for jj in hh:
                for _, bb in jj:
                    k.bury(bb)
        stg9.close()
        g9.close()
        g6b.close()
        g6.close()

        gW1 = ExitStack()
        W1, BW1g = self.alloc_w1(gW1)
        self.load_w1(W1, BW1g, self.i["fWff1"].tensor, 0)

        g10 = ExitStack()
        Wo, BWo = self.sb(g10, "fWoS", [128, 8 * D], BF16)
        g1s, Bg1s = self.sb(g10, "s_g1", [128, D], F32)
        b1s, Bb1s = self.sb(g10, "s_b1", [128, D], F32)
        for dc in range(8):
            self.k.dma("pool", Wo[:, dc * D:(dc + 1) * D],
                       bass.AP(self.i["fWo"].tensor, dc * 128 * D, [[D, 128], [1, D]]),
                       writes=[BWo])
        k.dma("sp", g1s[:, :], self.i["g1"], writes=[Bg1s])
        k.dma("sp", b1s[:, :], self.i["b1"], writes=[Bb1s])
        lx = [self.sb(g10, f"lx{j}", [128, D], F32) for j in range(3)]
        s1t = [self.sb(g10, f"s1{j}", [128, D], F32) for j in range(3)]
        h1b = [self.sb(g10, f"h1b{j}", [128, D], BF16) for j in range(3)]
        h1ts = [self.sb(g10, f"h1ts{j}", [128, 1024], BF16) for j in range(3)]
        ocb2 = [self.sb(g10, f"oc2{j}", [128, D], BF16) for j in range(3)]
        ocT2 = [self.sb(g10, f"oT2{j}", [128, D], BF16) for j in range(3)]
        h1f = [self.sb(g10, f"h1f{j}", [128, D], F32) for j in range(2)]

        def g10_s2(tt):
            s1, Bs1 = s1t[tt % 3]
            lf, Blf = h1f[tt % 2]
            self.ln_tile(s1[:, :], Bs1, lf[:, :], Blf,
                         g=g1s[:, :], Bg=Bg1s, b=b1s[:, :], Bb=Bb1s, slot=tt)
            self.rowstore(self.s["h1"], tt * 128, D, lf[:, :], Blf, self.Bs["h1"])
            hb, Bhb = h1b[tt % 3]
            k.op("dve", lambda e, hb=hb, lf=lf: e.tensor_copy(hb[:, :], lf[:, :]),
                 reads=[Blf], writes=[Bhb])

        def g10_s3(tt):
            hb, Bhb = h1b[tt % 3]
            ht, Bht = h1ts[tt % 3]
            self.t_tile_to_dram(hb, Bhb, ht, Bht, self.s["h1T"], self.Bs["h1T"],
                                tt, 4 + tt % 2)

        for it in range(TT + 2):
            if it < TT:
                tt = it
                l, Bl = lx[tt % 3]
                self.rowload("sp", l[:, :], self.s["lnx2"], tt * 128, D, Bl,
                             Bsrc=self.Bs["lnx2"])
                s1, Bs1 = s1t[tt % 3]
                sb_, Bsb_ = ocb2[tt % 3]
                sT, BsT = ocT2[tt % 3]
                self.ctx_proj_tile(tt, Wo, BWo, sb_, Bsb_, sT, BsT, 6 + tt % 2,
                                   (tt % 2) * 2, s1, Bs1, l, Bl)
            if 1 <= it <= TT:
                g10_s2(it - 1)
            if it >= 2:
                g10_s3(it - 2)
        k.bury(BWo, Bg1s, Bb1s, *[b for _, b in lx], *[b for _, b in s1t],
               *[b for _, b in h1b], *[b for _, b in h1ts],
               *[b for _, b in h1f],
               *[b for _, b in ocb2], *[b for _, b in ocT2])
        g10.close()

        self.ff_moe_out(gW1, W1, BW1g)

        for qe in ("sp", "pool", "act"):
            for j, c in enumerate(self.k.dcnt[qe]):
                if c > 0:
                    self.k.eng["sp"].wait_ge(self.k.dsem[qe][j], c)
        self.es.__exit__(None, None, None)
        return self.nc

    def qkv(self, Wqk, BWqk, wrow_qk, xT, BxTl, QTo, BQTl, KTo, BKTl):
        k, T, TT, TC = self.k, self.T, self.TT, self.TC
        for ec in range(16):
            dstT, Bdl = (QTo, BQTl) if ec < 8 else (KTo, BKTl)
            hp = ec % 8
            for tc in range(TC):
                psj = (ec * TC + tc) % 2
                xdeps = [BxTl[tc * 4 + i] for i in range(4)]
                for dc in range(8):
                    k.op("pe", lambda e, dc=dc, ec=ec, tc=tc, psj=psj: e.matmul(
                        self.ps[psj][:, :],
                        Wqk[:, dc * wrow_qk + ec * 128: dc * wrow_qk + ec * 128 + 128],
                        xT[:, dc * T + tc * 512: dc * T + tc * 512 + 512],
                        start=(dc == 0), stop=(dc == 7)),
                        reads=[BWqk] + xdeps, writes=[self.Bps[psj]], inc=(dc == 7))
                if (ec * TC + tc) % 2 == 0:
                    k.op("act", lambda e, hp=hp, tc=tc, psj=psj, dstT=dstT: e.activation(
                        dstT[:, hp * T + tc * 512: hp * T + tc * 512 + 512],
                        self.ps[psj][:, :], AF.Copy),
                        reads=[self.Bps[psj]], writes=[Bdl[hp]])
                else:
                    k.op("dve", lambda e, hp=hp, tc=tc, psj=psj, dstT=dstT: e.tensor_copy(
                        dstT[:, hp * T + tc * 512: hp * T + tc * 512 + 512],
                        self.ps[psj][:, :]),
                        reads=[self.Bps[psj]], writes=[Bdl[hp]])

    def qkv_v(self, Wv, BWv, wrow_v, v_off, xT, BxTl, VAo, BVAl):
        k, T, TT = self.k, self.T, self.TT
        for tt in range(TT):
            for half in range(2):
                psj = 2 + (tt * 2 + half) % 2
                for dc in range(8):
                    k.op("pe", lambda e, dc=dc, tt=tt, half=half, psj=psj: e.matmul(
                        self.ps[psj][:, :],
                        xT[:, dc * T + tt * 128: dc * T + tt * 128 + 128],
                        Wv[:, dc * wrow_v + v_off + half * 512: dc * wrow_v + v_off + half * 512 + 512],
                        start=(dc == 0), stop=(dc == 7)),
                        reads=[BxTl[tt], BWv], writes=[self.Bps[psj]], inc=(dc == 7))
                k.op("dve", lambda e, tt=tt, half=half, psj=psj: e.tensor_copy(
                    self.ap3(VAo, tt * 1040 + half * 8 * 65, [65, 8], [1, 64]),
                    self.ps[psj][:, :]),
                    reads=[self.Bps[psj]], writes=[BVAl[tt]])
            k.op("dve", lambda e, tt=tt: e.memset(
                self.ap3(VAo, tt * 1040 + 64, [65, 16], [1, 1]), 1.0),
                writes=[BVAl[tt]])

    def attention(self, QT, BQTl, KT, BKTl, VA, BVAl, Psets, ONc, bd, QuStg):
        k, T, TT, TC = self.k, self.T, self.TT, self.TC
        tri, zero = self.sb_p["tri"], self.sb_p["zero"]

        def pv_groups(tcc, hp, pset, onc, Bonc):
            P0, BP0l, P1, BP1l = pset
            Pt, BPt = (P0, P1), (BP0l, BP1l)
            out = []
            for j in range(4):
                tt = tcc * 4 + j
                for h in range(2):
                    head = 2 * hp + h
                    out.append((j, h, tt, head, Pt[h], BPt[h]))
            return out

        def emit_pv_group(grp, onc, Bonc):
            j, h, tt, head, pb, Bpl = grp
            if True:
                if True:
                    psj = 6 + (tt * 2 + h) % 2
                    for sc in range(tt + 1):
                        k.op("pe", lambda e, sc=sc, j=j, psj=psj, pb=pb,
                             tt=tt, head=head: e.matmul(
                            self.ps[psj][:, 0:65],
                            pb[:, sc * 512 + j * 128: sc * 512 + j * 128 + 128],
                            VA[:, sc * 1040 + head * 65: sc * 1040 + head * 65 + 65],
                            start=(sc == 0), stop=(sc == tt)),
                            reads=[Bpl[sc], BVAl[sc]], writes=[self.Bps[psj]],
                            inc=(sc == tt))
                    k.op("dve", lambda e, psj=psj: e.reciprocal(
                        self.sb_p["zrec"][:, :], self.ps[psj][:, 64:65]),
                        reads=[self.Bps[psj]], writes=[self.Bp["zrec"]])
                    k.op("dve", lambda e, psj=psj, j=j, head=head,
                         onc=onc: e.tensor_scalar(
                        onc[:, j * 1024 + head * 64: j * 1024 + head * 64 + 64],
                        self.ps[psj][:, 0:64], self.sb_p["zrec"][:, :], None,
                        ALU.mult, ALU.bypass),
                        reads=[self.Bps[psj], self.Bp["zrec"]], writes=[Bonc])

        for tcc in range(TC):
            onc, Bonc = ONc[0]
            pend = []
            for hp in range(8):
                pset = Psets[hp % 2]
                P0, BP0l, P1, BP1l = pset
                Pt, BPt = (P0, P1), (BP0l, BP1l)
                if bd:
                    qu, Bqu = QuStg[hp % 2]
                    k.op("dve", lambda e, hp=hp, tcc=tcc, qu=qu: e.tensor_scalar(
                        qu[:, :], QT[:, hp * T + tcc * 512: hp * T + tcc * 512 + 512],
                        self.sb_p["uv"][:, hp:hp + 1], None, ALU.add, ALU.bypass),
                        reads=[BQTl[hp], self.Bp["uv"]], writes=[Bqu])
                    rhs_q, rq_off, qdep = qu, 0, Bqu
                else:
                    rhs_q, rq_off, qdep = QT, None, None
                sc_max = tcc * 4 + 4
                GW = 3
                ngrp = (sc_max + GW - 1) // GW
                pv_per = (len(pend) + ngrp - 1) // ngrp if pend else 0
                for g0 in range(0, sc_max, GW):
                    scg = range(g0, min(g0 + GW, sc_max))
                    gpar = (g0 // GW) % 2
                    scols = (scg[-1] - g0 + 1) * 128
                    if bd:
                        jmin_g = max(0, g0 - tcc * 4)
                        for h in range(2):
                            head = 2 * hp + h
                            for j in range(jmin_g, 4):
                                scols_j = (min(scg[-1], tcc * 4 + j) - g0 + 1) * 128
                                if scols_j <= 0:
                                    continue
                                w, Bw = self.bdsh[h][j][gpar]
                                k.dma("sp", w[:, 0:scols_j],
                                      bass.AP(self.s["bd"][head].tensor,
                                              ((tcc * 4 + j) * 128) * (T - 1)
                                              + (T - 1) + g0 * 128,
                                              [[T - 1, 128], [1, scols_j]]),
                                      reads=[self.Bs["bd"][head],
                                             self.Bs["bdg"][head]],
                                      writes=[Bw])
                    for sc in scg:
                        c0 = max(0, sc - tcc * 4) * 128
                        for h in range(2):
                            psj = h + (sc % 3) * 2
                            qd = qdep if qdep is not None else BQTl[hp]
                            k.waits_for("pe", reads=[BKTl[hp], qd],
                                        writes=[self.Bps[psj]])
                    for sc in scg:
                        c0 = max(0, sc - tcc * 4) * 128
                        for h in range(2):
                            psj = h + (sc % 3) * 2
                            r0 = h * 64
                            qoff = rq_off if rq_off is not None else hp * T + tcc * 512
                            qd = qdep if qdep is not None else BQTl[hp]
                            k.op("pe", lambda e, hp=hp, sc=sc, h=h, psj=psj, r0=r0,
                                 c0=c0, rhs_q=rhs_q, qoff=qoff: e.matmul(
                                self.ps[psj][:, c0:512],
                                KT[r0:r0 + 64, hp * T + sc * 128: hp * T + sc * 128 + 128],
                                rhs_q[r0:r0 + 64, qoff + c0: qoff + 512],
                                start=True, stop=not bd, tile_position=(r0, 0)),
                                reads=[BKTl[hp], qd], writes=[self.Bps[psj]],
                                inc=not bd, nowait=True)
                    if bd:
                        for sc in scg:
                            jmin = max(0, sc - tcc * 4)
                            for h in range(2):
                                psj = h + (sc % 3) * 2
                                for j in range(jmin, 4):
                                    w, Bw = self.bdsh[h][j][gpar]
                                    k.waits_for("pe", reads=[Bw, self.Bp["ident"]],
                                                writes=[self.Bps[psj]])
                        for sc in scg:
                            jmin = max(0, sc - tcc * 4)
                            for h in range(2):
                                psj = h + (sc % 3) * 2
                                for j in range(jmin, 4):
                                    w, Bw = self.bdsh[h][j][gpar]
                                    k.op("pe", lambda e, j=j, sc=sc, g0=g0, psj=psj,
                                         w=w: e.matmul(
                                        self.ps[psj][:, j * 128:(j + 1) * 128],
                                        w[:, (sc - g0) * 128: (sc - g0) * 128 + 128],
                                        self.sb_p["ident"][:, :],
                                        start=False, stop=(j == 3)),
                                        reads=[Bw, self.Bp["ident"]],
                                        writes=[self.Bps[psj]], inc=(j == 3),
                                        nowait=True)
                    for sc in scg:
                        jmin = max(0, sc - tcc * 4)
                        c0 = jmin * 128
                        for h in range(2):
                            psj = h + (sc % 3) * 2
                            pb, Bpl = Pt[h], BPt[h]
                            po = sc * 512 + c0
                            ncol = 512 - c0
                            k.op("act", lambda e, psj=psj, pb=pb, po=po, ncol=ncol,
                                 c0=c0: e.activation(
                                pb[:, po: po + ncol], self.ps[psj][:, c0:512], AF.Exp,
                                scale=SCALE),
                                reads=[self.Bps[psj]], writes=[Bpl[sc]])
                            if not bd:
                                k.op("dve", lambda e, pb=pb, po=po, ncol=ncol: e.tensor_scalar(
                                    pb[:, po: po + ncol], pb[:, po: po + ncol],
                                    EMIN, EMAX, ALU.max, ALU.min),
                                    reads=[Bpl[sc]], writes=[Bpl[sc]])
                            if sc >= tcc * 4:
                                dof = sc * 512 + (sc - tcc * 4) * 128
                                k.op("dve", lambda e, pb=pb, dof=dof: e.copy_predicated(
                                    pb[:, dof: dof + 128], tri[:, :], zero[:, :]),
                                    reads=[self.Bp["tri"], self.Bp["zero"], Bpl[sc]],
                                    writes=[Bpl[sc]])
                            if DEBUG_TAPS and hp == 0 and h == 0 and not bd:
                                tp = self._tapP
                                k.dma("sp", bass.AP(tp.tensor,
                                                    (sc * 128) * T + tcc * 512 + c0,
                                                    [[T, 128], [1, ncol]]),
                                      pb[:, po: po + ncol], reads=[Bpl[sc]])
                    for _ in range(pv_per):
                        if pend:
                            emit_pv_group(pend.pop(0), onc, Bonc)
                for grp in pend:
                    emit_pv_group(grp, onc, Bonc)
                pend = pv_groups(tcc, hp, pset, onc, Bonc)
            for grp in pend:
                emit_pv_group(grp, onc, Bonc)
            for j in range(4):
                tt = tcc * 4 + j
                self.rowstore_bf(self.s["octx"], tt * 128, D,
                                 onc[:, j * 1024: j * 1024 + 1024], Bonc,
                                 self.Bs["octx"])

    def t_tile_to_dram(self, hb, Bhb, ht, Bht, dram, Bdram, tt, psj):
        k, T = self.k, self.T
        psb = self.ps[psj].ap().bitcast(BF16)
        for c in range(8):
            k.op("pe", lambda e, c=c, hb=hb, psb=psb: e.matmul(
                psb[:, c * 128:(c + 1) * 128], hb[:, c * 128:(c + 1) * 128],
                self.sb_p["ident"][:, :], start=True, stop=True,
                is_transpose=True, skip_group_check=True),
                reads=[Bhb, self.Bp["ident"]], writes=[self.Bps[psj]], inc=(c == 7))
        k.op("dve", lambda e, ht=ht, psb=psb: e.tensor_copy(ht[:, :], psb[:, 0:1024]),
             reads=[self.Bps[psj]], writes=[Bht])
        k.dma("act",
              bass.AP(dram.tensor, tt * 128, [[T, 128], [128 * T, 8], [1, 128]]),
              self.ap3(ht, 0, [128, 8], [1, 128]), reads=[Bht], writes_acc=[Bdram])

    def alloc_w1(self, es):
        W1, _ = self.sb(es, "W1", [128, 8 * FF], BF16, side="right")
        BW1g = [self.k.buf(f"W1g{g}") for g in range(4)]
        return W1, BW1g

    def load_w1(self, W1, BW1g, src_tensor, base_off):
        for fg in range(4):
            for dc in range(8):
                self.k.dma("pool",
                           W1[:, dc * FF + fg * 1024: dc * FF + fg * 1024 + 1024],
                           bass.AP(src_tensor, base_off + dc * 128 * FF + fg * 1024,
                                   [[FF, 128], [1, 1024]]),
                           writes_acc=[BW1g[fg]])

    def load_w2(self, W2, BW2g, src_tensor, base_off):
        for fg in range(4):
            for f8 in range(8):
                fc = fg * 8 + f8
                self.k.dma("pool", W2[:, fc * D:(fc + 1) * D],
                           bass.AP(src_tensor, base_off + fc * 128 * D,
                                   [[D, 128], [1, D]]),
                           writes_acc=[BW2g[fg]])

    def ff_moe_out(self, gW1, W1, BW1g):
        k, T, TT, NTC = self.k, self.T, self.TT, self.NTC
        gW = ExitStack()
        W2, _ = self.sb(gW, "W2", [128, 32 * D], BF16, side="right")
        BW2g = [k.buf(f"W2g{g}") for g in range(4)]
        hid, Bhid = self.sb(gW, "hidT", [128, 32 * 256], BF16, side="right")

        gF = ExitStack()
        g2s, Bg2s = self.sb(gF, "g2s", [128, D], F32)
        b2s, Bb2s = self.sb(gF, "b2s", [128, D], F32)
        bf2, Bbf2 = self.sb(gF, "bff2s", [128, D], F32)
        k.dma("sp", g2s[:, :], self.i["g2"], writes=[Bg2s])
        k.dma("sp", b2s[:, :], self.i["b2"], writes=[Bb2s])
        k.dma("sp", bf2[:, :], self.i["bff2r"], writes=[Bbf2])
        self.load_w2(W2, BW2g, self.i["fWff2"].tensor, 0)
        hw = [self.sb(gF, f"hw{j}", [128, 8 * 256], BF16) for j in range(2)]
        h1r = [self.sb(gF, f"h1r{j}", [128, D], F32) for j in range(2)]
        h2b = [self.sb(gF, f"h2b{j}", [128, D], BF16) for j in range(2)]
        h2ts = [self.sb(gF, f"h2ts{j}", [128, 1024], BF16) for j in range(2)]
        for tc in range(NTC):
            w, Bw = hw[tc % 2]
            k.dma("sp", w[:, :],
                  bass.AP(self.s["h1T"].tensor, tc * 256,
                          [[T, 128], [128 * T, 8], [1, 256]]),
                  reads=[self.Bs["h1T"]], writes=[Bw])
            self.ff_block(W1, BW1g, W2, BW2g, hid, Bhid, w, Bw,
                          (self.sb_p["bff1c"], self.Bp["bff1c"], 0))
            for t2 in range(2):
                ti = tc * 2 + t2
                hr, Bhr = h1r[t2]
                self.rowload("sp", hr[:, :], self.s["h1"], ti * 128, D, Bhr,
                             Bsrc=self.Bs["h1"])
                for dh in range(2):
                    psj = 4 + t2 * 2 + dh
                    k.op("dve", lambda e, psj=psj, dh=dh, hr=hr: e.tensor_tensor(
                        hr[:, dh * 512: dh * 512 + 512], self.ps[psj][:, :],
                        hr[:, dh * 512: dh * 512 + 512], ALU.add),
                        reads=[self.Bps[psj], Bhr], writes=[Bhr])
                    k.op("dve", lambda e, dh=dh, hr=hr: e.tensor_tensor(
                        hr[:, dh * 512: dh * 512 + 512],
                        hr[:, dh * 512: dh * 512 + 512],
                        bf2[:, dh * 512: dh * 512 + 512], ALU.add),
                        reads=[Bhr, Bbf2], writes=[Bhr])
                self.ln_tile(hr[:, :], Bhr, hr[:, :], Bhr,
                             g=g2s[:, :], Bg=Bg2s, b=b2s[:, :], Bb=Bb2s, slot=ti)
                self.rowstore(self.s["h2"], ti * 128, D, hr[:, :], Bhr, self.Bs["h2"])
                hb, Bhb = h2b[t2]
                k.op("dve", lambda e, hb=hb, hr=hr: e.tensor_copy(hb[:, :], hr[:, :]),
                     reads=[Bhr], writes=[Bhb])
                ht, Bht = h2ts[t2]
                self.t_tile_to_dram(hb, Bhb, ht, Bht, self.s["h2T"], self.Bs["h2T"],
                                    ti, 2)
                psj = 3
                for dc in range(8):
                    k.op("pe", lambda e, dc=dc, ht=ht, psj=psj: e.matmul(
                        self.ps[psj][:, 0:8], ht[:, dc * 128:(dc + 1) * 128],
                        self.sb_p["wg"][:, dc * 8: dc * 8 + 8],
                        start=(dc == 0), stop=(dc == 7)),
                        reads=[Bht, self.Bp["wg"]], writes=[self.Bps[psj]],
                        inc=(dc == 7))
                self.gating(psj, ti)
        k.bury(Bg2s, Bb2s, Bbf2, *[b for _, b in hw], *[b for _, b in h1r],
               *[b for _, b in h2b], *[b for _, b in h2ts])
        gF.close()

        gM = ExitStack()
        msel, Bmsel = self.sb(gM, "msel", [128, TT * 1024], BF16)
        k.op("dve", lambda e: e.memset(msel[:, :], 0.0), writes=[Bmsel])
        mhw = [self.sb(gM, f"mhw{j}", [128, 8 * 256], BF16) for j in range(2)]
        for ex in range(NE):
            self.load_w1(W1, BW1g, self.i["fWe1"].tensor, ex * D * FF)
            self.load_w2(W2, BW2g, self.i["fWe2"].tensor, ex * FF * D)
            for tc in range(NTC):
                w, Bw = mhw[tc % 2]
                k.dma("sp", w[:, :],
                      bass.AP(self.s["h2T"].tensor, tc * 256,
                              [[T, 128], [128 * T, 8], [1, 256]]),
                      reads=[self.Bs["h2T"]], writes=[Bw])
                self.ff_block(W1, BW1g, W2, BW2g, hid, Bhid, w, Bw,
                              (self.sb_p["be1c"], self.Bp["be1c"], ex * 32))
                for t2 in range(2):
                    ti = tc * 2 + t2
                    for dh in range(2):
                        psj = 4 + t2 * 2 + dh
                        k.op("dve", lambda e, psj=psj, ti=ti, dh=dh, ex=ex: e.scalar_tensor_tensor(
                            msel[:, ti * 1024 + dh * 512: ti * 1024 + dh * 512 + 512],
                            self.ps[psj][:, :],
                            self.sb_p["maskg"][:, ti * 8 + ex: ti * 8 + ex + 1],
                            msel[:, ti * 1024 + dh * 512: ti * 1024 + dh * 512 + 512],
                            ALU.mult, ALU.add),
                            reads=[self.Bps[psj], self.Bp["maskg"], Bmsel],
                            writes=[Bmsel])
        k.bury(*BW1g, *BW2g, Bhid, *[b for _, b in mhw])
        gW.close()
        gW1.close()

        gO = ExitStack()
        WoS, BWoS = self.sb(gO, "fWoutS", [128, 8 * D], BF16, side="right")
        be2b, Bbe2b = self.sb(gO, "be2b", [NE, D], BF16, side="right")
        g3s, Bg3s = self.sb(gO, "g3s", [128, D], F32, side="right")
        b3s, Bb3s = self.sb(gO, "b3s", [128, D], F32, side="right")
        for dc in range(8):
            k.dma("pool", WoS[:, dc * D:(dc + 1) * D],
                  bass.AP(self.i["fWout"].tensor, dc * 128 * D, [[D, 128], [1, D]]),
                  writes=[BWoS])
        k.dma("pool", be2b[:, :], self.i["be2"], writes=[Bbe2b])
        k.dma("sp", g3s[:, :], self.i["g3"], writes=[Bg3s])
        k.dma("sp", b3s[:, :], self.i["b3"], writes=[Bb3s])
        h2r = [self.sb(gO, f"h2r{j}", [128, D], F32, side="right") for j in range(3)]
        h3b = [self.sb(gO, f"h3b{j}", [128, D], BF16, side="right") for j in range(3)]
        h3ts = [self.sb(gO, f"h3ts{j}", [128, D], BF16, side="right") for j in range(3)]
        x2r = [self.sb(gO, f"x2r{j}", [128, D], F32, side="right") for j in range(3)]
        outt = [self.sb(gO, f"ou{j}", [128, D], F32, side="right") for j in range(3)]
        h3f = [self.sb(gO, f"h3f{j}", [128, D], F32, side="right") for j in range(2)]

        def go_s2(tt):
            hr, Bhr = h2r[tt % 3]
            lf, Blf = h3f[tt % 2]
            self.ln_tile(hr[:, :], Bhr, lf[:, :], Blf,
                         g=g3s[:, :], Bg=Bg3s, b=b3s[:, :], Bb=Bb3s, slot=tt)
            hb, Bhb = h3b[tt % 3]
            k.op("dve", lambda e, hb=hb, lf=lf: e.tensor_copy(hb[:, :], lf[:, :]),
                 reads=[Blf], writes=[Bhb])

        def go_s3(tt):
            hb, Bhb = h3b[tt % 3]
            ht, Bht = h3ts[tt % 3]
            psjT = 2 + tt % 2
            psb = self.ps[psjT].ap().bitcast(BF16)
            for c in range(8):
                k.op("pe", lambda e, c=c, hb=hb, psb=psb: e.matmul(
                    psb[:, c * 128:(c + 1) * 128], hb[:, c * 128:(c + 1) * 128],
                    self.sb_p["ident"][:, :], start=True, stop=True,
                    is_transpose=True, skip_group_check=True),
                    reads=[Bhb, self.Bp["ident"]], writes=[self.Bps[psjT]],
                    inc=(c == 7))
            k.op("dve", lambda e, ht=ht, psb=psb: e.tensor_copy(ht[:, :],
                                                               psb[:, 0:1024]),
                 reads=[self.Bps[psjT]], writes=[Bht])
            xr, Bxr = x2r[tt % 3]
            self.rowload("sp", xr[:, :], self.s["x2"], tt * 128, D, Bxr,
                         Bsrc=self.Bs["x2"])
            ot, Bot = outt[tt % 3]
            for half in range(2):
                psj = 4 + (tt % 2) * 2 + half
                for dc in range(8):
                    k.op("pe", lambda e, dc=dc, tt=tt, half=half, psj=psj, ht=ht: e.matmul(
                        self.ps[psj][:, :],
                        ht[:, dc * 128:(dc + 1) * 128],
                        WoS[:, dc * D + half * 512: dc * D + half * 512 + 512],
                        start=(dc == 0), stop=(dc == 7)),
                        reads=[Bht, BWoS], writes=[self.Bps[psj]], inc=(dc == 7))
                k.op("dve", lambda e, half=half, psj=psj, ot=ot, xr=xr: e.tensor_tensor(
                    ot[:, half * 512: half * 512 + 512], self.ps[psj][:, :],
                    xr[:, half * 512: half * 512 + 512], ALU.add),
                    reads=[self.Bps[psj], Bxr], writes=[Bot])
            k.dma("act", bass.AP(self.out.tensor, tt * 128 * D, [[D, 128], [1, D]]),
                  ot[:, :], reads=[Bot])

        def go_s1(tt):
            hr, Bhr = h2r[tt % 3]
            self.rowload("sp", hr[:, :], self.s["h2"], tt * 128, D, Bhr,
                         Bsrc=self.Bs["h2"])
            for dh in range(2):
                psj = dh
                k.op("pe", lambda e, tt=tt, dh=dh, psj=psj: e.matmul(
                    self.ps[psj][:, :],
                    self.sb_p["maskgT"][0:NE, tt * 128: tt * 128 + 128],
                    be2b[0:NE, dh * 512: dh * 512 + 512],
                    start=True, stop=True),
                    reads=[self.Bp["maskgT"], Bbe2b], writes=[self.Bps[psj]])
                k.op("dve", lambda e, psj=psj, dh=dh, tt=tt, hr=hr: e.tensor_tensor(
                    hr[:, dh * 512: dh * 512 + 512], self.ps[psj][:, :],
                    hr[:, dh * 512: dh * 512 + 512], ALU.add),
                    reads=[self.Bps[psj], Bhr], writes=[Bhr])
                k.op("dve", lambda e, dh=dh, tt=tt, hr=hr: e.tensor_tensor(
                    hr[:, dh * 512: dh * 512 + 512],
                    hr[:, dh * 512: dh * 512 + 512],
                    msel[:, tt * 1024 + dh * 512: tt * 1024 + dh * 512 + 512],
                    ALU.add),
                    reads=[Bhr, Bmsel], writes=[Bhr])

        for it in range(TT + 2):
            if it < TT:
                go_s1(it)
            if 1 <= it <= TT:
                go_s2(it - 1)
            if it >= 2:
                go_s3(it - 2)
        gM.close()
        gO.close()

    def ff_block(self, W1, BW1g, W2, BW2g, hid, Bhid, hT, BhT, bias1):
        k = self.k
        b1sb, Bb1, boff = bias1
        for fc in range(32):
            psj = fc % 2
            for dc in range(8):
                k.op("pe", lambda e, fc=fc, dc=dc, psj=psj: e.matmul(
                    self.ps[psj][:, 0:256],
                    W1[:, dc * FF + fc * 128: dc * FF + fc * 128 + 128],
                    hT[:, dc * 256: dc * 256 + 256],
                    start=(dc == 0), stop=(dc == 7)),
                    reads=[BW1g[fc // 8], BhT], writes=[self.Bps[psj]], inc=(dc == 7))
            k.op("act", lambda e, fc=fc, psj=psj: e.activation(
                hid[:, fc * 256: fc * 256 + 256], self.ps[psj][:, 0:256], AF.Relu,
                bias=b1sb[:, boff + fc: boff + fc + 1]),
                reads=[self.Bps[psj], Bb1], writes=[Bhid])
        for t2 in range(2):
            for dh in range(2):
                psj = 4 + t2 * 2 + dh
                for fc in range(32):
                    k.op("pe", lambda e, fc=fc, t2=t2, dh=dh, psj=psj: e.matmul(
                        self.ps[psj][:, :],
                        hid[:, fc * 256 + t2 * 128: fc * 256 + t2 * 128 + 128],
                        W2[:, fc * D + dh * 512: fc * D + dh * 512 + 512],
                        start=(fc == 0), stop=(fc == 31)),
                        reads=[Bhid, BW2g[fc // 8]], writes=[self.Bps[psj]],
                        inc=(fc == 31))

    def gating(self, psj, ti):
        k, p, B = self.k, self.sb_p, self.Bp
        k.op("dve", lambda e: e.reduce_max(p["gm"][:, :], self.ps[psj][:, 0:8], axis=AX.X),
             reads=[self.Bps[psj]], writes=[B["gm"]])
        k.op("dve", lambda e: e.tensor_scalar(p["ge"][:, :], self.ps[psj][:, 0:8],
                                              p["gm"][:, :], None, ALU.subtract,
                                              ALU.bypass),
             reads=[self.Bps[psj], B["gm"]], writes=[B["ge"]])
        k.op("act", lambda e: e.activation(p["ge"][:, :], p["ge"][:, :], AF.Exp,
                                           accum_out=p["gz"][:, :]),
             reads=[B["ge"]], writes=[B["ge"], B["gz"]])
        k.op("dve", lambda e: e.tensor_scalar(p["gv"][:, :], p["ge"][:, :], 1.0, None,
                                              ALU.is_equal, ALU.bypass),
             reads=[B["ge"]], writes=[B["gv"]])
        k.op("dve", lambda e: e.tensor_scalar(p["ge"][:, :], p["iota8"][:, :], -1.0,
                                              8.0, ALU.mult, ALU.add),
             reads=[B["iota8"]], writes=[B["ge"]])
        k.op("dve", lambda e: e.tensor_tensor(p["ge"][:, :], p["ge"][:, :],
                                              p["gv"][:, :], ALU.mult),
             reads=[B["ge"], B["gv"]], writes=[B["ge"]])
        k.op("dve", lambda e: e.reduce_max(p["gm"][:, :], p["ge"][:, :], axis=AX.X),
             reads=[B["ge"]], writes=[B["gm"]])
        k.op("dve", lambda e: e.tensor_scalar(p["gv"][:, :], p["ge"][:, :],
                                              p["gm"][:, :], None, ALU.is_equal,
                                              ALU.bypass),
             reads=[B["ge"], B["gm"]], writes=[B["gv"]])
        k.op("dve", lambda e: e.reciprocal(p["gz"][:, :], p["gz"][:, :]),
             reads=[B["gz"]], writes=[B["gz"]])
        k.op("dve", lambda e, ti=ti: e.tensor_scalar(
            p["maskg"][:, ti * 8: ti * 8 + 8], p["gv"][:, :], p["gz"][:, :], None,
            ALU.mult, ALU.bypass),
            reads=[B["gv"], B["gz"]], writes=[B["maskg"]])
        psT = 2
        k.op("pe", lambda e, ti=ti: e.matmul(
            self.ps[psT][0:8, 0:128], p["maskg"][:, ti * 8: ti * 8 + 8],
            p["identf"][:, :], start=True, stop=True, is_transpose=True),
            reads=[B["maskg"], B["identf"]], writes=[self.Bps[psT]])
        k.op("dve", lambda e, ti=ti: e.tensor_copy(
            p["maskgT"][0:NE, ti * 128: ti * 128 + 128], self.ps[psT][0:8, 0:128]),
            reads=[self.Bps[psT]], writes=[B["maskgT"]])


_CACHE = {}


def build_program(T):
    if T not in _CACHE:
        _CACHE[T] = Builder(T).build()
    return _CACHE[T]


def make_inputs(full, T):
    peT, ident, identf, tri, iota8 = host_consts(T)
    rep = lambda v: np.ascontiguousarray(np.broadcast_to(v, (128, D)))
    u = full["L0_ffn_u_bias"]
    v = full["L0_ffn_v_bias"]
    uv = np.ascontiguousarray(
        np.concatenate([np.asarray(u).reshape(8, 128).T,
                        np.asarray(v).reshape(8, 128).T], axis=1))
    be1c = np.ascontiguousarray(
        np.asarray(full["L0_ffn_be1"]).reshape(8, 32, 128).transpose(2, 0, 1)
        .reshape(128, 256))
    bff1c = np.ascontiguousarray(np.asarray(full["L0_ffn_bff1"]).reshape(32, 128).T)
    shared = dict(
        aWqkv=full["L0_attn_Wqkv"], aWout=full["L0_attn_Wout"],
        fWqkv=full["L0_ffn_Wqkv"], fWo=full["L0_ffn_Wo"], fWr=full["L0_ffn_Wr"],
        fWff1=full["L0_ffn_Wff1"], fWff2=full["L0_ffn_Wff2"], fWg=full["L0_ffn_Wg"],
        fWe1=full["L0_ffn_We1"], fWe2=full["L0_ffn_We2"], fWout=full["L0_ffn_Wout"],
        uv=uv, be1c=be1c, bff1c=bff1c,
        bff2r=rep(np.asarray(full["L0_ffn_bff2"])), be2=full["L0_ffn_be2"],
        g1=rep(np.asarray(full["L0_ffn_ln1_g"])), b1=rep(np.asarray(full["L0_ffn_ln1_b"])),
        g2=rep(np.asarray(full["L0_ffn_ln2_g"])), b2=rep(np.asarray(full["L0_ffn_ln2_b"])),
        g3=rep(np.asarray(full["L0_ffn_ln3_g"])), b3=rep(np.asarray(full["L0_ffn_ln3_b"])),
        peT=peT, ident=ident, identf=identf, tri=tri, iota8=iota8,
    )
    shared = {k2: np.ascontiguousarray(v2) for k2, v2 in shared.items()}
    x = np.asarray(full["L0_attn_x"])
    in_maps = []
    for c in range(x.shape[0]):
        d = dict(shared)
        d["x"] = np.ascontiguousarray(x[c])
        in_maps.append(d)
    return in_maps


def kernel(**inputs):
    x = inputs["L0_attn_x"]
    B, T, _ = x.shape
    nc = build_program(T)
    in_maps = make_inputs(inputs, T)
    res = run_bass_kernel_spmd(nc, in_maps, core_ids=list(range(B)))
    out = np.stack([np.asarray(r["out"], dtype=np.float32) for r in res.results])
    return out
